# version 10
# baseline (speedup 1.0000x reference)
import numpy as np
import ml_dtypes
import concourse.bass as bass
import concourse.mybir as mybir
from concourse.bass_utils import run_bass_kernel_spmd
from contextlib import ExitStack

F32 = mybir.dt.float32
BF16 = mybir.dt.bfloat16
F16 = mybir.dt.float16
I16 = mybir.dt.int16
I8 = mybir.dt.int8
AF = mybir.ActivationFunctionType
ALU = mybir.AluOpType
AX = mybir.AxisListType

ENGS = ("tensor", "vector", "scalar", "gpsimd", "sync")


def ecopy(e, out, in_):
    if hasattr(e, "tensor_copy"):
        return e.tensor_copy(out, in_)
    return e.copy(out, in_)
DMAQ = {"sync": "dY", "gpsimd": "dG", "scalar": "dS"}
EPS = 1e-5


class Prog:

    import os as _os
    NLANES = int(_os.environ.get("KNLANES", "4"))

    def __init__(self):
        self.items = {e: [] for e in ENGS}
        self.cnt = {e: 0 for e in ENGS}
        self.dma_idx = {"dY": 0, "dG": 0, "dS": 0}
        self.lane_val = {}
        for q in ("dY", "dG", "dS"):
            for ln in range(self.NLANES):
                self.lane_val[f"{q}{ln}"] = 0
        self.cc_cnt = 0
        self.lastw = {}
        self.readers = {}
        self.waited = {e: {} for e in ENGS}
        self.drain_at = {e: 0 for e in ENGS}
        self.epoch = {}

    def _wait(self, eng, key, val):
        if self.waited[eng].get(key, 0) >= val:
            return
        self.waited[eng][key] = val
        self.items[eng].append(("wait", key, val))

    def _need(self, eng, tok):
        kind = tok[0]
        if kind == "e":
            e2, seq = tok[1], tok[2]
            if e2 == eng:
                if eng == "tensor":
                    return
                if self.drain_at[eng] < seq:
                    self.items[eng].append(("drain",))
                    self.drain_at[eng] = self.cnt[eng]
            else:
                self._wait(eng, e2, seq)
        elif kind == "d":
            self._wait(eng, tok[1], tok[2])
        else:
            self._wait(eng, "cc", tok[1])

    def _deps(self, eng, reads, writes):
        for r in reads:
            lw = self.lastw.get(r)
            if lw is not None:
                self._need(eng, lw)
        for w in writes:
            lw = self.lastw.get(w)
            if lw is not None:
                self._need(eng, lw)
            for rd in self.readers.get(w, ()):
                self._need(eng, rd)
            reg = w.split("/", 1)[0]
            for tk in self.epoch.get(reg, ()):
                self._need(eng, tk)

    def _record(self, tok, reads, writes):
        for r in reads:
            self.readers.setdefault(r, []).append(tok)
        for w in writes:
            self.lastw[w] = tok
            self.readers[w] = []

    def op(self, eng, fn, r=(), w=()):
        self._deps(eng, r, w)
        self.cnt[eng] += 1
        tok = ("e", eng, self.cnt[eng])
        self.items[eng].append(("call", fn, eng, 1))
        self._record(tok, r, w)
        return tok

    def dma(self, eng, fn, r=(), w=()):
        q = DMAQ[eng]
        self._deps(eng, r, w)
        k = self.dma_idx[q]
        self.dma_idx[q] += 1
        lanekey = f"{q}{k % self.NLANES}"
        prewait = self.lane_val[lanekey]
        self.lane_val[lanekey] = prewait + 16
        tok = ("d", lanekey, prewait + 16)
        self.items[eng].append(("dma", fn, lanekey, prewait))
        self._record(tok, r, w)
        return tok

    def cc(self, fn, r=(), w=()):
        eng = "gpsimd"
        self._deps(eng, r, w)
        self.cc_cnt += 1
        tok = ("c", self.cc_cnt)
        self.items[eng].append(("call", fn, "cc", 1))
        self._record(tok, r, w)
        return tok

    def close_role(self, region):
        toks = []
        for res, tk in self.lastw.items():
            if res.split("/", 1)[0] == region:
                toks.append(tk)
        for res, lst in self.readers.items():
            if res.split("/", 1)[0] == region:
                toks.extend(lst)
        self.epoch[region] = toks
        for res in [x for x in self.lastw if x.split("/", 1)[0] == region]:
            del self.lastw[res]
        for res in [x for x in self.readers if x.split("/", 1)[0] == region]:
            del self.readers[res]

    def barrier_engine(self, eng, upto=None):
        for lk, v in self.lane_val.items():
            if v:
                self._wait(eng, lk, v)
        for e2 in ENGS:
            if e2 != eng and self.cnt[e2]:
                self._wait(eng, e2, self.cnt[e2])
        if self.cc_cnt:
            self._wait(eng, "cc", self.cc_cnt)

    def emit(self, block, sems):
        import os as _os
        _dense = "denseinc" in _os.environ.get("KABL", "")
        needed = {e: set() for e in ENGS}
        for e in ENGS:
            for it in self.items[e]:
                if it[0] == "wait" and it[1] in self.cnt:
                    needed[it[1]].add(it[2])
        if _dense:
            for e in ENGS:
                needed[e] = set(range(1, self.cnt[e] + 1))
        rank = {}
        for e in ENGS:
            r = 0
            m = {}
            for seq in sorted(needed[e]):
                r += 1
                m[seq] = r
            rank[e] = m

        def run(ename, items):
            seq_ctr = [0]

            def body(eng):
                for it in items:
                    if it[0] == "wait":
                        if it[1] in self.cnt:
                            eng.wait_ge(sems[it[1]], rank[it[1]][it[2]])
                        else:
                            eng.wait_ge(sems[it[1]], it[2])
                    elif it[0] == "drain":
                        eng.drain()
                    elif it[0] == "dma":
                        _, fn, lanekey, prewait = it
                        eng.wait_ge(sems[lanekey], prewait)
                        fn(eng).then_inc(sems[lanekey], 16)
                    else:
                        _, fn, semkey, n = it
                        inst = fn(eng)
                        if semkey in self.cnt:
                            seq_ctr[0] += 1
                            if inst is not None and seq_ctr[0] in needed[semkey]:
                                inst.then_inc(sems[semkey], n)
                        elif inst is not None:
                            inst.then_inc(sems[semkey], n)
            return body

        block.tensor(run("tensor", self.items["tensor"]))
        block.vector(run("vector", self.items["vector"]))
        block.scalar(run("scalar", self.items["scalar"]))
        block.gpsimd(run("gpsimd", self.items["gpsimd"]))
        block.sync(run("sync", self.items["sync"]))


def rview(reg, dt, shape, off=0):
    ap = reg.ap().rearrange("p a b -> p (a b)")
    if dt != F32:
        ap = ap.bitcast(dt)
    tot = ap.shape[1]
    n = 1
    for s in shape:
        n *= s
    assert off + n <= tot, (shape, off, tot)
    ap = ap[:, off:off + n]
    if len(shape) == 1:
        return ap
    if len(shape) == 2:
        return ap.rearrange("p (a b) -> p a b", a=shape[0], b=shape[1])
    if len(shape) == 3:
        return ap.rearrange("p (a b c) -> p a b c", a=shape[0], b=shape[1], c=shape[2])
    raise AssertionError(shape)


def build_nc(seg=2048, upto=99, debug=False):
    import os as _os
    ABL = set(_os.environ.get("KABL", "").split(","))
    """Build the per-core program. seg = tokens per core (multiple of 512)."""
    assert seg % 512 == 0
    NCH = seg // 128
    NG = seg // 512
    H = 4

    nc = bass.Bass("TRN2", target_bir_lowering=False)

    x_in = nc.dram_tensor("x_in", [seg, 1024], F32, kind="ExternalInput")
    wqT = nc.dram_tensor("wqT", [1024, 1024], F32, kind="ExternalInput")
    wkT = nc.dram_tensor("wkT", [1024, 1024], F32, kind="ExternalInput")
    wvT = nc.dram_tensor("wvT", [1024, 1024], F32, kind="ExternalInput")
    wfT = nc.dram_tensor("wfT", [1024, 1024], F32, kind="ExternalInput")
    woT = nc.dram_tensor("woT", [1024, 1024], F32, kind="ExternalInput")
    g_wp = nc.dram_tensor("g_wp", [128, 8], F32, kind="ExternalInput")
    winT_sh = nc.dram_tensor("winT_sh", [512, 1024], F32, kind="ExternalInput")
    wout_sh = nc.dram_tensor("wout_sh", [512, 1024], F32, kind="ExternalInput")
    b_inp = nc.dram_tensor("b_inp", [128, 32], F32, kind="ExternalInput")
    b_outb = nc.dram_tensor("b_outb", [128, 1024], F32, kind="ExternalInput")
    identB_d = nc.dram_tensor("identB", [128, 128], BF16, kind="ExternalInput")
    identF_d = nc.dram_tensor("identF", [128, 128], F32, kind="ExternalInput")
    maskB_d = nc.dram_tensor("maskB", [128, 128], BF16, kind="ExternalInput")
    onesr_d = nc.dram_tensor("onesr", [1, 128], BF16, kind="ExternalInput")
    onesrf_d = nc.dram_tensor("onesrf", [1, 128], F32, kind="ExternalInput")
    onesc_d = nc.dram_tensor("onesc", [128, 1], BF16, kind="ExternalInput")
    onesB_d = nc.dram_tensor("onesB", [128, 128], BF16, kind="ExternalInput")
    selm_d = nc.dram_tensor("selm", [128, 12], F32, kind="ExternalInput")
    selT_d = nc.dram_tensor("selT", [16, 2048], BF16, kind="ExternalInput")

    y_out = nc.dram_tensor("y_out", [seg, 1024], F32, kind="ExternalOutput")

    v_spill = nc.dram_tensor("v_spill", [seg, 1024], BF16)
    x1_spill = nc.dram_tensor("x1_spill", [seg, 1024], F32)
    hid_spill = nc.dram_tensor("hid_spill", [32, 128, seg], BF16)
    ag1_in = nc.dram_tensor("ag1_in", [128, 2056], F32)
    ag1_out = nc.dram_tensor("ag1_out", [4, 128, 2056], F32)
    ag2_in = nc.dram_tensor("ag2_in", [128, 2048], F32)
    ag2_out = nc.dram_tensor("ag2_out", [4, 128, 2048], F32)
    arm1_in = nc.dram_tensor("arm1_in", [128, 1], F32)
    arm1_out = nc.dram_tensor("arm1_out", [128, 1], F32)
    arm2_in = nc.dram_tensor("arm2_in", [128, 1], F32)
    arm2_out = nc.dram_tensor("arm2_out", [128, 1], F32)
    agwi_in = nc.dram_tensor("agwi_in", [1024, 512], BF16)
    agwi_out = nc.dram_tensor("agwi_out", [8, 1024, 512], BF16)
    agwo_in = nc.dram_tensor("agwo_in", [512, 1024], BF16)
    agwo_out = nc.dram_tensor("agwo_out", [8, 512, 1024], BF16)

    dbg = {}
    if debug:
        for nm, shp, dt in [
            ("d_xnF", [128, 8, seg], F32), ("d_qF", [128, 8, seg], F32),
            ("d_kF", [128, 8, seg], F32), ("d_sigF", [128, 8, seg], F32),
            ("d_A", [128, NCH, 1024], F32), ("d_U", [128, NCH, 1024], F32),
            ("d_acF", [128, 8, NCH], F32), ("d_acum", [128, 8, NCH + 1], F32),
            ("d_ok", [128, NCH, 1024], F32), ("d_s1T", [128, 4, 512], F32),
            ("d_qt", [128, 8, seg], F32), ("d_ovF", [128, 8, seg], F32),
            ("d_s2T", [128, 4, 512], F32), ("d_orms", [128, 8, seg], F32),
            ("d_x1", [seg, 1024], F32), ("d_xn2", [seg // 128, 128, 1024], F32),
            ("d_amax", [128, 2], F32), ("d_wins", [128, 8, 4096], F32),
            ("d_wous", [128, 32, 1024], F32), ("d_hid", [32, 128, seg], F32),
            ("d_sin", [128, 4, 512], F32), ("d_sin2", [128, 4, 512], F32),
            ("d_v", [seg, 1024], F32), ("d_x1s", [seg, 1024], F32),
            ("d_hids", [32, 128, seg], F32),
        ]:
            dbg[nm] = nc.dram_tensor(nm, shp, dt, kind="ExternalOutput")

    p = Prog()
    st = ExitStack()
    with st:
        sb = lambda n, s, d=F32: st.enter_context(nc.sbuf_tensor(n, s, d))
        rA = sb("rA", [128, 8, 1024], F32)
        rB = sb("rB", [128, 8, 1024], F32)
        rC = sb("rC", [128, 8, 1024], F32)
        rD = sb("rD", [128, 8, 1024], F32)
        rE = sb("rE", [128, 8, 1024], F32)
        xnF = rview(rA, BF16, [8, seg])
        A_tok = rview(rA, BF16, [NCH, 1024])
        ovF = rview(rA, F16, [8, seg])
        xsimF = rview(rA, BF16, [8, seg])
        hidqG = rview(rA, BF16, [32, 512])
        qF = rview(rB, BF16, [8, seg])
        ormsF = rview(rB, BF16, [8, seg])
        kF = rview(rC, BF16, [8, seg])
        qtF = rview(rC, BF16, [8, seg])
        woTg = rview(rC, BF16, [8, 1024], off=8192)
        sigF = rview(rD, F16, [8, seg])
        okloc = rview(rD, F16, [NCH, 1024])
        xn2a = rview(rD, F32, [8, 1024])
        U_tok = rview(rE, BF16, [NCH, 1024])
        xn2b = rview(rE, F32, [8, 1024])
        boutb_v = rview(rD, F32, [1024], 0)
        wbuf0 = rview(rE, BF16, [2, 8, 1024])
        winsA = rview(rB, BF16, [8, 2048])
        winsB = rview(rC, BF16, [8, 2048])
        wousA = rview(rB, BF16, [16, 1024])
        wousB = rview(rC, BF16, [16, 1024])

        identB = sb("identBs", [128, 128], BF16)
        identF = sb("identFs", [128, 128], F32)
        maskB = sb("maskBs", [128, 128], BF16)
        onesr = sb("onesrs", [1, 128], BF16)
        onesrf = sb("onesrfs", [1, 128], F32)
        onesc = sb("onescs", [128, 1], BF16)
        onesB = sb("onesBs", [128, 128], BF16)
        selT = sb("selTs", [16, 2048], BF16)
        diagt = sb("diagts", [128, 256], BF16)
        selm = sb("selms", [128, 12], F32)
        gwp = sb("gwps", [128, 8], F32)
        binp = sb("binps", [128, 32], F32)
        boutb = None
        zeros = sb("zeross", [128, 128], F32)
        xin = [rview(rB, F32, [1024], i * 1024) for i in range(4)]
        xmb = [rview(rB, BF16, [1024], 8 * 1024 + i * 1024) for i in range(2)]
        bnall = rview(rB, F32, [4, 12], 5 * 1024)
        mvall = rview(rB, F32, [4, 2], 5 * 1024 + 64)
        rrq = rview(rB, F32, [4], 5 * 1024 + 80)
        mrrq = rview(rB, F32, [4], 5 * 1024 + 96)
        bn12 = sb("bn12", [128, 12], F32)
        mv = sb("mv", [128, 2], F32)
        lt = sb("lt", [128, 8], F32)
        wq_f = rview(rC, F32, [4, 1024], 0)
        wq_i8 = rview(rC, I8, [4, 1024], 4 * 4096)
        wq_bf = rview(rD, BF16, [4, 1024], 0)
        wq_st = rview(rD, BF16, [8, 512], 4 * 2048)
        wq_am = sb("wq_am", [128, 4], F32)
        wq_am2 = sb("wq_am2", [128, 4], F32)
        wq_r = sb("wq_r", [128, 4], F32)
        wq_sc = sb("wq_sc", [128, 4], F32)
        wq_is = sb("wq_is", [128, 4], F32)
        sgt = [sb(f"sgt{i}", [128, 512], F32) for i in range(2)]
        aFt = sb("aFt", [128, 512], BF16)
        uFt = sb("uFt", [128, 512], BF16)
        zer16 = sb("zer16", [128, 16], F32)
        acF = sb("acF", [128, 8, NCH], F32)
        acum = sb("acum", [128, 8, NCH + 1], F32)
        acFT16 = sb("acFT16", [16, 1024], BF16)
        acumT16 = sb("acumT16", [16, 1024], BF16)
        ktT = [sb(f"ktT{i}", [128, 256], BF16) for i in range(2)]
        pmk = [sb(f"pmk{i}", [128, 128], BF16) for i in range(2)]
        ufT = [sb(f"ufT{i}", [128, 256], BF16) for i in range(2)]
        vst = [sb(f"vst{i}", [128, 256], BF16) for i in range(2)]
        acb = sb("acb", [128, 512], BF16)
        s1f = [sb(f"s1f{i}", [128, 512], F32) for i in range(2)]
        s1b = sb("s1b", [128, 512], BF16)
        s2f = [sb(f"s2f{i}", [128, 512], F32) for i in range(2)]
        s2b = sb("s2b", [128, 512], BF16)
        sinft = sb("sinft", [128, 2, 512], F32)
        sinf = [sinft[:, i, :] for i in range(2)]
        sinb = sb("sinb", [128, 512], BF16)
        sin2ft = sb("sin2ft", [128, 2, 512], F32)
        sin2f = [sin2ft[:, i, :] for i in range(2)]
        sin2b = sb("sin2b", [128, 512], BF16)
        lnf = sinf[0]
        csb = sinf[1]
        aiv = sin2f[0]
        peerD = sb("peerD", [128, 8], F32)
        sms = [sb(f"sms{i}", [128, 3], F32) for i in range(2)]
        qtt = [sb(f"qtt{i}", [128, 256], BF16) for i in range(2)]
        smx = sgt[0][:, 0:256]
        sme = sgt[0][:, 256:512]
        srow = sgt[1][0:1, 0:512]
        srowB = sgt[0][0:1, 0:512]
        rrow = aFt[0:1, 0:512]
        dsel2 = s1f[0]
        peerTh = s1f[1]
        o2t = sb("o2t", [128, 512], BF16)
        rcol = sb("rcol", [128, 1], F32)
        xrl = sinft.ap().rearrange("p a b -> p (a b)")
        x1s = sin2ft.ap().rearrange("p a b -> p (a b)")
        outs = sinft.ap().rearrange("p a b -> p (a b)")
        amxall = sb("amxall", [128, 32], F32)
        armx = sb("armx", [128, 1], F32)
        armxb = sb("armxb", [128, 1], F32)
        scp = sb("scp", [128, 4], F32)
        sc1 = sb("sc1", [1, 4], F32)
        hst_v = s2f[0].ap().bitcast(BF16)[:, 0:1024]
        hst = [hst_v[:, 0:512], hst_v[:, 512:1024]]
        q16 = s1f[0].ap().bitcast(I16)[:, 0:1024]
        xsimt = s1f[1].ap().bitcast(BF16)[:, 0:1024]

        pb = [st.enter_context(nc.psum_tensor(f"pb{i}", [128, 512], F32)) for i in range(6)]
        pt = [st.enter_context(nc.psum_tensor(f"pt{i}", [128, 1024], BF16)) for i in range(2)]

        sems = {}
        semkeys = ["tensor", "vector", "scalar", "gpsimd", "sync", "cc"]
        for q in ("dY", "dG", "dS"):
            for ln in range(Prog.NLANES):
                semkeys.append(f"{q}{ln}")
        for key in semkeys:
            sems[key] = st.enter_context(nc.semaphore("sem_" + key))

        _clist = [
            ("identB", identB, identB_d), ("identF", identF, identF_d),
            ("maskB", maskB, maskB_d), ("onesr", onesr, onesr_d),
            ("onesrf", onesrf, onesrf_d), ("onesc", onesc, onesc_d),
            ("onesB", onesB, onesB_d), ("selT", selT, selT_d),
            ("selm", selm, selm_d), ("gwp", gwp, g_wp),
            ("binp", binp, b_inp),
        ]
        if "minconst" in ABL:
            _clist = _clist[:1]
        for nm, dst, src in _clist:
            p.dma("sync", (lambda e, d=dst, s=src: e.dma_start(d[:], s.ap())),
                  r=(), w=(f"c/{nm}",))
        p.op("vector", lambda e: e.memset(zeros[:], 0.0), w=("c/zeros",))
        p.op("vector", lambda e: e.memset(zer16[:], 0.0), w=("c/zer16",))
        epsc = sb("epsc", [128, 1], F32)
        p.op("vector", lambda e: e.memset(epsc[:], EPS), w=("c/epsc",))
        nc.const_aps.register(F32, EPS, epsc[:, 0:1])

        def quant_shard(src_dram, do_transpose, agi_dram, tag):
            p.dma("gpsimd", lambda e: e.dma_start(
                wq_f[:], src_dram.ap().rearrange("(a p) b -> p a b", p=128)),
                w=("rC/wqf",))
            if "noquant" in ABL:
                for rt in range(4):
                    p.op("vector", lambda e, rt=rt: e.tensor_copy(
                        wq_bf[:, rt], wq_f[:, rt]),
                        r=("rC/wqf",), w=(f"rD/wqbf/{rt}",))
            if "noquant" not in ABL:
             for rt in range(4):
                p.op("vector", lambda e, rt=rt: e.tensor_reduce(
                    wq_am[:, rt:rt + 1], wq_f[:, rt], AX.X, ALU.max,
                    apply_absolute_value=True),
                    r=("rC/wqf",), w=("s/wqam",))
            if "noquant" not in ABL:
             p.op("vector", lambda e: e.tensor_scalar(
                wq_am2[:], wq_am[:], 1e-8, None, ALU.max),
                r=("s/wqam",), w=("s/wqam2",))
             p.op("vector", lambda e: e.reciprocal(wq_r[:], wq_am2[:]),
                 r=("s/wqam2",), w=("s/wqr",))
             p.op("vector", lambda e: e.tensor_scalar(
                wq_sc[:], wq_r[:], 127.0, None, ALU.mult),
                r=("s/wqr",), w=("s/wqsc",))
             p.op("vector", lambda e: e.tensor_scalar(
                wq_is[:], wq_am2[:], 1.0 / 127.0, None, ALU.mult),
                r=("s/wqam2",), w=("s/wqis",))
             for rt in range(4):
                p.op("vector", lambda e, rt=rt: e.tensor_scalar(
                    wq_i8[:, rt], wq_f[:, rt], wq_sc[:, rt:rt + 1], None, ALU.mult),
                    r=("rC/wqf", "s/wqsc"), w=(f"rC/wqi8/{rt}",))
                p.op("vector", lambda e, rt=rt: e.tensor_scalar(
                    wq_bf[:, rt], wq_i8[:, rt], wq_is[:, rt:rt + 1], None, ALU.mult),
                    r=(f"rC/wqi8/{rt}", "s/wqis"), w=(f"rD/wqbf/{rt}",))
            if do_transpose and "notransp" not in ABL:
                for rt in range(4):
                    bank = rt % 2
                    for dch in range(8):
                        p.op("tensor", lambda e, rt=rt, dch=dch, bank=bank:
                             e.transpose(pt[bank][:, dch * 128:(dch + 1) * 128],
                                         wq_bf[:, rt, dch * 128:(dch + 1) * 128],
                                         identB[:]),
                             r=(f"rD/wqbf/{rt}", "c/identB"),
                             w=(f"pt{bank}/{dch}",))
                    eng = "vector" if "noscal" in ABL else ("scalar" if rt % 2 else "vector")
                    p.op(eng, lambda e, rt=rt, bank=bank: ecopy(
                        e, wq_st[:, :, rt * 128:(rt + 1) * 128],
                        pt[bank].ap().rearrange("p (a b) -> p a b", a=8, b=128)),
                        r=tuple(f"pt{bank}/{d}" for d in range(8)),
                        w=tuple(f"rD/wqst/{d}/{rt}" for d in range(8)))
                p.dma("gpsimd", lambda e: e.dma_start(
                    agi_dram.ap().rearrange("(a p) b -> p a b", p=128),
                    wq_st[:]),
                    r=tuple(f"rD/wqst/{dch}/{rt}" for dch in range(8) for rt in range(4)),
                    w=(f"s/{tag}",))
            else:
                p.dma("gpsimd", lambda e: e.dma_start(
                    agi_dram.ap().rearrange("(a p) b -> p a b", p=128),
                    wq_bf[:]),
                    r=tuple(f"rD/wqbf/{rt}" for rt in range(4)),
                    w=(f"s/{tag}",))

        if upto > -3:
            if "nowin" not in ABL:
                quant_shard(winT_sh, True, agwi_in, "agwi_in")
            if upto > -2:
                p.cc(lambda e: e.collective_compute(
                    "AllGather", ALU.bypass, replica_groups=[list(range(8))],
                    ins=[agwi_in.ap()], outs=[agwi_out.ap()]),
                    r=("s/agwi_in",), w=("s/agwi_out",))
            if "nowout" not in ABL:
                quant_shard(wout_sh, False, agwo_in, "agwo_in")
            if upto > -2:
                p.cc(lambda e: e.collective_compute(
                    "AllGather", ALU.bypass, replica_groups=[list(range(8))],
                    ins=[agwo_in.ap()], outs=[agwo_out.ap()]),
                    r=("s/agwo_in",), w=("s/agwo_out",))
        p.close_role("rC")
        p.close_role("rD")

        if upto <= 0:
            p.barrier_engine("sync")
            block = st.enter_context(nc.Block())
            p.emit(block, sems)
            return nc

        NT = seg // 128
        for gq in range(NT // 4):
            for k4 in range(4):
                ti = gq * 4 + k4
                xres = f"rB/xin{k4}"
                p.dma("sync", lambda e, ti=ti, k4=k4: e.dma_start(
                    xin[k4][:], x_in.ap()[ti * 128:(ti + 1) * 128, :]), w=(xres,))
                p.op("vector", lambda e, k4=k4: e.bn_stats(
                    bnall[:, k4, 0:6], xin[k4][:, 0:512]),
                    r=(xres,), w=(f"rB/bn/{k4}",))
                p.op("vector", lambda e, k4=k4: e.bn_stats(
                    bnall[:, k4, 6:12], xin[k4][:, 512:1024]),
                    r=(xres,), w=(f"rB/bn/{k4}",))
                p.op("vector", lambda e, k4=k4: e.bn_aggr(
                    mvall[:, k4, :], bnall[:, k4, :]),
                    r=(f"rB/bn/{k4}",), w=(f"rB/mv/{k4}",))
            p.op("scalar", lambda e: e.activation(
                rrq[:, 0:4], mvall[:, :, 1], AF.Ln, bias=EPS),
                r=tuple(f"rB/mv/{k}" for k in range(4)) + ("c/epsc",), w=("rB/rr",))
            p.op("scalar", lambda e: e.activation(
                mrrq[:, 0:4], rrq[:, 0:4], AF.Exp, scale=-0.5),
                r=("rB/rr",), w=("rB/rr2",))
            p.op("vector", lambda e: e.scalar_tensor_tensor(
                rrq[:, 0:4], mvall[:, :, 0], -1.0,
                mrrq[:, 0:4], ALU.mult, ALU.mult),
                r=tuple(f"rB/mv/{k}" for k in range(4)) + ("rB/rr2",), w=("rB/mrr",))
            for k4 in range(4):
                ti = gq * 4 + k4
                par = ti % 2
                p.op("scalar", lambda e, k4=k4, par=par: e.activation(
                    xmb[par][:], xin[k4][:], AF.Identity,
                    scale=mrrq[:, k4:k4 + 1], bias=rrq[:, k4:k4 + 1]),
                    r=(f"rB/xin{k4}", "rB/rr2", "rB/mrr"), w=(f"rB/xmb{par}",))
                for dch in range(8):
                    p.op("tensor", lambda e, dch=dch, par=par: e.transpose(
                        pt[par][:, dch * 128:(dch + 1) * 128],
                        xmb[par][:, dch * 128:(dch + 1) * 128], identB[:]),
                        r=(f"rB/xmb{par}", "c/identB"), w=(f"pt{par}/{dch}",))
                eng = "scalar" if par else "vector"
                p.op(eng, lambda e, ti=ti, par=par: ecopy(
                    e, xnF[:, :, ti * 128:(ti + 1) * 128],
                    pt[par].ap().rearrange("p (a b) -> p a b", a=8, b=128)),
                    r=tuple(f"pt{par}/{dch}" for dch in range(8)),
                    w=tuple(f"rA/xnF/{ti}/{dch}" for dch in range(8)))
        if debug and upto >= 1:
            p.dma("gpsimd", lambda e: e.dma_start(dbg["d_xnF"].ap(), xnF[:]),
                  r=tuple(f"rA/xnF/{ti}/{d}" for ti in range(NT) for d in range(8)))
        if upto <= 1:
            p.barrier_engine("sync")
            block = st.enter_context(nc.Block())
            p.emit(block, sems)
            return nc

        p.close_role("rB")
        vstage = hst_v
        vstage2 = sinft.ap().rearrange("p a b -> p (a b)").bitcast(BF16)[:, 0:1024]

        def proj_featmaj(wd, wi, dstF, dreg, act):
            wb = wbuf0[:, wi % 2]
            wres = f"rE/wb{wi % 2}"
            p.dma("gpsimd", lambda e, wb=wb, wd=wd: e.dma_start(
                wb, wd.ap().rearrange("(a p) b -> p a b", p=128)), w=(wres,))
            for g in range(NG):
                gsl = slice(g * 512, (g + 1) * 512)
                xres = tuple(f"rA/xnF/{g * 4 + t}/{kch}"
                             for t in range(4) for kch in range(8))
                for mch in range(8):
                    bank = mch % 4
                    for kch in range(8):
                        p.op("tensor", lambda e, wb=wb, mch=mch, kch=kch, gsl=gsl, bank=bank:
                             e.matmul(pb[bank][:], wb[:, kch, mch * 128:(mch + 1) * 128],
                                      xnF[:, kch, gsl], start=(kch == 0), stop=(kch == 7)),
                             r=(wres,) + xres, w=(f"pb{bank}",))
                    if act == "sig":
                        p.op("scalar", lambda e, mch=mch, gsl=gsl, bank=bank:
                             e.activation(sigF[:, mch, gsl], pb[bank][:], AF.Sigmoid),
                             r=(f"pb{bank}",), w=(f"{dreg}/sigF/{mch}/{g}",))
                    else:
                        p.op("scalar", lambda e, dstF=dstF, mch=mch, gsl=gsl, bank=bank:
                             e.activation(dstF[:, mch, gsl], pb[bank][:], AF.Silu),
                             r=(f"pb{bank}",),
                             w=(f"{dreg}/{'qF' if dreg == 'rB' else 'kF'}/{mch}/{g}",))

        proj_featmaj(wqT, 0, qF, "rB", "silu")
        proj_featmaj(wkT, 1, kF, "rC", "silu")
        wb = wbuf0[:, 0]
        p.dma("gpsimd", lambda e, wb=wb: e.dma_start(
            wb, wvT.ap().rearrange("(a p) b -> p a b", p=128)), w=("rE/wb0",))
        for g in range(NG):
            for tt in range(4):
                ti = g * 4 + tt
                par = ti % 2
                xres = tuple(f"rA/xnF/{ti}/{kch}" for kch in range(8))
                for nh in range(2):
                    bank = par * 2 + nh
                    for kch in range(8):
                        p.op("tensor", lambda e, wb=wb, ti=ti, nh=nh, kch=kch, bank=bank:
                             e.matmul(pb[bank][:], xnF[:, kch, ti * 128:(ti + 1) * 128],
                                      wb[:, kch, nh * 512:(nh + 1) * 512],
                                      start=(kch == 0), stop=(kch == 7)),
                             r=("rE/wb0",) + xres, w=(f"pb{bank}",))
                    vs = vstage if par == 0 else vstage2
                    vr = f"s/s2f0/{nh}" if par == 0 else "s/sinf0"
                    p.op("scalar", lambda e, nh=nh, bank=bank, vs=vs:
                         e.activation(vs[:, nh * 512:(nh + 1) * 512],
                                      pb[bank][:], AF.Silu),
                         r=(f"pb{bank}",), w=(vr,))
                vs = vstage if par == 0 else vstage2
                vres = ("s/s2f0/0", "s/s2f0/1") if par == 0 else ("s/sinf0",)
                p.dma("sync", lambda e, ti=ti, vs=vs: e.dma_start(
                    v_spill.ap()[ti * 128:(ti + 1) * 128, :], vs[:, 0:1024]),
                    r=vres, w=(f"s/vsp/{ti}",))
        proj_featmaj(wfT, 1, None, "rD", "sig")
        if debug and upto >= 2:
            p.dma("gpsimd", lambda e: e.dma_start(dbg["d_qF"].ap(), qF[:]),
                  r=tuple(f"rB/qF/{m}/{g}" for m in range(8) for g in range(NG)))
            p.dma("gpsimd", lambda e: e.dma_start(dbg["d_kF"].ap(), kF[:]),
                  r=tuple(f"rC/kF/{m}/{g}" for m in range(8) for g in range(NG)))
            p.dma("gpsimd", lambda e: e.dma_start(dbg["d_sigF"].ap(), sigF[:]),
                  r=tuple(f"rD/sigF/{m}/{g}" for m in range(8) for g in range(NG)))
            p.dma("gpsimd", lambda e: e.dma_start(dbg["d_v"].ap(), v_spill.ap()),
                  r=tuple(f"s/vsp/{t}" for t in range(NT)))
        if upto <= 2:
            p.barrier_engine("sync")
            block = st.enter_context(nc.Block())
            p.emit(block, sems)
            return nc

        p.close_role("rA")

        def pipeline(niter, stages, pre=None, start=0):
            maxsk = max(sk for sk, _ in stages)
            for slot in range(start, niter + maxsk + 1):
                if pre is not None and slot in pre:
                    for fn in pre[slot]:
                        fn()
                for sk, fn in stages:
                    i = slot - sk
                    if 0 <= i < niter:
                        fn(i)

        lnfP = [(lnf, "s/sinf0"), (s1f[0], "s/s1f0")]
        csbP = [(csb, "s/sinf1"), (s1f[1], "s/s1f1")]
        aivP = [(aiv, "s/sin2f0"), (s2f[1], "s/s2f1")]
        aFtP = [(aFt, "s/aFt"), (s1b, "s/s1b")]
        uFtP = [(uFt, "s/uFt"), (s2b, "s/s2b")]

        def ph2b_stages():
            def ctx(i):
                g, mt = divmod(i, 8)
                return g, mt, slice(g * 512, (g + 1) * 512)

            def Sln(i):
                g, mt, gsl = ctx(i)
                lnfX, lnfR = lnfP[i % 2]
                p.op("scalar", lambda e, mt=mt, gsl=gsl, lnfX=lnfX: e.activation(
                    lnfX[:], sigF[:, mt, gsl], AF.Ln),
                    r=(f"rD/sigF/{mt}/{g}",), w=(lnfR,))

            def Vcs(i):
                lnfX, lnfR = lnfP[i % 2]
                csbX, csbR = csbP[i % 2]
                for cj in range(4):
                    p.op("vector", lambda e, cj=cj, lnfX=lnfX, csbX=csbX:
                         e.tensor_tensor_scan(
                             csbX[:, cj * 128:(cj + 1) * 128],
                             lnfX[:, cj * 128:(cj + 1) * 128],
                             zeros[:], 0.0, ALU.add, ALU.add),
                         r=(lnfR, "c/zeros"), w=(csbR,))

            def Sai(i):
                csbX, csbR = csbP[i % 2]
                aivX, aivR = aivP[i % 2]
                p.op("scalar", lambda e, csbX=csbX, aivX=aivX: e.activation(
                    aivX[:], csbX[:], AF.Exp, scale=-0.125),
                    r=(csbR,), w=(aivR,))

            def Vu(i):
                g, mt, gsl = ctx(i)
                aivX, aivR = aivP[i % 2]
                aFtX, aFtR = aFtP[i % 2]
                uFtX, uFtR = uFtP[i % 2]
                if "norecip" in ABL:
                    csbX, csbR = csbP[i % 2]
                    p.op("scalar", lambda e, csbX=csbX, aFtX=aFtX: e.activation(
                        aFtX[:], csbX[:], AF.Exp, scale=0.125),
                        r=(csbR,), w=(aFtR,))
                    p.op("scalar", lambda e, mt=mt, g=g, csbX=csbX: e.activation(
                        acF[:, mt, g * 4:(g + 1) * 4],
                        csbX[:, 127:512:128], AF.Exp, scale=0.125),
                        r=(csbR,), w=(f"s/acF/{mt}/{g}",))
                else:
                    def _recip_bf(e, aivX=aivX, aFtX=aFtX):
                        with nc.allow_low_precision(reason="aFt consumed as bf16"):
                            return e.reciprocal(aFtX[:], aivX[:])
                    p.op("vector", _recip_bf, r=(aivR,), w=(aFtR,))
                    p.op("vector", lambda e, mt=mt, g=g, aivX=aivX: e.reciprocal(
                        acF[:, mt, g * 4:(g + 1) * 4], aivX[:, 127:512:128]),
                        r=(aivR,), w=(f"s/acF/{mt}/{g}",))
                p.op("vector", lambda e, aivX=aivX, uFtX=uFtX: e.tensor_tensor(
                    uFtX[:, 1:512], aivX[:, 1:512], aivX[:, 0:511], ALU.subtract),
                    r=(aivR,), w=(uFtR,))
                p.op("vector", lambda e, aivX=aivX, uFtX=uFtX: e.tensor_scalar(
                    uFtX[:, 0:512:128], aivX[:, 0:512:128], 1.0, None, ALU.subtract),
                    r=(aivR,), w=(uFtR,))

            pbt = [pb[4].ap().bitcast(BF16), pb[5].ap().bitcast(BF16)]

            def Ttr(i):
                ptA = pt[i % 2]
                ptU = pbt[i % 2]
                aFtX, aFtR = aFtP[i % 2]
                uFtX, uFtR = uFtP[i % 2]
                for cj in range(4):
                    p.op("tensor", lambda e, cj=cj, ptA=ptA, aFtX=aFtX:
                         e.transpose(ptA[:, cj * 128:(cj + 1) * 128],
                                     aFtX[:, cj * 128:(cj + 1) * 128], identB[:]),
                         r=(aFtR, "c/identB"), w=(f"pt{i % 2}/{cj}",))
                    p.op("tensor", lambda e, cj=cj, ptU=ptU, uFtX=uFtX:
                         e.transpose(ptU[:, cj * 128:(cj + 1) * 128],
                                     uFtX[:, cj * 128:(cj + 1) * 128], identB[:]),
                         r=(uFtR, "c/identB"), w=(f"pb{4 + i % 2}",))

            def Vea(i):
                g, mt, gsl = ctx(i)
                ptA = pt[i % 2]
                p.op("vector", lambda e, g=g, mt=mt, ptA=ptA: e.tensor_copy(
                    A_tok[:, g * 4:(g + 1) * 4, mt * 128:(mt + 1) * 128],
                    ptA[:, 0:512].rearrange("p (a b) -> p a b", a=4, b=128)),
                    r=tuple(f"pt{i % 2}/{c}" for c in range(4)),
                    w=tuple(f"rA/A/{g * 4 + c}/{mt}" for c in range(4)))

            def Seu(i):
                g, mt, gsl = ctx(i)
                ptU = pbt[i % 2]
                p.op("scalar", lambda e, g=g, mt=mt, ptU=ptU: e.copy(
                    U_tok[:, g * 4:(g + 1) * 4, mt * 128:(mt + 1) * 128],
                    ptU[:, 0:512].rearrange("p (a b) -> p a b", a=4, b=128)),
                    r=(f"pb{4 + i % 2}",),
                    w=tuple(f"rE/U/{g * 4 + c}/{mt}" for c in range(4)))

            if "noppl" in ABL:
                return [(0, Sln), (0, Vcs), (0, Sai), (0, Vu), (0, Ttr),
                        (0, Vea), (0, Seu)]
            return [(0, Sln), (1, Vcs), (2, Sai), (3, Vu), (4, Ttr),
                    (5, Vea), (5, Seu)]

        pipeline(NG * 8, ph2b_stages())
        p.op("vector", lambda e: e.memset(acum[:, :, 0:1], 1.0), w=("s/acum0",))
        for mt in range(8):
            p.op("vector", lambda e, mt=mt: e.tensor_tensor_scan(
                acum[:, mt, 1:NCH + 1], acF[:, mt, 0:NCH], zer16[:, 0:NCH],
                1.0, ALU.mult, ALU.add),
                r=tuple(f"s/acF/{mt}/{g}" for g in range(NG)) + ("c/zer16",),
                w=(f"s/acum/{mt}",))
        acF_all = tuple(f"s/acF/{mt}/{g}" for mt in range(8) for g in range(NG))
        acum_all = ("s/acum0",) + tuple(f"s/acum/{mt}" for mt in range(8))
        if "notab" not in ABL:
            for half in range(2):
                for mm_ in range(4):
                    mt = half * 4 + mm_
                    p.op("tensor", lambda e, mt=mt, mm_=mm_: e.transpose(
                        pb[5][0:NCH, mm_ * 128:(mm_ + 1) * 128],
                        acF[:, mt, 0:NCH], identF[:]),
                        r=acF_all + ("c/identF",), w=("pb5",))
                p.op("vector", lambda e, half=half: e.tensor_copy(
                    acFT16[0:NCH, half * 512:(half + 1) * 512],
                    pb[5][0:NCH, 0:512]),
                    r=("pb5",), w=("s/acFT16",))
            for half in range(2):
                for mm_ in range(4):
                    mt = half * 4 + mm_
                    p.op("tensor", lambda e, mt=mt, mm_=mm_: e.transpose(
                        pb[5][0:NCH, mm_ * 128:(mm_ + 1) * 128],
                        acum[:, mt, 0:NCH], identF[:]),
                        r=acum_all + ("c/identF",), w=("pb5",))
                p.op("scalar", lambda e, half=half: e.copy(
                    acumT16[0:NCH, half * 512:(half + 1) * 512],
                    pb[5][0:NCH, 0:512]),
                    r=("pb5",), w=("s/acumT16",))
        if debug and upto >= 3:
            p.dma("gpsimd", lambda e: e.dma_start(dbg["d_A"].ap(), A_tok[:]),
                  r=tuple(f"rA/A/{c}/{m}" for c in range(NCH) for m in range(8)))
            p.dma("gpsimd", lambda e: e.dma_start(dbg["d_U"].ap(), U_tok[:]),
                  r=tuple(f"rE/U/{c}/{m}" for c in range(NCH) for m in range(8)))
            p.dma("gpsimd", lambda e: e.dma_start(dbg["d_acF"].ap(), acF[:]),
                  r=tuple(f"s/acF/{m}/{g}" for m in range(8) for g in range(NG)))
            p.dma("gpsimd", lambda e: e.dma_start(dbg["d_acum"].ap(), acum[:]),
                  r=("s/acum0",) + tuple(f"s/acum/{m}" for m in range(8)))
        if upto <= 3:
            p.barrier_engine("sync")
            block = st.enter_context(nc.Block())
            p.emit(block, sems)
            return nc

        p.close_role("rD")
        s1bP = [(s1b, "s/s1b"), (sinb, "s/sinb")]

        def ph3_stages():
            def ctx(i):
                h, ci = divmod(i, NCH)
                return (h, ci, slice(h * 256, (h + 1) * 256),
                        slice(ci * 128, (ci + 1) * 128))

            def Tkt(i):
                h, ci, hsl, cisl = ctx(i)
                ptX = pt[i % 2]
                for cc in range(2):
                    p.op("tensor", lambda e, cc=cc, cisl=cisl, h=h, ptX=ptX:
                         e.transpose(ptX[:, cc * 128:(cc + 1) * 128],
                                     kF[:, 2 * h + cc, cisl], identB[:]),
                         r=(f"rC/kF/{2 * h + cc}/{ci // 4}", "c/identB"),
                         w=(f"pt{i % 2}/{cc}",))

            pbP = [(pb[4], "pb4"), (pb[0], "pb0")]

            def Tp(i):
                h, ci, hsl, cisl = ctx(i)
                bk, bkr = pbP[i % 2]
                for cc in range(2):
                    p.op("tensor", lambda e, cc=cc, cisl=cisl, h=h, bk=bk:
                         e.matmul(bk[:, 0:128], kF[:, 2 * h + cc, cisl],
                                  qF[:, 2 * h + cc, cisl],
                                  start=(cc == 0), stop=(cc == 1),
                                  skip_group_check=True),
                         r=(f"rC/kF/{2 * h + cc}/{ci // 4}",
                            f"rB/qF/{2 * h + cc}/{ci // 4}"),
                         w=(bkr,))

            def Skt(i):
                ptX = pt[i % 2]
                p.op("scalar", lambda e, ptX=ptX, cur=i % 2: e.copy(
                    ktT[cur][:], ptX[:, 0:256]),
                    r=(f"pt{i % 2}/0", f"pt{i % 2}/1"), w=(f"s/ktT{i % 2}",))

            def Vp(i):
                bk, bkr = pbP[i % 2]
                p.op("vector", lambda e, bk=bk, cur=i % 2: e.tensor_tensor(
                    pmk[cur][:], bk[:, 0:128], maskB[:], ALU.mult),
                    r=(bkr, "c/maskB"), w=(f"s/pmk{i % 2}",))

            def Tku(i):
                h, ci, hsl, cisl = ctx(i)
                pbC, pbCr = (pb[2], "pb2") if i % 2 == 0 else (pb[1], "pb1")
                for cc in range(2):
                    p.op("tensor", lambda e, cc=cc, cur=i % 2, ci=ci, hsl=hsl,
                         pbC=pbC: e.matmul(
                             pbC[:, cc * 256:(cc + 1) * 256],
                             ktT[cur][:, cc * 128:(cc + 1) * 128],
                             U_tok[:, ci, hsl], start=True, stop=True),
                         r=(f"s/ktT{i % 2}", f"rE/U/{ci}/{2 * h + cc}"),
                         w=(pbCr,))

            pbB = [(pb[3], "pb3"), (pb[5], "pb5")]

            def Tbc(i):
                h, ci, hsl, cisl = ctx(i)
                bk, bkr = pbB[i % 2]
                p.op("tensor", lambda e, ci=ci, hsl=hsl, bk=bk: e.matmul(
                    bk[:, 0:256], selT[0:NCH, ci * 128:(ci + 1) * 128],
                    acFT16[0:NCH, hsl], start=True, stop=True),
                    r=("c/selT", "s/acFT16"), w=(bkr,))

            def Tok(i):
                h, ci, hsl, cisl = ctx(i)
                bk, bkr = pbP[i % 2]
                one = (ci == 0)
                p.op("tensor", lambda e, cur=i % 2, ci=ci, hsl=hsl, bk=bk,
                     one=one: e.matmul(
                         bk[:, 128:384], pmk[cur][:], U_tok[:, ci, hsl],
                         start=True, stop=one, skip_group_check=True),
                     r=(f"s/pmk{i % 2}", f"rE/U/{ci}/{2 * h}",
                        f"rE/U/{ci}/{2 * h + 1}"),
                     w=(bkr,))
                if not one:
                    sb_, sbr = s1bP[(i - 1) % 2]
                    for cc in range(2):
                        p.op("tensor", lambda e, cc=cc, cisl=cisl, h=h, bk=bk,
                             sb_=sb_: e.matmul(
                                 bk[:, 128:384], qF[:, 2 * h + cc, cisl],
                                 sb_[:, cc * 256:(cc + 1) * 256],
                                 start=False, stop=(cc == 1),
                                 skip_group_check=True),
                             r=(f"rB/qF/{2 * h + cc}/{ci // 4}", sbr), w=(bkr,))

            def Vst(i):
                h, ci, hsl, cisl = ctx(i)
                pbC, pbCr = (pb[2], "pb2") if i % 2 == 0 else (pb[1], "pb1")
                bcb, bcr = pbB[i % 2]
                if ci == 0:
                    p.op("vector", lambda e, pbC=pbC: e.tensor_copy(
                        s2f[1][:], pbC[:]), r=(pbCr,), w=("s/s2f1",))
                else:
                    p.op("vector", lambda e, pbC=pbC, cur=i % 2:
                         e.scalar_tensor_tensor(
                             s2f[1][:], pbC[:], 1.0, s1f[cur][:],
                             ALU.mult, ALU.add),
                         r=(pbCr, f"s/s1f{i % 2}"), w=("s/s2f1",))
                sb_, sbr = s1bP[i % 2]
                for half in range(2):
                    hs2 = slice(half * 256, (half + 1) * 256)
                    p.op("vector", lambda e, hs2=hs2, sb_=sb_, bcb=bcb:
                         e.tensor_tensor(sb_[:, hs2], s2f[1][:, hs2],
                                         bcb[:, 0:256], ALU.mult),
                         r=("s/s2f1", bcr), w=(sbr,))
                nxt = (i + 1) % 2
                for half in range(2):
                    hs2 = slice(half * 256, (half + 1) * 256)
                    p.op("vector", lambda e, hs2=hs2, nxt=nxt, bcb=bcb:
                         e.tensor_tensor(s1f[nxt][:, hs2], s2f[1][:, hs2],
                                         bcb[:, 0:256], ALU.mult),
                         r=("s/s2f1", bcr), w=(f"s/s1f{nxt}",))
                if ci == NCH - 1:
                    p.dma("gpsimd", lambda e, h=h, nxt=nxt: e.dma_start(
                        ag1_in.ap()[:, h * 512:(h + 1) * 512], s1f[nxt][:]),
                        r=(f"s/s1f{nxt}",), w=(f"s/ag1T/{h}",))

            def Vok(i):
                h, ci, hsl, cisl = ctx(i)
                bk, bkr = pbP[i % 2]
                p.op("vector", lambda e, ci=ci, hsl=hsl, bk=bk: e.tensor_tensor(
                    okloc[:, ci, hsl], bk[:, 128:384], A_tok[:, ci, hsl],
                    ALU.mult),
                    r=(bkr, f"rA/A/{ci}/{2 * h}", f"rA/A/{ci}/{2 * h + 1}"),
                    w=(f"rD/ok/{ci}/{h}",))

            return [(0, Tkt), (1, Skt), (1, Tku), (1, Tbc),
                    (2, Vp), (2, Vst), (2, Tok), (2, Vok), (0, Tp)]

        pipeline(4 * NCH, ph3_stages())
        p.dma("gpsimd", lambda e: e.dma_start(
            ag1_in.ap()[:, 2048:2056], acum[:, :, NCH],
            allow_slow_non_contiguous=True),
            r=tuple(f"s/acum/{m}" for m in range(8)), w=("s/ag1D",))
        p.cc(lambda e: e.collective_compute(
            "AllGather", ALU.bypass, replica_groups=[[0, 1, 2, 3], [4, 5, 6, 7]],
            ins=[ag1_in.ap()], outs=[ag1_out.ap()]),
            r=tuple(f"s/ag1T/{h}" for h in range(4)) + ("s/ag1D",),
            w=("s/ag1_out",))
        if debug and upto >= 4:
            p.dma("gpsimd", lambda e: e.dma_start(dbg["d_ok"].ap(), okloc[:]),
                  r=tuple(f"rD/ok/{c}/{h}" for c in range(NCH) for h in range(4)))
            p.dma("gpsimd", lambda e: e.dma_start(
                dbg["d_s1T"].ap(),
                ag1_in.ap()[:, 0:2048].rearrange("p (a b) -> p a b", a=4, b=512)),
                r=tuple(f"s/ag1T/{h}" for h in range(4)))
        if upto <= 4:
            p.barrier_engine("sync")
            block = st.enter_context(nc.Block())
            p.emit(block, sems)
            return nc

        p.close_role("rC")
        smP = [(sgt[0][:, 0:256], sgt[0][:, 256:512], "s/sgt0", "s/sgt0b"),
               (sgt[1][:, 0:256], sgt[1][:, 256:512], "s/sgt1", "s/sgt1"),
               (s2f[0][:, 0:256], s2f[0][:, 256:512], "s/s2f0/0", "s/s2f0/1")]
        sinbP = [(sinb, "s/sinb"), (acb, "s/acb")]

        def c1_comb_dma(h, sp):
            def fn():
                p.dma("sync", lambda e, sp=sp, h=h: e.dma_start(
                    peerTh[:], ag1_out.ap()[sp, :, h * 512:(h + 1) * 512]),
                    r=("s/ag1_out",), w=("s/s1f1",))
                p.dma("sync", lambda e, sp=sp: e.dma_start(
                    peerD[:], ag1_out.ap()[sp, :, 2048:2056]),
                    r=("s/ag1_out",), w=("s/peerD",))
            return fn

        def c1_comb_step(h, sp):
            F, Fr = sinf[h % 2], f"s/sinf{h % 2}"

            def fn():
                if sp == 0:
                    p.op("vector", lambda e, F=F: e.memset(F[:], 0.0), w=(Fr,))
                p.op("vector", lambda e, h=h: e.tensor_scalar(
                    diagt[:, 0:128], identB[:], peerD[:, 2 * h:2 * h + 1],
                    None, ALU.mult),
                    r=("c/identB", "s/peerD"), w=("s/diagt",))
                p.op("vector", lambda e, h=h: e.tensor_scalar(
                    diagt[:, 128:256], identB[:], peerD[:, 2 * h + 1:2 * h + 2],
                    None, ALU.mult),
                    r=("c/identB", "s/peerD"), w=("s/diagt",))
                p.op("tensor", lambda e: e.matmul(
                    pb[2][:, 0:256], onesB[:], diagt[:, 0:256],
                    start=True, stop=True),
                    r=("c/onesB", "s/diagt"), w=("pb2",))
                for half in range(2):
                    p.op("scalar", lambda e, sp=sp, half=half: e.activation(
                        dsel2[:, half * 256:(half + 1) * 256], pb[2][:, 0:256],
                        AF.Identity, scale=selm[:, 3 + sp:4 + sp],
                        bias=selm[:, 6 + sp:7 + sp]),
                        r=("pb2", "c/selm"), w=("s/s1f0",))
                p.op("vector", lambda e, F=F: e.tensor_tensor(
                    s2f[1][:], F[:], dsel2[:], ALU.mult),
                    r=(Fr, "s/s1f0"), w=("s/s2f1",))
                p.op("vector", lambda e, sp=sp, F=F: e.scalar_tensor_tensor(
                    F[:], peerTh[:], selm[:, sp:sp + 1], s2f[1][:],
                    ALU.mult, ALU.add),
                    r=("s/s1f1", "c/selm", "s/s2f1"), w=(Fr,))
            return fn

        spg = 3 if NCH >= 10 else 1
        c1_pre = {}
        for h in range(4):
            base = h * NCH
            for sp in range(3):
                stepslot = base - 1 - (2 - sp) * spg
                c1_pre.setdefault(stepslot - 1, []).append(c1_comb_dma(h, sp))
                c1_pre.setdefault(stepslot, []).append(c1_comb_step(h, sp))

        def c1_stages():
            def ctx(i):
                h, ci = divmod(i, NCH)
                return (h, ci, slice(h * 256, (h + 1) * 256),
                        slice(ci * 128, (ci + 1) * 128))

            bcP = [(pb[3], "pb3"), (pb[0], "pb0")]
            corP1 = [(pb[5], "pb5"), (pb[4], "pb4")]

            def Tbc1(i):
                h, ci, hsl, cisl = ctx(i)
                bk, bkr = bcP[i % 2]
                p.op("tensor", lambda e, ci=ci, hsl=hsl, bk=bk: e.matmul(
                    bk[:, 0:256], selT[0:NCH, ci * 128:(ci + 1) * 128],
                    acumT16[0:NCH, hsl], start=True, stop=True),
                    r=("c/selT", "s/acumT16"), w=(bkr,))

            def Vsb(i):
                h, ci, hsl, cisl = ctx(i)
                bk, bkr = bcP[i % 2]
                sbX, sbr = sinbP[i % 2]
                Fr = f"s/sinf{h % 2}"
                for half in range(2):
                    hs2 = slice(half * 256, (half + 1) * 256)
                    p.op("vector", lambda e, hs2=hs2, h=h, bk=bk, sbX=sbX:
                         e.tensor_tensor(sbX[:, hs2], sinf[h % 2][:, hs2],
                                         bk[:, 0:256], ALU.mult),
                         r=(Fr, bkr), w=(sbr,))
                if debug and ci == 0:
                    p.dma("gpsimd", lambda e, h=h, sbX=sbX: e.dma_start(
                        dbg["d_sin"].ap()[:, h, :], sbX[:]), r=(sbr,))

            def Tcor(i):
                h, ci, hsl, cisl = ctx(i)
                bk, bkr = corP1[i % 2]
                sbX, sbr = sinbP[i % 2]
                for cc in range(2):
                    p.op("tensor", lambda e, cc=cc, cisl=cisl, h=h, bk=bk,
                         sbX=sbX: e.matmul(
                             bk[:, 0:256], qF[:, 2 * h + cc, cisl],
                             sbX[:, cc * 256:(cc + 1) * 256],
                             start=(cc == 0), stop=(cc == 1)),
                         r=(f"rB/qF/{2 * h + cc}/{ci // 4}", sbr),
                         w=(bkr,))

            def Vsm(i):
                h, ci, hsl, cisl = ctx(i)
                smxX, smeX, smxR, smeR = smP[i % 3]
                ares = (f"rA/A/{ci}/{2 * h}", f"rA/A/{ci}/{2 * h + 1}")
                p.op("vector", lambda e, ci=ci, hsl=hsl, bk=corP1[i % 2][0],
                     smxX=smxX: e.tensor_tensor(smxX[:], bk[:, 0:256],
                                                A_tok[:, ci, hsl], ALU.mult),
                     r=(corP1[i % 2][1],) + ares, w=(smxR,))
                p.op("vector", lambda e, ci=ci, hsl=hsl, smxX=smxX, smeX=smeX:
                     e.tensor_tensor(smeX[:], smxX[:], okloc[:, ci, hsl],
                                     ALU.add),
                     r=(smxR, f"rD/ok/{ci}/{h}"), w=(smeR,))
                p.op("vector", lambda e, smeX=smeX, smsX=sms[i % 2]:
                     e.reduce_max(smsX[:, 0:1], smeX[:], AX.X, negate=True),
                     r=(smeR,), w=(f"s/sms{i % 2}0",))

            def Sex(i):
                smxX, smeX, smxR, smeR = smP[i % 3]
                p.op("scalar", lambda e, smxX=smxX, smeX=smeX, smsX=sms[i % 2]:
                     e.activation(smxX[:], smeX[:], AF.Exp, bias=smsX[:, 0:1],
                                  accum_out=smsX[:, 1:2]),
                     r=(smeR, f"s/sms{i % 2}0"), w=(smxR, f"s/sms{i % 2}1"))

            def Vqt(i):
                h, ci, hsl, cisl = ctx(i)
                smxX, smeX, smxR, smeR = smP[i % 3]
                ares = (f"rA/A/{ci}/{2 * h}", f"rA/A/{ci}/{2 * h + 1}")
                p.op("vector", lambda e, smsX=sms[i % 2]: e.reciprocal(
                    smsX[:, 2:3], smsX[:, 1:2]),
                    r=(f"s/sms{i % 2}1",), w=(f"s/sms{i % 2}2",))
                p.op("vector", lambda e, ci=ci, hsl=hsl, smxX=smxX, smeX=smeX:
                     e.tensor_tensor(smeX[:], smxX[:], A_tok[:, ci, hsl],
                                     ALU.mult),
                     r=(smxR,) + ares, w=(smeR,))
                p.op("vector", lambda e, smeX=smeX, smsX=sms[i % 2],
                     qttX=qtt[i % 2]: e.tensor_scalar(
                         qttX[:], smeX[:], smsX[:, 2:3], None, ALU.mult),
                     r=(smeR, f"s/sms{i % 2}2"), w=(f"s/qtt{i % 2}",))

            def Ttr(i):
                ptX = pt[i % 2]
                for cc in range(2):
                    p.op("tensor", lambda e, cc=cc, ptX=ptX, qttX=qtt[i % 2]:
                         e.transpose(ptX[:, cc * 128:(cc + 1) * 128],
                                     qttX[:, cc * 128:(cc + 1) * 128], identB[:]),
                         r=(f"s/qtt{i % 2}", "c/identB"), w=(f"pt{i % 2}/{cc}",))

            def Sev(i):
                h, ci, hsl, cisl = ctx(i)
                ptX = pt[i % 2]
                p.op("scalar", lambda e, h=h, cisl=cisl, ptX=ptX: e.copy(
                    qtF[:, 2 * h:2 * h + 2, cisl],
                    ptX[:, 0:256].rearrange("p (a b) -> p a b", a=2, b=128)),
                    r=(f"pt{i % 2}/0", f"pt{i % 2}/1"),
                    w=(f"rC/qt/{ci}/{2 * h}", f"rC/qt/{ci}/{2 * h + 1}"))

            return [(0, Tbc1), (1, Vsb), (1, Tcor), (2, Vsm), (3, Sex),
                    (4, Vqt), (5, Ttr), (6, Sev)]

        pipeline(4 * NCH, c1_stages(), pre=c1_pre, start=-9)
        if debug and upto >= 5:
            p.dma("gpsimd", lambda e: e.dma_start(dbg["d_qt"].ap(), qtF[:]),
                  r=tuple(f"rC/qt/{c}/{m}" for c in range(NCH) for m in range(8)))
        if upto <= 5:
            p.barrier_engine("sync")
            block = st.enter_context(nc.Block())
            p.emit(block, sems)
            return nc

        p.close_role("rA")
        vstP = [(vst[0], "s/vst0"), (vst[1], "s/vst1"),
                (ktT[0], "s/ktT0"), (ktT[1], "s/ktT1"),
                (qtt[0], "s/qtt0"), (qtt[1], "s/qtt1")]
        s2bP = [(s2b, "s/s2b"), (sin2b, "s/sin2b")]

        def ph4_stages():
            def ctx(i):
                h, ci = divmod(i, NCH)
                return (h, ci, slice(h * 256, (h + 1) * 256),
                        slice(ci * 128, (ci + 1) * 128))

            def Dv(i):
                h, ci, hsl, cisl = ctx(i)
                vb, vbr = vstP[i % 6]
                p.dma("sync", lambda e, ci=ci, h=h, vb=vb: e.dma_start(
                    vb[:], v_spill.ap()[ci * 128:(ci + 1) * 128,
                                        h * 256:(h + 1) * 256]),
                    r=(f"s/vsp/{ci}",), w=(vbr,))

            pbP2 = [(pb[4], "pb4"), (pb[5], "pb5")]

            def Tuf(i):
                h, ci, hsl, cisl = ctx(i)
                ptX = pt[i % 2]
                for mc in range(2):
                    p.op("tensor", lambda e, mc=mc, ci=ci, h=h, ptX=ptX:
                         e.transpose(
                             ptX[:, mc * 128:(mc + 1) * 128],
                             U_tok[:, ci, h * 256 + mc * 128:h * 256 + (mc + 1) * 128],
                             identB[:]),
                         r=(f"rE/U/{ci}/{2 * h + mc}", "c/identB"),
                         w=(f"pt{i % 2}/{mc}",))

            def Vuf(i):
                ptX = pt[i % 2]
                p.op("vector", lambda e, ptX=ptX, cur=i % 2: e.tensor_copy(
                    ufT[cur][:], ptX[:, 0:256]),
                    r=(f"pt{i % 2}/0", f"pt{i % 2}/1"), w=(f"s/ufT{i % 2}",))

            def Tp2(i):
                h, ci, hsl, cisl = ctx(i)
                bk, bkr = pbP2[i % 2]
                for mc in range(2):
                    p.op("tensor", lambda e, mc=mc, cur=i % 2, ci=ci, h=h,
                         cisl=cisl, bk=bk: e.matmul(
                             bk[:, 0:128],
                             ufT[cur][:, mc * 128:(mc + 1) * 128],
                             qtF[:, 2 * h + mc, cisl],
                             start=(mc == 0), stop=(mc == 1),
                             skip_group_check=True),
                         r=(f"s/ufT{i % 2}", f"rC/qt/{ci}/{2 * h + mc}"),
                         w=(bkr,))

            def Vp2(i):
                bk, bkr = pbP2[i % 2]
                p.op("vector", lambda e, bk=bk, cur=i % 2: e.tensor_tensor(
                    pmk[cur][:], bk[:, 0:128], maskB[:], ALU.mult),
                    r=(bkr, "c/maskB"), w=(f"s/pmk{i % 2}",))

            def Tku2(i):
                h, ci, hsl, cisl = ctx(i)
                vb, vbr = vstP[i % 6]
                pbC, pbCr = (pb[2], "pb2") if i % 2 == 0 else (pb[1], "pb1")
                for mc in range(2):
                    p.op("tensor", lambda e, mc=mc, ci=ci, h=h, vb=vb, pbC=pbC:
                         e.matmul(
                             pbC[:, mc * 256:(mc + 1) * 256],
                             U_tok[:, ci, h * 256 + mc * 128:h * 256 + (mc + 1) * 128],
                             vb[:], start=True, stop=True),
                         r=(f"rE/U/{ci}/{2 * h + mc}", vbr), w=(pbCr,))

            def Tov(i):
                h, ci, hsl, cisl = ctx(i)
                bk, bkr = pbP2[i % 2]
                vb, vbr = vstP[i % 6]
                one = (ci == 0)
                for vv in range(2):
                    vsl = slice(128 + vv * 128, 128 + (vv + 1) * 128)
                    p.op("tensor", lambda e, vv=vv, cur=i % 2, vsl=vsl, vb=vb,
                         bk=bk, one=one: e.matmul(
                             bk[:, vsl], vb[:, vv * 128:(vv + 1) * 128],
                             pmk[cur][:], start=True, stop=one,
                             skip_group_check=True),
                         r=(vbr, f"s/pmk{i % 2}"), w=(bkr,))
                    if not one:
                        sb_, sbr = s2bP[(i - 1) % 2]
                        for mc in range(2):
                            p.op("tensor", lambda e, vv=vv, mc=mc, h=h,
                                 cisl=cisl, vsl=vsl, bk=bk, sb_=sb_: e.matmul(
                                     bk[:, vsl],
                                     sb_[:, mc * 256 + vv * 128:mc * 256 + (vv + 1) * 128],
                                     qtF[:, 2 * h + mc, cisl],
                                     start=False, stop=(mc == 1),
                                     skip_group_check=True),
                                 r=(sbr, f"rC/qt/{ci}/{2 * h + mc}"), w=(bkr,))

            def Sov(i):
                h, ci, hsl, cisl = ctx(i)
                bk, bkr = pbP2[i % 2]
                p.op("scalar", lambda e, h=h, cisl=cisl, bk=bk: e.copy(
                    ovF[:, 2 * h:2 * h + 2, cisl],
                    bk[:, 128:384].rearrange("p (a b) -> p a b", a=2, b=128)),
                    r=(bkr,),
                    w=(f"rA/ov/{ci}/{2 * h}", f"rA/ov/{ci}/{2 * h + 1}"))

            def Vst2(i):
                h, ci, hsl, cisl = ctx(i)
                pbC, pbCr = (pb[2], "pb2") if i % 2 == 0 else (pb[1], "pb1")
                cur, nxt = i % 2, (i + 1) % 2
                curres = ("s/s2f0/0", "s/s2f0/1") if cur == 0 else ("s/s2f1",)
                nxtres = ("s/s2f0/0", "s/s2f0/1") if nxt == 0 else ("s/s2f1",)
                if ci == 0:
                    p.op("vector", lambda e, pbC=pbC: e.tensor_copy(
                        s1f[0][:], pbC[:]), r=(pbCr,), w=("s/s1f0",))
                else:
                    p.op("vector", lambda e, pbC=pbC, cur=cur:
                         e.scalar_tensor_tensor(
                             s1f[0][:], pbC[:], 1.0, s2f[cur][:],
                             ALU.mult, ALU.add),
                         r=(pbCr,) + curres, w=("s/s1f0",))
                sb_, sbr = s2bP[i % 2]
                for mc in range(2):
                    msl = slice(mc * 256, (mc + 1) * 256)
                    p.op("scalar", lambda e, mc=mc, msl=msl, ci=ci, h=h,
                         sb_=sb_: e.activation(
                             sb_[:, msl], s1f[0][:, msl], AF.Identity,
                             scale=acF[:, 2 * h + mc, ci:ci + 1]),
                         r=("s/s1f0", f"s/acF/{2 * h + mc}/{ci // 4}"),
                         w=(sbr,))
                for mc in range(2):
                    msl = slice(mc * 256, (mc + 1) * 256)
                    p.op("vector", lambda e, mc=mc, msl=msl, ci=ci, h=h,
                         nxt=nxt: e.tensor_scalar(
                             s2f[nxt][:, msl], s1f[0][:, msl],
                             acF[:, 2 * h + mc, ci:ci + 1], None, ALU.mult),
                         r=("s/s1f0", f"s/acF/{2 * h + mc}/{ci // 4}"),
                         w=nxtres)
                if ci == NCH - 1:
                    finres = ("s/s2f0/0", "s/s2f0/1") if nxt == 0 else ("s/s2f1",)
                    p.dma("gpsimd", lambda e, h=h, nxt=nxt: e.dma_start(
                        ag2_in.ap()[:, h * 512:(h + 1) * 512], s2f[nxt][:]),
                        r=finres, w=(f"s/ag2T/{h}",))

            return [(0, Dv), (0, Tuf), (0, Vuf), (2, Tku2),
                    (2, Vp2), (2, Vst2), (2, Tov), (2, Sov), (0, Tp2)]

        pipeline(4 * NCH, ph4_stages())
        p.cc(lambda e: e.collective_compute(
            "AllGather", ALU.bypass, replica_groups=[[0, 1, 2, 3], [4, 5, 6, 7]],
            ins=[ag2_in.ap()], outs=[ag2_out.ap()]),
            r=tuple(f"s/ag2T/{h}" for h in range(4)), w=("s/ag2_out",))
        if debug and upto >= 6:
            p.dma("gpsimd", lambda e: e.dma_start(dbg["d_ovF"].ap(), ovF[:]),
                  r=tuple(f"rA/ov/{c}/{m}" for c in range(NCH) for m in range(8)))
            p.dma("gpsimd", lambda e: e.dma_start(
                dbg["d_s2T"].ap(),
                ag2_in.ap()[:, 0:2048].rearrange("p (a b) -> p a b", a=4, b=512)),
                r=tuple(f"s/ag2T/{h}" for h in range(4)))
        if upto <= 6:
            p.barrier_engine("sync")
            block = st.enter_context(nc.Block())
            p.emit(block, sems)
            return nc

        p.close_role("rB")
        ssqbank = [0, 1, 2, 3]
        ssqcnt = {}

        def c2_comb_dma(h, sp):
            def fn():
                p.dma("sync", lambda e, sp=sp, h=h: e.dma_start(
                    peerTh[:], ag2_out.ap()[sp, :, h * 512:(h + 1) * 512]),
                    r=("s/ag2_out",), w=("s/s1f1",))
                p.dma("sync", lambda e, sp=sp: e.dma_start(
                    peerD[:], ag1_out.ap()[sp, :, 2048:2056]),
                    r=("s/ag1_out",), w=("s/peerD",))
            return fn

        def c2_comb_step(h, sp):
            F, Fr = sin2f[h % 2], f"s/sin2f{h % 2}"

            def fn():
                if sp == 0:
                    p.op("vector", lambda e, F=F: e.memset(F[:], 0.0), w=(Fr,))
                for mc in range(2):
                    msl = slice(mc * 256, (mc + 1) * 256)
                    p.op("vector", lambda e, sp=sp, h=h, mc=mc: e.tensor_scalar(
                        lt[:, 4 + mc:5 + mc], peerD[:, 2 * h + mc:2 * h + mc + 1],
                        selm[:, 3 + sp:4 + sp], selm[:, 6 + sp:7 + sp],
                        ALU.mult, ALU.add),
                        r=("s/peerD", "c/selm"), w=(f"s/lt{4 + mc}",))
                    p.op("vector", lambda e, mc=mc, msl=msl, F=F: e.tensor_scalar(
                        s1f[0][:, msl], F[:, msl], lt[:, 4 + mc:5 + mc],
                        None, ALU.mult),
                        r=(Fr, f"s/lt{4 + mc}"), w=("s/s1f0",))
                p.op("vector", lambda e, sp=sp, F=F: e.scalar_tensor_tensor(
                    F[:], peerTh[:], selm[:, sp:sp + 1], s1f[0][:],
                    ALU.mult, ALU.add),
                    r=("s/s1f1", "c/selm", "s/s1f0"), w=(Fr,))
                if debug and sp == 2:
                    p.dma("gpsimd", lambda e, h=h, F=F: e.dma_start(
                        dbg["d_sin2"].ap()[:, h, :], F[:]), r=(Fr,))
            return fn

        c2_pre = {}
        for h in range(4):
            base = h * NCH
            for sp in range(3):
                stepslot = base - 1 - (2 - sp) * spg
                c2_pre.setdefault(stepslot - 1, []).append(c2_comb_dma(h, sp))
                c2_pre.setdefault(stepslot, []).append(c2_comb_step(h, sp))

        sgP = [(sgt[0][:, 0:256], "s/sgt0"), (sgt[1][:, 0:256], "s/sgt1")]
        o2P = [(o2t[:, 0:256], "s/o2t"), (o2t[:, 256:512], "s/o2tb")]
        corP = [(pb[4], ("pb4",)), (pb[5], ("pb5",))]

        def c2p1_stages():
            def ctx(i):
                h, ci = divmod(i, NCH)
                return (h, ci, slice(h * 256, (h + 1) * 256),
                        slice(ci * 128, (ci + 1) * 128))

            def Vs2(i):
                h, ci, hsl, cisl = ctx(i)
                s2bX, s2bR = s2bP[i % 2]
                for mc in range(2):
                    msl = slice(mc * 256, (mc + 1) * 256)
                    acres = "s/acum0" if ci == 0 else f"s/acum/{2 * h + mc}"
                    p.op("vector", lambda e, mc=mc, msl=msl, ci=ci, h=h,
                         s2bX=s2bX, F=sin2f[h % 2]: e.tensor_scalar(
                             s2bX[:, msl], F[:, msl],
                             acum[:, 2 * h + mc, ci:ci + 1], None, ALU.mult),
                         r=(f"s/sin2f{h % 2}", acres), w=(s2bR,))

            def Tcor2(i):
                h, ci, hsl, cisl = ctx(i)
                s2bX, s2bR = s2bP[i % 2]
                pbA, pbAw = corP[i % 2]
                for vv in range(2):
                    vsl = slice(vv * 128, (vv + 1) * 128)
                    for mc in range(2):
                        p.op("tensor", lambda e, vv=vv, mc=mc, h=h, cisl=cisl,
                             vsl=vsl, pbA=pbA, s2bX=s2bX: e.matmul(
                                 pbA[:, vsl],
                                 s2bX[:, mc * 256 + vv * 128:mc * 256 + (vv + 1) * 128],
                                 qtF[:, 2 * h + mc, cisl],
                                 start=(mc == 0), stop=(mc == 1),
                                 skip_group_check=True),
                             r=(s2bR, f"rC/qt/{ci}/{2 * h + mc}"), w=pbAw)

            def Vov(i):
                h, ci, hsl, cisl = ctx(i)
                sgX, sgR = sgP[i % 2]
                pbA, pbAw = corP[i % 2]
                p.op("vector", lambda e, h=h, cisl=cisl, pbA=pbA, sgX=sgX:
                     e.tensor_tensor(
                         sgX.rearrange("p (a b) -> p a b", a=2, b=128),
                         pbA[:, 0:256].rearrange("p (a b) -> p a b", a=2, b=128),
                         ovF[:, 2 * h:2 * h + 2, cisl], ALU.add),
                     r=pbAw + (f"rA/ov/{ci}/{2 * h}", f"rA/ov/{ci}/{2 * h + 1}"),
                     w=(sgR,))

            def Ssi(i):
                h, ci, hsl, cisl = ctx(i)
                sgX, sgR = sgP[i % 2]
                p.op("scalar", lambda e, h=h, cisl=cisl, sgX=sgX: e.activation(
                    ormsF[:, 2 * h:2 * h + 2, cisl],
                    sgX.rearrange("p (a b) -> p a b", a=2, b=128), AF.Silu),
                    r=(sgR,),
                    w=(f"rB/orms/{ci}/{2 * h}", f"rB/orms/{ci}/{2 * h + 1}"))

            def Vsq(i):
                h, ci, hsl, cisl = ctx(i)
                o2X, o2R = o2P[i % 2]
                p.op("vector", lambda e, h=h, cisl=cisl, o2X=o2X: e.tensor_tensor(
                    o2X.rearrange("p (a b) -> p a b", a=2, b=128),
                    ormsF[:, 2 * h:2 * h + 2, cisl],
                    ormsF[:, 2 * h:2 * h + 2, cisl], ALU.mult),
                    r=(f"rB/orms/{ci}/{2 * h}", f"rB/orms/{ci}/{2 * h + 1}"),
                    w=(o2R,))

            def Tssq(i):
                h, ci, hsl, cisl = ctx(i)
                g = ci // 4
                o2X, o2R = o2P[i % 2]
                for vv in range(2):
                    nfirst = ssqcnt.get(g, 0)
                    ssqcnt[g] = nfirst + 1
                    p.op("tensor", lambda e, g=g, ci=ci, vv=vv, nfirst=nfirst,
                         o2X=o2X: e.matmul(
                             pb[ssqbank[g]][0:1, (ci % 4) * 128:(ci % 4 + 1) * 128],
                             onesc[:], o2X[:, vv * 128:(vv + 1) * 128],
                             start=(nfirst == 0), stop=(nfirst == 31),
                             skip_group_check=True),
                         r=(o2R, "c/onesc"), w=(f"pb{ssqbank[g]}",))

            return [(0, Vs2), (1, Tcor2), (2, Vov), (3, Ssi), (4, Vsq),
                    (5, Tssq)]

        pipeline(4 * NCH, c2p1_stages(), pre=c2_pre, start=-9)
        if debug and upto >= 7:
            p.dma("gpsimd", lambda e: e.dma_start(dbg["d_orms"].ap(), ormsF[:]),
                  r=tuple(f"rB/orms/{c}/{m}" for c in range(NCH) for m in range(8)))
        if upto <= 7:
            p.barrier_engine("sync")
            block = st.enter_context(nc.Block())
            p.emit(block, sems)
            return nc

        p.close_role("rC")
        p.close_role("rD")
        wost = rview(rC, F32, [4, 1024], 0)
        for half in range(2):
            p.dma("sync", lambda e, half=half: e.dma_start(
                wost[:], woT.ap().rearrange("(a p) b -> p a b", p=128)[
                    :, half * 4:(half + 1) * 4, :]),
                r=(), w=("rC/wost",))
            for dq in range(4):
                dch = half * 4 + dq
                p.op("scalar", lambda e, dch=dch, dq=dq: e.activation(
                    woTg[:, dch, :], wost[:, dq, :], AF.Identity,
                    scale=gwp[:, dch:dch + 1]),
                    r=("rC/wost", "c/gwp"), w=(f"rC/woTg/{dch}",))
        ptf = [pt[0].ap().bitcast(F32), pt[1].ap().bitcast(F32)]
        ptfR = [tuple(f"pt0/{c}" for c in range(8)),
                tuple(f"pt1/{c}" for c in range(8))]
        rc16 = sb("rc16", [128, 16], F32)
        for g in range(NG):
            gb = ssqbank[g]
            p.op("vector", lambda e, gb=gb: e.tensor_scalar(
                srow, pb[gb][0:1, 0:512], 1.0 / 1024.0, EPS, ALU.mult, ALU.add),
                r=(f"pb{gb}",), w=("s/sgt1",))
            p.op("scalar", lambda e: e.activation(srowB, srow, AF.Ln),
                 r=("s/sgt1",), w=("s/sgt0",))
            p.op("scalar", lambda e: e.activation(rrow, srowB, AF.Exp, scale=-0.5),
                 r=("s/sgt0",), w=("s/aFt",))
            for tt in range(4):
                p.op("tensor", lambda e, tt=tt, gb=gb: e.matmul(
                    pb[gb][:, tt:tt + 1], rrow[0:1, tt * 128:(tt + 1) * 128],
                    onesr[0:1, 0:1], start=True, stop=True,
                    skip_group_check=True),
                    r=("s/aFt", "c/onesr"), w=(f"pb{gb}",))
            p.op("vector", lambda e, g=g, gb=gb: e.tensor_copy(
                rc16[:, g * 4:(g + 1) * 4], pb[gb][:, 0:4]),
                r=(f"pb{gb}",), w=(f"s/rc16/{g}",))

        lt2t = sb("lt2t", [128, 8], F32)
        mv2 = sb("mv2t", [128, 2], F32)
        bn12s = sb("bn12s", [128, 12], F32)
        ltP2 = [(lt, "s/lt"), (lt2t, "s/lt2t")]
        mvP2 = [(mv, "s/mv"), (mv2, "s/mv2t")]
        bnP2 = [(bn12, "s/bn12"), (bn12s, "s/bn12s")]

        def c2p2_stages():
            def WO(k):
                tisl = slice(k * 128, (k + 1) * 128)
                bks = (4, 5) if k % 2 == 0 else (0, 1)
                ormres = tuple(f"rB/orms/{k}/{m}" for m in range(8))
                for nh in range(2):
                    bank = bks[nh]
                    for dvch in range(8):
                        p.op("tensor", lambda e, dvch=dvch, nh=nh, tisl=tisl,
                             bank=bank: e.matmul(
                                 pb[bank][:], ormsF[:, dvch, tisl],
                                 woTg[:, dvch, nh * 512:(nh + 1) * 512],
                                 start=(dvch == 0), stop=(dvch == 7)),
                             r=ormres + (f"rC/woTg/{dvch}",), w=(f"pb{bank}",))

            def TL(k):
                ti = k
                tisl = slice(ti * 128, (ti + 1) * 128)
                bks = (4, 5) if k % 2 == 0 else (0, 1)
                b0r, b1r = f"pb{bks[0]}", f"pb{bks[1]}"
                ltX, ltR = ltP2[k % 2]
                mvX, mvR = mvP2[k % 2]
                bnX, bnR = bnP2[k % 2]
                rcr = f"s/rc16/{k // 4}"
                rck = rc16[:, k:k + 1]
                p.dma("sync", lambda e, tisl=tisl: e.dma_start(
                    xrl[:, 0:1024], x_in.ap()[tisl, :]),
                    r=(), w=("s/sinf0", "s/sinf1"))
                p.op("vector", lambda e, bnX=bnX, b0=bks[0]: e.bn_stats(
                    bnX[:, 0:6], pb[b0][:]), r=(b0r,), w=(bnR,))
                p.op("vector", lambda e, bnX=bnX, b1=bks[1]: e.bn_stats(
                    bnX[:, 6:12], pb[b1][:]), r=(b1r,), w=(bnR,))
                p.op("vector", lambda e, bnX=bnX, mvX=mvX: e.bn_aggr(
                    mvX[:], bnX[:]), r=(bnR,), w=(mvR,))
                p.op("vector", lambda e, ltX=ltX, rck=rck: e.tensor_tensor(
                    ltX[:, 4:5], rck, rck, ALU.mult), r=(rcr,), w=(ltR,))
                p.op("vector", lambda e, ltX=ltX, mvX=mvX, rck=rck:
                     e.tensor_tensor(ltX[:, 5:6], mvX[:, 0:1], rck, ALU.mult),
                     r=(mvR, rcr), w=(ltR,))
                p.op("vector", lambda e, ltX=ltX, mvX=mvX: e.tensor_tensor(
                    ltX[:, 6:7], mvX[:, 1:2], ltX[:, 4:5], ALU.mult),
                    r=(mvR, ltR), w=(ltR,))
                p.op("vector", lambda e, ltX=ltX: e.tensor_scalar(
                    ltX[:, 7:8], ltX[:, 6:7], EPS, None, ALU.add),
                    r=(ltR,), w=(ltR,))
                p.op("scalar", lambda e, ltX=ltX: e.activation(
                    ltX[:, 0:1], ltX[:, 7:8], AF.Ln), r=(ltR,), w=(ltR,))
                p.op("scalar", lambda e, ltX=ltX: e.activation(
                    ltX[:, 1:2], ltX[:, 0:1], AF.Exp, scale=-0.5),
                    r=(ltR,), w=(ltR,))
                p.op("vector", lambda e, ltX=ltX, rck=rck: e.tensor_tensor(
                    ltX[:, 2:3], rck, ltX[:, 1:2], ALU.mult),
                    r=(rcr, ltR), w=(ltR,))
                p.op("vector", lambda e, ltX=ltX: e.scalar_tensor_tensor(
                    ltX[:, 3:4], ltX[:, 5:6], -1.0, ltX[:, 1:2],
                    ALU.mult, ALU.mult),
                    r=(ltR,), w=(ltR,))
                p.op("scalar", lambda e, ltX=ltX, b0=bks[0]: e.activation(
                    ptf[0][:], pb[b0][:], AF.Identity,
                    scale=ltX[:, 2:3], bias=ltX[:, 3:4]),
                    r=(b0r, ltR), w=ptfR[0])
                p.op("scalar", lambda e, ltX=ltX, b1=bks[1]: e.activation(
                    ptf[1][:], pb[b1][:], AF.Identity,
                    scale=ltX[:, 2:3], bias=ltX[:, 3:4]),
                    r=(b1r, ltR), w=ptfR[1])
                p.op("vector", lambda e: e.tensor_tensor(
                    x1s[:, 0:512], ptf[0][:], xrl[:, 0:512], ALU.add),
                    r=ptfR[0] + ("s/sinf0", "s/sinf1"),
                    w=("s/sin2f0", "s/sin2f1"))
                p.op("vector", lambda e: e.tensor_tensor(
                    x1s[:, 512:1024], ptf[1][:], xrl[:, 512:1024], ALU.add),
                    r=ptfR[1] + ("s/sinf0", "s/sinf1"),
                    w=("s/sin2f0", "s/sin2f1"))
                p.dma("sync", lambda e, tisl=tisl, ti=ti: e.dma_start(
                    x1_spill.ap()[tisl, :], x1s[:, 0:1024]),
                    r=("s/sin2f0", "s/sin2f1"), w=(f"s/x1sp/{ti}",))
                p.op("vector", lambda e, bnX=bnX: e.bn_stats(
                    bnX[:, 0:6], x1s[:, 0:512]),
                    r=("s/sin2f0", "s/sin2f1"), w=(bnR,))
                p.op("vector", lambda e, bnX=bnX: e.bn_stats(
                    bnX[:, 6:12], x1s[:, 512:1024]),
                    r=("s/sin2f0", "s/sin2f1"), w=(bnR,))
                p.op("vector", lambda e, bnX=bnX, mvX=mvX: e.bn_aggr(
                    mvX[:], bnX[:]), r=(bnR,), w=(mvR,))
                p.op("vector", lambda e, ltX=ltX, mvX=mvX: e.tensor_scalar(
                    ltX[:, 7:8], mvX[:, 1:2], EPS, None, ALU.add),
                    r=(mvR,), w=(ltR,))
                p.op("scalar", lambda e, ltX=ltX: e.activation(
                    ltX[:, 0:1], ltX[:, 7:8], AF.Ln), r=(ltR,), w=(ltR,))
                p.op("scalar", lambda e, ltX=ltX: e.activation(
                    ltX[:, 1:2], ltX[:, 0:1], AF.Exp, scale=-0.5),
                    r=(ltR,), w=(ltR,))
                p.op("vector", lambda e, ltX=ltX, mvX=mvX: e.scalar_tensor_tensor(
                    ltX[:, 3:4], mvX[:, 0:1], -1.0, ltX[:, 1:2],
                    ALU.mult, ALU.mult),
                    r=(mvR, ltR), w=(ltR,))
                xt = xn2a[:, ti, :] if ti < 8 else xn2b[:, ti - 8, :]
                xreg = f"rD/xn2/{ti}" if ti < 8 else f"rE/xn2/{ti}"
                p.op("scalar", lambda e, xt=xt, ltX=ltX: e.activation(
                    xt, x1s[:, 0:1024], AF.Identity,
                    scale=ltX[:, 1:2], bias=ltX[:, 3:4]),
                    r=("s/sin2f0", "s/sin2f1", ltR), w=(xreg,))
                p.op("vector", lambda e, xt=xt, ti=ti: e.tensor_reduce(
                    amxall[:, ti:ti + 1], xt, AX.X, ALU.max,
                    apply_absolute_value=True),
                    r=(xreg,), w=(f"s/amx/{ti}",))

            return [(1, TL), (0, WO)]

        pipeline(NT, c2p2_stages())
        p.op("vector", lambda e: e.tensor_reduce(
            armx[:], amxall[:, 0:NT], AX.X, ALU.max),
            r=tuple(f"s/amx/{t}" for t in range(NT)), w=("s/armx",))
        p.dma("gpsimd", lambda e: e.dma_start(arm1_in.ap(), armx[:]),
              r=("s/armx",), w=("s/arm1_in",))
        p.cc(lambda e: e.collective_compute(
            "AllReduce", ALU.max, replica_groups=[list(range(8))],
            ins=[arm1_in.ap()], outs=[arm1_out.ap()]),
            r=("s/arm1_in",), w=("s/arm1_out",))
        p.dma("gpsimd", lambda e: e.dma_start(armx[:], arm1_out.ap()),
              r=("s/arm1_out",), w=("s/armx",))
        if debug and upto >= 8:
            p.dma("gpsimd", lambda e: e.dma_start(dbg["d_x1s"].ap(), x1_spill.ap()),
                  r=tuple(f"s/x1sp/{t}" for t in range(NT)))
            for ti in range(NT):
                xt = xn2a[:, ti, :] if ti < 8 else xn2b[:, ti - 8, :]
                xreg = f"rD/xn2/{ti}" if ti < 8 else f"rE/xn2/{ti}"
                p.dma("gpsimd", lambda e, xt=xt, ti=ti: e.dma_start(
                    dbg["d_xn2"].ap()[ti], xt), r=(xreg,))
            p.dma("gpsimd", lambda e: e.dma_start(
                dbg["d_amax"].ap()[:, 0:1], armx[:],
                allow_slow_non_contiguous=True), r=("s/armx",))
        if upto <= 8:
            p.barrier_engine("sync")
            block = st.enter_context(nc.Block())
            p.emit(block, sems)
            return nc

        p.close_role("rA")
        p.close_role("rB")
        p.close_role("rC")

        def scale_pair(col_lo, r_res, w_res):
            p.op("tensor", lambda e: e.matmul(
                pb[2][0:1, 0:128], armx[:], identF[:], start=True, stop=True),
                r=(r_res, "c/identF"), w=("pb2",))
            p.op("vector", lambda e: e.reduce_max(
                sc1[0:1, 0:1], pb[2][0:1, 0:128], AX.X),
                r=("pb2",), w=("s/sc1",))
            p.op("vector", lambda e: e.tensor_scalar(
                sc1[0:1, 1:2], sc1[0:1, 0:1], 1e-8, None, ALU.max),
                r=("s/sc1",), w=("s/sc1",))
            p.op("vector", lambda e: e.reciprocal(sc1[0:1, 2:3], sc1[0:1, 1:2]),
                 r=("s/sc1",), w=("s/sc1",))
            p.op("vector", lambda e: e.tensor_scalar(
                sc1[0:1, 3:4], sc1[0:1, 2:3], 32767.0, None, ALU.mult),
                r=("s/sc1",), w=("s/sc1",))
            p.op("vector", lambda e: e.tensor_scalar(
                sc1[0:1, 0:1], sc1[0:1, 1:2], 1.0 / 32767.0, None, ALU.mult),
                r=("s/sc1",), w=("s/sc1",))
            p.op("tensor", lambda e: e.matmul(
                pb[2][:, 0:1], onesrf[:], sc1[0:1, 3:4], start=True, stop=True),
                r=("c/onesrf", "s/sc1"), w=("pb2",))
            p.op("tensor", lambda e: e.matmul(
                pb[2][:, 1:2], onesrf[:], sc1[0:1, 0:1], start=True, stop=True),
                r=("c/onesrf", "s/sc1"), w=("pb2",))
            p.op("vector", lambda e, col_lo=col_lo: e.tensor_copy(
                scp[:, col_lo:col_lo + 2], pb[2][:, 0:2]),
                r=("pb2",), w=(w_res,))

        scale_pair(0, "s/armx", "s/scp01")
        for r_ in range(8):
            dst = winsA if r_ < 4 else winsB
            reg = "rB" if r_ < 4 else "rC"
            p.dma("sync", lambda e, r_=r_, dst=dst: e.dma_start(
                dst[:, :, (r_ % 4) * 512:(r_ % 4 + 1) * 512],
                agwi_out.ap()[r_].rearrange("(a p) b -> p a b", p=128)),
                r=("s/agwi_out",), w=(f"{reg}/wins/{r_ % 4}",))

        hst8 = [(hst[0], ("s/s2f0/0",)), (hst[1], ("s/s2f0/1",)),
                (s1b, ("s/s1b",)), (s2b, ("s/s2b",)),
                (sinb, ("s/sinb",)), (acb, ("s/acb",)),
                (sin2b, ("s/sin2b",)), (o2t, ("s/o2t", "s/o2tb"))]
        amx128 = sb("amx128", [128, 128], F32)

        def f1x_stages():
            def XQ(ti):
                xt = xn2a[:, ti, :] if ti < 8 else xn2b[:, ti - 8, :]
                xreg = f"rD/xn2/{ti}" if ti < 8 else f"rE/xn2/{ti}"
                p.op("vector", lambda e, xt=xt: e.tensor_scalar(
                    q16[:, 0:1024], xt, scp[:, 0:1], None, ALU.mult),
                    r=(xreg, "s/scp01"), w=("s/s1f0",))
                p.op("vector", lambda e: e.tensor_scalar(
                    xsimt[:, 0:1024], q16[:, 0:1024], scp[:, 1:2], None,
                    ALU.mult),
                    r=("s/s1f0", "s/scp01"), w=("s/s1f1",))
                bank = ti % 2
                for dch in range(8):
                    p.op("tensor", lambda e, dch=dch, bank=bank: e.transpose(
                        pt[bank][:, dch * 128:(dch + 1) * 128],
                        xsimt[:, dch * 128:(dch + 1) * 128], identB[:]),
                        r=("s/s1f1", "c/identB"), w=(f"pt{bank}/{dch}",))

            def XE(ti):
                bank = ti % 2
                eng = "scalar" if ti % 2 else "vector"
                p.op(eng, lambda e, ti=ti, bank=bank: ecopy(
                    e, xsimF[:, :, ti * 128:(ti + 1) * 128],
                    pt[bank].ap().rearrange("p (a b) -> p a b", a=8, b=128)),
                    r=tuple(f"pt{bank}/{dch}" for dch in range(8)),
                    w=tuple(f"rA/xs/{ti}/{dch}" for dch in range(8)))

            return [(0, XQ), (1, XE)]

        pipeline(NT, f1x_stages())

        def f1mm_stages():
            def MM(m):
                g, dffc = divmod(m, 32)
                wbufv = winsA if dffc < 16 else winsB
                wres = f"{'rB' if dffc < 16 else 'rC'}/wins/{(dffc % 16) // 4}"
                bank = m % 2
                xres = tuple(f"rA/xs/{g * 4 + t}/{k}"
                             for t in range(4) for k in range(8))
                for kch in range(8):
                    p.op("tensor", lambda e, dffc=dffc, g=g, kch=kch,
                         bank=bank, wbufv=wbufv: e.matmul(
                             pb[bank][:],
                             wbufv[:, kch, (dffc % 16) * 128:(dffc % 16 + 1) * 128],
                             xsimF[:, kch, g * 512:(g + 1) * 512],
                             start=(kch == 0), stop=(kch == 7)),
                         r=(wres,) + xres, w=(f"pb{bank}",))

            def GE(m):
                g, dffc = divmod(m, 32)
                hb, hbr = hst8[m % 8]
                p.op("scalar", lambda e, dffc=dffc, bank=m % 2, hb=hb:
                     e.activation(hb[:, 0:512], pb[bank][:], AF.Gelu,
                                  bias=binp[:, dffc:dffc + 1]),
                     r=(f"pb{m % 2}", "c/binp"), w=hbr)

            def AM(m):
                hb, hbr = hst8[m % 8]
                p.op("vector", lambda e, m=m, hb=hb: e.tensor_reduce(
                    amx128[:, m:m + 1], hb[:, 0:512], AX.X, ALU.max,
                    apply_absolute_value=True),
                    r=hbr, w=(f"s/amx/{m}",))

            def SP(m):
                g, dffc = divmod(m, 32)
                hb, hbr = hst8[m % 8]
                p.dma("sync", lambda e, dffc=dffc, g=g, hb=hb: e.dma_start(
                    hid_spill.ap()[dffc, :, g * 512:(g + 1) * 512],
                    hb[:, 0:512]),
                    r=hbr, w=(f"s/hsp/{dffc}/{g}",))

            return [(0, MM), (1, GE), (2, AM), (2, SP)]

        pipeline(NG * 32, f1mm_stages())
        p.op("vector", lambda e: e.tensor_reduce(
            armx[:], amx128[:, 0:NG * 32], AX.X, ALU.max),
            r=tuple(f"s/amx/{d}" for d in range(NG * 32)), w=("s/armx",))
        p.dma("gpsimd", lambda e: e.dma_start(arm2_in.ap(), armx[:]),
              r=("s/armx",), w=("s/arm2_in",))
        p.cc(lambda e: e.collective_compute(
            "AllReduce", ALU.max, replica_groups=[list(range(8))],
            ins=[arm2_in.ap()], outs=[arm2_out.ap()]),
            r=("s/arm2_in",), w=("s/arm2_out",))
        p.dma("gpsimd", lambda e: e.dma_start(armx[:], arm2_out.ap()),
              r=("s/arm2_out",), w=("s/armx",))
        if debug and upto >= 9:
            p.dma("gpsimd", lambda e: e.dma_start(dbg["d_hids"].ap(), hid_spill.ap()),
                  r=tuple(f"s/hsp/{d}/{g}" for d in range(32) for g in range(NG)))
            p.dma("gpsimd", lambda e: e.dma_start(
                dbg["d_wins"].ap()[:, :, 0:2048], winsA[:]),
                r=tuple(f"rB/wins/{r_}" for r_ in range(4)))
            p.dma("gpsimd", lambda e: e.dma_start(
                dbg["d_wins"].ap()[:, :, 2048:4096], winsB[:]),
                r=tuple(f"rC/wins/{r_}" for r_ in range(4)))
            p.dma("gpsimd", lambda e: e.dma_start(
                dbg["d_amax"].ap()[:, 1:2], armx[:],
                allow_slow_non_contiguous=True), r=("s/armx",))
        if upto <= 9:
            p.barrier_engine("sync")
            block = st.enter_context(nc.Block())
            p.emit(block, sems)
            return nc

        p.close_role("rA")
        p.close_role("rB")
        p.close_role("rC")
        p.close_role("rD")
        p.close_role("rE")
        scale_pair(2, "s/armx", "s/scp23")
        for r_ in range(8):
            dst = wousA if r_ < 4 else wousB
            reg = "rB" if r_ < 4 else "rC"
            p.dma("sync", lambda e, r_=r_, dst=dst: e.dma_start(
                dst[:, (r_ % 4) * 4:(r_ % 4) * 4 + 4, :],
                agwo_out.ap()[r_].rearrange("(a p) b -> p a b", p=128)),
                r=("s/agwo_out",), w=(f"{reg}/wous/{r_ % 4}",))
        p.dma("sync", lambda e: e.dma_start(boutb_v[:, 0:1024], b_outb.ap()),
              w=("rD/boutb",))
        outs_v = rview(rD, F32, [1024], 1024)
        hidqG2 = rview(rE, BF16, [32, 512])
        hidqP = [(hidqG, "rA"), (hidqG2, "rE")]

        def f2_stages():
            def QD(j):
                g, cb = j // 4, (j % 4) * 8
                for c in range(8):
                    dffc = cb + c
                    hb, hbr = hst8[c]
                    p.dma("sync", lambda e, dffc=dffc, g=g, hb=hb: e.dma_start(
                        hb[:, 0:512],
                        hid_spill.ap()[dffc, :, g * 512:(g + 1) * 512]),
                        r=(f"s/hsp/{dffc}/{g}",), w=hbr)

            def QT(j):
                g, cb = j // 4, (j % 4) * 8
                hq, hqreg = hidqP[g % 2]
                for c in range(8):
                    dffc = cb + c
                    hb, hbr = hst8[c]
                    p.op("vector", lambda e, hb=hb: e.tensor_scalar(
                        q16[:, 0:512], hb[:, 0:512], scp[:, 2:3], None,
                        ALU.mult),
                        r=hbr + ("s/scp23",), w=("s/s1f0",))
                    p.op("vector", lambda e, dffc=dffc, hq=hq: e.tensor_scalar(
                        hq[:, dffc, :], q16[:, 0:512], scp[:, 3:4], None,
                        ALU.mult),
                        r=("s/s1f0", "s/scp23"), w=(f"{hqreg}/hq/{dffc}",))

            def MM(k):
                g, tt_ = divmod(k, 4)
                hq, hqreg = hidqP[g % 2]
                bks = (4, 5) if k % 2 == 0 else (0, 1)
                for nh in range(2):
                    bank = bks[nh]
                    for dffc in range(32):
                        wv = wousA if dffc < 16 else wousB
                        wres = f"{'rB' if dffc < 16 else 'rC'}/wous/{(dffc % 16) // 4}"
                        p.op("tensor", lambda e, dffc=dffc, tt_=tt_, nh=nh,
                             bank=bank, wv=wv, hq=hq: e.matmul(
                                 pb[bank][:],
                                 hq[:, dffc, tt_ * 128:(tt_ + 1) * 128],
                                 wv[:, dffc % 16, nh * 512:(nh + 1) * 512],
                                 start=(dffc == 0), stop=(dffc == 31)),
                             r=(f"{hqreg}/hq/{dffc}", wres), w=(f"pb{bank}",))

            def TAIL(k):
                ti = k
                tisl = slice(ti * 128, (ti + 1) * 128)
                bks = (4, 5) if k % 2 == 0 else (0, 1)
                p.dma("sync", lambda e, tisl=tisl, ti=ti: e.dma_start(
                    xrl[:, 0:1024], x1_spill.ap()[tisl, :]),
                    r=(f"s/x1sp/{ti}",), w=("s/sinf0", "s/sinf1"))
                p.op("vector", lambda e, b0=bks[0]: e.tensor_tensor(
                    pb[2][:, 0:512], pb[b0][:], boutb_v[:, 0:512], ALU.add),
                    r=(f"pb{bks[0]}", "rD/boutb"), w=("pb2",))
                p.op("vector", lambda e, b1=bks[1]: e.tensor_tensor(
                    pb[3][:, 0:512], pb[b1][:], boutb_v[:, 512:1024], ALU.add),
                    r=(f"pb{bks[1]}", "rD/boutb"), w=("pb3",))
                p.op("vector", lambda e: e.bn_stats(bn12[:, 0:6], pb[2][:]),
                     r=("pb2",), w=("s/bn12a",))
                p.op("vector", lambda e: e.bn_stats(bn12[:, 6:12], pb[3][:]),
                     r=("pb3",), w=("s/bn12b",))
                p.op("vector", lambda e: e.bn_aggr(mv[:], bn12[:]),
                     r=("s/bn12a", "s/bn12b"), w=("s/mv",))
                p.op("vector", lambda e: e.tensor_scalar(
                    lt[:, 7:8], mv[:, 1:2], EPS, None, ALU.add),
                    r=("s/mv",), w=("s/lt7",))
                p.op("scalar", lambda e: e.activation(
                    lt[:, 0:1], lt[:, 7:8], AF.Ln),
                    r=("s/lt7",), w=("s/lt0",))
                p.op("scalar", lambda e: e.activation(
                    lt[:, 1:2], lt[:, 0:1], AF.Exp, scale=-0.5),
                    r=("s/lt0",), w=("s/lt1",))
                p.op("vector", lambda e: e.scalar_tensor_tensor(
                    lt[:, 3:4], mv[:, 0:1], -1.0, lt[:, 1:2],
                    ALU.mult, ALU.mult),
                    r=("s/mv", "s/lt1"), w=("s/lt3",))
                p.op("scalar", lambda e: e.activation(
                    ptf[0][:], pb[2][:], AF.Identity,
                    scale=lt[:, 1:2], bias=lt[:, 3:4]),
                    r=("pb2", "s/lt1", "s/lt3"), w=ptfR[0])
                p.op("scalar", lambda e: e.activation(
                    ptf[1][:], pb[3][:], AF.Identity,
                    scale=lt[:, 1:2], bias=lt[:, 3:4]),
                    r=("pb3", "s/lt1", "s/lt3"), w=ptfR[1])
                p.op("vector", lambda e: e.tensor_tensor(
                    outs_v[:, 0:512], ptf[0][:], xrl[:, 0:512], ALU.add),
                    r=ptfR[0] + ("s/sinf0", "s/sinf1"), w=("rD/outs",))
                p.op("vector", lambda e: e.tensor_tensor(
                    outs_v[:, 512:1024], ptf[1][:], xrl[:, 512:1024], ALU.add),
                    r=ptfR[1] + ("s/sinf0", "s/sinf1"), w=("rD/outs",))
                p.dma("sync", lambda e, tisl=tisl, ti=ti: e.dma_start(
                    y_out.ap()[tisl, :], outs_v[:, 0:1024]),
                    r=("rD/outs",), w=(f"s/y/{ti}",))

            def DBG(j):
                if not debug:
                    return
                g, cb = j // 4, (j % 4) * 8
                hq, hqreg = hidqP[g % 2]
                for c in range(8):
                    dffc = cb + c
                    p.dma("gpsimd", lambda e, dffc=dffc, g=g, hq=hq: e.dma_start(
                        dbg["d_hid"].ap()[dffc, :, g * 512:(g + 1) * 512],
                        hq[:, dffc, :]),
                        r=(f"{hqreg}/hq/{dffc}",))

            return [(1, QT), (0, QD), (2, DBG), (5, MM), (6, TAIL)]

        pipeline(NT, f2_stages())
        p.barrier_engine("sync")
        block = st.enter_context(nc.Block())
        p.emit(block, sems)
    return nc


def _host_consts(seg):
    identB = np.eye(128, dtype=ml_dtypes.bfloat16)
    identF = np.eye(128, dtype=np.float32)
    j = np.arange(128)
    maskB = (j[:, None] <= j[None, :]).astype(ml_dtypes.bfloat16)
    onesr = np.ones((1, 128), ml_dtypes.bfloat16)
    onesB = np.ones((128, 128), ml_dtypes.bfloat16)
    onesrf = np.ones((1, 128), np.float32)
    onesc = np.ones((128, 1), ml_dtypes.bfloat16)
    selT = np.kron(np.eye(16, dtype=np.float32),
                   np.ones((1, 128), np.float32)).astype(ml_dtypes.bfloat16)
    return identB, identF, maskB, onesr, onesrf, onesc, onesB, selT


def make_in_maps(inputs, seg=2048):
    x = np.ascontiguousarray(np.asarray(inputs["L0_attn_hidden_states"], np.float32))
    B, T, D = x.shape
    xf = x.reshape(B * T, D)
    wqT = np.ascontiguousarray(np.asarray(inputs["L0_attn_Wq"], np.float32).T)
    wkT = np.ascontiguousarray(np.asarray(inputs["L0_attn_Wk"], np.float32).T)
    wvT = np.ascontiguousarray(np.asarray(inputs["L0_attn_Wv"], np.float32).T)
    wfT = np.ascontiguousarray(np.asarray(inputs["L0_attn_Wf"], np.float32).T)
    woT = np.ascontiguousarray(np.asarray(inputs["L0_attn_Wo"], np.float32).T)
    g_w = np.asarray(inputs["L0_attn_g_w"], np.float32)
    g_wp = np.ascontiguousarray(g_w.reshape(8, 128).T)
    w_in = np.asarray(inputs["L0_ffn_W_in"], np.float32)
    w_out = np.asarray(inputs["L0_ffn_W_out"], np.float32)
    winT = np.ascontiguousarray(w_in.T)
    b_in = np.asarray(inputs["L0_ffn_b_in"], np.float32)
    b_inp = np.ascontiguousarray(b_in.reshape(32, 128).T)
    b_out = np.asarray(inputs["L0_ffn_b_out"], np.float32)
    b_outb = np.ascontiguousarray(np.broadcast_to(b_out, (128, 1024)))
    identB, identF, maskB, onesr, onesrf, onesc, onesB, selT = _host_consts(seg)

    in_maps = []
    for c in range(8):
        b, s = divmod(c, 4)
        tmask = np.array([1.0 if sp < s else 0.0 for sp in range(3)], np.float32)
        dflag = tmask.copy()
        selm = np.zeros((128, 12), np.float32)
        selm[:, 0:3] = tmask
        selm[:, 3:6] = dflag
        selm[:, 6:9] = 1.0 - dflag
        in_maps.append({
            "x_in": np.ascontiguousarray(xf[c * seg:(c + 1) * seg]),
            "wqT": wqT, "wkT": wkT, "wvT": wvT, "wfT": wfT, "woT": woT,
            "g_wp": g_wp,
            "winT_sh": np.ascontiguousarray(winT[c * 512:(c + 1) * 512]),
            "wout_sh": np.ascontiguousarray(w_out[c * 512:(c + 1) * 512]),
            "b_inp": b_inp, "b_outb": b_outb,
            "identB": identB, "identF": identF, "maskB": maskB,
            "onesr": onesr, "onesrf": onesrf, "onesc": onesc, "onesB": onesB,
            "selm": selm, "selT": selT,
        })
    return in_maps


def kernel(**inputs):
    seg = 2048
    nc = build_nc(seg=seg)
    in_maps = make_in_maps(inputs, seg)
    res = run_bass_kernel_spmd(nc, in_maps, core_ids=list(range(8)))
    y = np.concatenate([np.asarray(r["y_out"]) for r in res.results], axis=0)
    return y.reshape(2, 8192, 1024).astype(np.float32)


# version 13 (final)
# speedup vs baseline: 1.0303x; 1.0303x over previous
import numpy as np
import ml_dtypes
import concourse.bass as bass
import concourse.mybir as mybir
from concourse.bass_utils import run_bass_kernel_spmd
from contextlib import ExitStack

F32 = mybir.dt.float32
BF16 = mybir.dt.bfloat16
F16 = mybir.dt.float16
I16 = mybir.dt.int16
I8 = mybir.dt.int8
AF = mybir.ActivationFunctionType
ALU = mybir.AluOpType
AX = mybir.AxisListType

ENGS = ("tensor", "vector", "scalar", "gpsimd", "sync")


def ecopy(e, out, in_):
    if hasattr(e, "tensor_copy"):
        return e.tensor_copy(out, in_)
    return e.copy(out, in_)
DMAQ = {"sync": "dY", "gpsimd": "dG", "scalar": "dS"}
EPS = 1e-5


class Prog:

    import os as _os
    NLANES = int(_os.environ.get("KNLANES", "4"))

    def __init__(self):
        self.items = {e: [] for e in ENGS}
        self.cnt = {e: 0 for e in ENGS}
        self.dma_idx = {"dY": 0, "dG": 0, "dS": 0}
        self.lane_val = {}
        for q in ("dY", "dG", "dS"):
            for ln in range(self.NLANES):
                self.lane_val[f"{q}{ln}"] = 0
        self.cc_cnt = 0
        self.lastw = {}
        self.readers = {}
        self.waited = {e: {} for e in ENGS}
        self.drain_at = {e: 0 for e in ENGS}
        self.epoch = {}

    def _wait(self, eng, key, val):
        if self.waited[eng].get(key, 0) >= val:
            return
        self.waited[eng][key] = val
        self.items[eng].append(("wait", key, val))

    def _need(self, eng, tok):
        kind = tok[0]
        if kind == "e":
            e2, seq = tok[1], tok[2]
            if e2 == eng:
                if eng == "tensor":
                    return
                if self.drain_at[eng] < seq:
                    self.items[eng].append(("drain",))
                    self.drain_at[eng] = self.cnt[eng]
            else:
                self._wait(eng, e2, seq)
        elif kind == "d":
            self._wait(eng, tok[1], tok[2])
        else:
            self._wait(eng, "cc", tok[1])

    def _deps(self, eng, reads, writes):
        for r in reads:
            lw = self.lastw.get(r)
            if lw is not None:
                self._need(eng, lw)
        for w in writes:
            lw = self.lastw.get(w)
            if lw is not None:
                self._need(eng, lw)
            for rd in self.readers.get(w, ()):
                self._need(eng, rd)
            reg = w.split("/", 1)[0]
            for tk in self.epoch.get(reg, ()):
                self._need(eng, tk)

    def _record(self, tok, reads, writes):
        for r in reads:
            self.readers.setdefault(r, []).append(tok)
        for w in writes:
            self.lastw[w] = tok
            self.readers[w] = []

    def op(self, eng, fn, r=(), w=()):
        self._deps(eng, r, w)
        self.cnt[eng] += 1
        tok = ("e", eng, self.cnt[eng])
        self.items[eng].append(("call", fn, eng, 1))
        self._record(tok, r, w)
        return tok

    def dma(self, eng, fn, r=(), w=()):
        q = DMAQ[eng]
        self._deps(eng, r, w)
        k = self.dma_idx[q]
        self.dma_idx[q] += 1
        lanekey = f"{q}{k % self.NLANES}"
        prewait = self.lane_val[lanekey]
        self.lane_val[lanekey] = prewait + 16
        tok = ("d", lanekey, prewait + 16)
        self.items[eng].append(("dma", fn, lanekey, prewait))
        self._record(tok, r, w)
        return tok

    def cc(self, fn, r=(), w=()):
        eng = "gpsimd"
        self._deps(eng, r, w)
        self.cc_cnt += 1
        tok = ("c", self.cc_cnt)
        self.items[eng].append(("call", fn, "cc", 1))
        self._record(tok, r, w)
        return tok

    def close_role(self, region):
        toks = []
        for res, tk in self.lastw.items():
            if res.split("/", 1)[0] == region:
                toks.append(tk)
        for res, lst in self.readers.items():
            if res.split("/", 1)[0] == region:
                toks.extend(lst)
        self.epoch[region] = toks
        for res in [x for x in self.lastw if x.split("/", 1)[0] == region]:
            del self.lastw[res]
        for res in [x for x in self.readers if x.split("/", 1)[0] == region]:
            del self.readers[res]

    def barrier_engine(self, eng, upto=None):
        for lk, v in self.lane_val.items():
            if v:
                self._wait(eng, lk, v)
        for e2 in ENGS:
            if e2 != eng and self.cnt[e2]:
                self._wait(eng, e2, self.cnt[e2])
        if self.cc_cnt:
            self._wait(eng, "cc", self.cc_cnt)

    def emit(self, block, sems):
        import os as _os
        _dense = "denseinc" in _os.environ.get("KABL", "")
        needed = {e: set() for e in ENGS}
        for e in ENGS:
            for it in self.items[e]:
                if it[0] == "wait" and it[1] in self.cnt:
                    needed[it[1]].add(it[2])
        if _dense:
            for e in ENGS:
                needed[e] = set(range(1, self.cnt[e] + 1))
        rank = {}
        for e in ENGS:
            r = 0
            m = {}
            for seq in sorted(needed[e]):
                r += 1
                m[seq] = r
            rank[e] = m

        def run(ename, items):
            seq_ctr = [0]

            def body(eng):
                for it in items:
                    if it[0] == "wait":
                        if it[1] in self.cnt:
                            eng.wait_ge(sems[it[1]], rank[it[1]][it[2]])
                        else:
                            eng.wait_ge(sems[it[1]], it[2])
                    elif it[0] == "drain":
                        eng.drain()
                    elif it[0] == "dma":
                        _, fn, lanekey, prewait = it
                        eng.wait_ge(sems[lanekey], prewait)
                        fn(eng).then_inc(sems[lanekey], 16)
                    else:
                        _, fn, semkey, n = it
                        inst = fn(eng)
                        if semkey in self.cnt:
                            seq_ctr[0] += 1
                            if inst is not None and seq_ctr[0] in needed[semkey]:
                                inst.then_inc(sems[semkey], n)
                        elif inst is not None:
                            inst.then_inc(sems[semkey], n)
            return body

        block.tensor(run("tensor", self.items["tensor"]))
        block.vector(run("vector", self.items["vector"]))
        block.scalar(run("scalar", self.items["scalar"]))
        block.gpsimd(run("gpsimd", self.items["gpsimd"]))
        block.sync(run("sync", self.items["sync"]))


def rview(reg, dt, shape, off=0):
    ap = reg.ap().rearrange("p a b -> p (a b)")
    if dt != F32:
        ap = ap.bitcast(dt)
    tot = ap.shape[1]
    n = 1
    for s in shape:
        n *= s
    assert off + n <= tot, (shape, off, tot)
    ap = ap[:, off:off + n]
    if len(shape) == 1:
        return ap
    if len(shape) == 2:
        return ap.rearrange("p (a b) -> p a b", a=shape[0], b=shape[1])
    if len(shape) == 3:
        return ap.rearrange("p (a b c) -> p a b c", a=shape[0], b=shape[1], c=shape[2])
    raise AssertionError(shape)


def build_nc(seg=2048, upto=99, debug=False):
    import os as _os
    ABL = set(_os.environ.get("KABL", "").split(","))
    """Build the per-core program. seg = tokens per core (multiple of 512)."""
    assert seg % 512 == 0
    NCH = seg // 128
    NG = seg // 512
    H = 4

    nc = bass.Bass("TRN2", target_bir_lowering=False)

    x_in = nc.dram_tensor("x_in", [seg, 1024], F32, kind="ExternalInput")
    wqT = nc.dram_tensor("wqT", [1024, 1024], F32, kind="ExternalInput")
    wkT = nc.dram_tensor("wkT", [1024, 1024], F32, kind="ExternalInput")
    wvT = nc.dram_tensor("wvT", [1024, 1024], F32, kind="ExternalInput")
    wfT = nc.dram_tensor("wfT", [1024, 1024], F32, kind="ExternalInput")
    woT = nc.dram_tensor("woT", [1024, 1024], F32, kind="ExternalInput")
    g_wp = nc.dram_tensor("g_wp", [128, 8], F32, kind="ExternalInput")
    winT_sh = nc.dram_tensor("winT_sh", [512, 1024], F32, kind="ExternalInput")
    wout_sh = nc.dram_tensor("wout_sh", [512, 1024], F32, kind="ExternalInput")
    b_inp = nc.dram_tensor("b_inp", [128, 32], F32, kind="ExternalInput")
    b_outb = nc.dram_tensor("b_outb", [128, 1024], F32, kind="ExternalInput")
    identB_d = nc.dram_tensor("identB", [128, 128], BF16, kind="ExternalInput")
    identF_d = nc.dram_tensor("identF", [128, 128], F32, kind="ExternalInput")
    maskB_d = nc.dram_tensor("maskB", [128, 128], BF16, kind="ExternalInput")
    onesr_d = nc.dram_tensor("onesr", [1, 128], BF16, kind="ExternalInput")
    onesrf_d = nc.dram_tensor("onesrf", [1, 128], F32, kind="ExternalInput")
    onesc_d = nc.dram_tensor("onesc", [128, 1], BF16, kind="ExternalInput")
    onesB_d = nc.dram_tensor("onesB", [128, 128], BF16, kind="ExternalInput")
    selm_d = nc.dram_tensor("selm", [128, 12], F32, kind="ExternalInput")
    selT_d = nc.dram_tensor("selT", [16, 2048], BF16, kind="ExternalInput")

    y_out = nc.dram_tensor("y_out", [seg, 1024], F32, kind="ExternalOutput")

    v_spill = nc.dram_tensor("v_spill", [seg, 1024], BF16)
    x1_spill = nc.dram_tensor("x1_spill", [seg, 1024], F32)
    hid_spill = nc.dram_tensor("hid_spill", [32, 128, seg], BF16)
    ag1_in = nc.dram_tensor("ag1_in", [128, 2056], F32)
    ag1_out = nc.dram_tensor("ag1_out", [4, 128, 2056], F32)
    ag2_in = nc.dram_tensor("ag2_in", [128, 2048], F32)
    ag2_out = nc.dram_tensor("ag2_out", [4, 128, 2048], F32)
    arm1_in = nc.dram_tensor("arm1_in", [128, 1], F32)
    arm1_out = nc.dram_tensor("arm1_out", [128, 1], F32)
    arm2_in = nc.dram_tensor("arm2_in", [128, 1], F32)
    arm2_out = nc.dram_tensor("arm2_out", [128, 1], F32)
    agwi_in = nc.dram_tensor("agwi_in", [1024, 512], BF16)
    agwi_out = nc.dram_tensor("agwi_out", [8, 1024, 512], BF16)
    agwo_in = nc.dram_tensor("agwo_in", [512, 1024], BF16)
    agwo_out = nc.dram_tensor("agwo_out", [8, 512, 1024], BF16)

    dbg = {}
    if debug:
        for nm, shp, dt in [
            ("d_xnF", [128, 8, seg], F32), ("d_qF", [128, 8, seg], F32),
            ("d_kF", [128, 8, seg], F32), ("d_sigF", [128, 8, seg], F32),
            ("d_A", [128, NCH, 1024], F32), ("d_U", [128, NCH, 1024], F32),
            ("d_acF", [128, 8, NCH], F32), ("d_acum", [128, 8, NCH + 1], F32),
            ("d_ok", [128, NCH, 1024], F32), ("d_s1T", [128, 4, 512], F32),
            ("d_qt", [128, 8, seg], F32), ("d_ovF", [128, 8, seg], F32),
            ("d_s2T", [128, 4, 512], F32), ("d_orms", [128, 8, seg], F32),
            ("d_x1", [seg, 1024], F32), ("d_xn2", [seg // 128, 128, 1024], F32),
            ("d_amax", [128, 2], F32), ("d_wins", [128, 8, 4096], F32),
            ("d_wous", [128, 32, 1024], F32), ("d_hid", [32, 128, seg], F32),
            ("d_sin", [128, 4, 512], F32), ("d_sin2", [128, 4, 512], F32),
            ("d_v", [seg, 1024], F32), ("d_x1s", [seg, 1024], F32),
            ("d_hids", [32, 128, seg], F32),
        ]:
            dbg[nm] = nc.dram_tensor(nm, shp, dt, kind="ExternalOutput")

    p = Prog()
    st = ExitStack()
    with st:
        sb = lambda n, s, d=F32: st.enter_context(nc.sbuf_tensor(n, s, d))
        rA = sb("rA", [128, 8, 1024], F32)
        rB = sb("rB", [128, 8, 1024], F32)
        rC = sb("rC", [128, 8, 1024], F32)
        rD = sb("rD", [128, 8, 1024], F32)
        rE = sb("rE", [128, 8, 1024], F32)
        xnF = rview(rA, BF16, [8, seg])
        A_tok = rview(rA, BF16, [NCH, 1024])
        ovF = rview(rA, F16, [8, seg])
        xsimF = rview(rA, BF16, [8, seg])
        hidqG = rview(rA, BF16, [32, 512])
        qF = rview(rB, BF16, [8, seg])
        ormsF = rview(rB, BF16, [8, seg])
        kF = rview(rC, BF16, [8, seg])
        qtF = rview(rC, BF16, [8, seg])
        woTg = rview(rC, BF16, [8, 1024], off=8192)
        sigF = rview(rD, F16, [8, seg])
        okloc = rview(rD, F16, [NCH, 1024])
        xn2a = rview(rD, F32, [8, 1024])
        U_tok = rview(rE, BF16, [NCH, 1024])
        xn2b = rview(rE, F32, [8, 1024])
        boutb_v = rview(rD, F32, [1024], 0)
        wbuf0 = rview(rE, BF16, [2, 8, 1024])
        winsA = rview(rB, BF16, [8, 2048])
        winsB = rview(rC, BF16, [8, 2048])
        wousA = rview(rB, BF16, [16, 1024])
        wousB = rview(rC, BF16, [16, 1024])

        identB = sb("identBs", [128, 128], BF16)
        identF = sb("identFs", [128, 128], F32)
        maskB = sb("maskBs", [128, 128], BF16)
        onesr = sb("onesrs", [1, 128], BF16)
        onesrf = sb("onesrfs", [1, 128], F32)
        onesc = sb("onescs", [128, 1], BF16)
        onesB = sb("onesBs", [128, 128], BF16)
        selT = sb("selTs", [16, 2048], BF16)
        diagt = sb("diagts", [128, 256], BF16)
        selm = sb("selms", [128, 12], F32)
        gwp = sb("gwps", [128, 8], F32)
        binp = sb("binps", [128, 32], F32)
        boutb = None
        zeros = sb("zeross", [128, 128], F32)
        xin = [rview(rB, F32, [1024], i * 1024) for i in range(4)]
        xmb = [rview(rB, BF16, [1024], 8 * 1024 + i * 1024) for i in range(2)]
        bnall = rview(rB, F32, [4, 12], 5 * 1024)
        mvall = rview(rB, F32, [4, 2], 5 * 1024 + 64)
        rrq = rview(rB, F32, [4], 5 * 1024 + 80)
        mrrq = rview(rB, F32, [4], 5 * 1024 + 96)
        bn12 = sb("bn12", [128, 12], F32)
        mv = sb("mv", [128, 2], F32)
        lt = sb("lt", [128, 8], F32)
        wq_f = rview(rC, F32, [4, 1024], 0)
        wq_i8 = rview(rC, I8, [4, 1024], 4 * 4096)
        wq_bf = rview(rD, BF16, [4, 1024], 0)
        wq_st = rview(rD, BF16, [8, 512], 4 * 2048)
        wq_am = sb("wq_am", [128, 4], F32)
        wq_am2 = sb("wq_am2", [128, 4], F32)
        wq_r = sb("wq_r", [128, 4], F32)
        wq_sc = sb("wq_sc", [128, 4], F32)
        wq_is = sb("wq_is", [128, 4], F32)
        sgt = [sb(f"sgt{i}", [128, 512], F32) for i in range(2)]
        aFt = sb("aFt", [128, 512], BF16)
        uFt = sb("uFt", [128, 512], BF16)
        zer16 = sb("zer16", [128, 16], F32)
        acF = sb("acF", [128, 8, NCH], F32)
        acum = sb("acum", [128, 8, NCH + 1], F32)
        acFT16 = sb("acFT16", [16, 1024], BF16)
        acumT16 = sb("acumT16", [16, 1024], BF16)
        ktT = [sb(f"ktT{i}", [128, 256], BF16) for i in range(2)]
        pmk = [sb(f"pmk{i}", [128, 128], BF16) for i in range(2)]
        ufT = [sb(f"ufT{i}", [128, 256], BF16) for i in range(2)]
        vst = [sb(f"vst{i}", [128, 256], BF16) for i in range(2)]
        acb = sb("acb", [128, 512], BF16)
        s1f = [sb(f"s1f{i}", [128, 512], F32) for i in range(2)]
        s1b = sb("s1b", [128, 512], BF16)
        s2f = [sb(f"s2f{i}", [128, 512], F32) for i in range(2)]
        s2b = sb("s2b", [128, 512], BF16)
        sinft = sb("sinft", [128, 2, 512], F32)
        sinf = [sinft[:, i, :] for i in range(2)]
        sinb = sb("sinb", [128, 512], BF16)
        sin2ft = sb("sin2ft", [128, 2, 512], F32)
        sin2f = [sin2ft[:, i, :] for i in range(2)]
        sin2b = sb("sin2b", [128, 512], BF16)
        lnf = sinf[0]
        csb = sinf[1]
        aiv = sin2f[0]
        peerD = sb("peerD", [128, 8], F32)
        sms = [sb(f"sms{i}", [128, 3], F32) for i in range(2)]
        qtt = [sb(f"qtt{i}", [128, 256], BF16) for i in range(2)]
        smx = sgt[0][:, 0:256]
        sme = sgt[0][:, 256:512]
        srow = sgt[1][0:1, 0:512]
        srowB = sgt[0][0:1, 0:512]
        rrow = aFt[0:1, 0:512]
        dsel2 = s1f[0]
        peerTh = s1f[1]
        o2t = sb("o2t", [128, 512], BF16)
        rcol = sb("rcol", [128, 1], F32)
        xrl = sinft.ap().rearrange("p a b -> p (a b)")
        x1s = sin2ft.ap().rearrange("p a b -> p (a b)")
        outs = sinft.ap().rearrange("p a b -> p (a b)")
        amxall = sb("amxall", [128, 32], F32)
        armx = sb("armx", [128, 1], F32)
        armxb = sb("armxb", [128, 1], F32)
        scp = sb("scp", [128, 4], F32)
        sc1 = sb("sc1", [1, 4], F32)
        hst_v = s2f[0].ap().bitcast(BF16)[:, 0:1024]
        hst = [hst_v[:, 0:512], hst_v[:, 512:1024]]
        q16 = s1f[0].ap().bitcast(I16)[:, 0:1024]
        xsimt = s1f[1].ap().bitcast(BF16)[:, 0:1024]

        pb = [st.enter_context(nc.psum_tensor(f"pb{i}", [128, 512], F32)) for i in range(6)]
        pt = [st.enter_context(nc.psum_tensor(f"pt{i}", [128, 1024], BF16)) for i in range(2)]

        sems = {}
        semkeys = ["tensor", "vector", "scalar", "gpsimd", "sync", "cc"]
        for q in ("dY", "dG", "dS"):
            for ln in range(Prog.NLANES):
                semkeys.append(f"{q}{ln}")
        for key in semkeys:
            sems[key] = st.enter_context(nc.semaphore("sem_" + key))

        _clist = [
            ("identB", identB, identB_d), ("identF", identF, identF_d),
            ("maskB", maskB, maskB_d), ("onesr", onesr, onesr_d),
            ("onesrf", onesrf, onesrf_d), ("onesc", onesc, onesc_d),
            ("onesB", onesB, onesB_d), ("selT", selT, selT_d),
            ("selm", selm, selm_d), ("gwp", gwp, g_wp),
            ("binp", binp, b_inp),
        ]
        if "minconst" in ABL:
            _clist = _clist[:1]
        for nm, dst, src in _clist:
            p.dma("sync", (lambda e, d=dst, s=src: e.dma_start(d[:], s.ap())),
                  r=(), w=(f"c/{nm}",))
        p.op("vector", lambda e: e.memset(zeros[:], 0.0), w=("c/zeros",))
        p.op("vector", lambda e: e.memset(zer16[:], 0.0), w=("c/zer16",))
        epsc = sb("epsc", [128, 1], F32)
        p.op("vector", lambda e: e.memset(epsc[:], EPS), w=("c/epsc",))
        nc.const_aps.register(F32, EPS, epsc[:, 0:1])

        def quant_shard(src_dram, do_transpose, agi_dram, tag):
            p.dma("gpsimd", lambda e: e.dma_start(
                wq_f[:], src_dram.ap().rearrange("(a p) b -> p a b", p=128)),
                w=("rC/wqf",))
            if "noquant" in ABL:
                for rt in range(4):
                    p.op("vector", lambda e, rt=rt: e.tensor_copy(
                        wq_bf[:, rt], wq_f[:, rt]),
                        r=("rC/wqf",), w=(f"rD/wqbf/{rt}",))
            if "noquant" not in ABL:
             for rt in range(4):
                p.op("vector", lambda e, rt=rt: e.tensor_reduce(
                    wq_am[:, rt:rt + 1], wq_f[:, rt], AX.X, ALU.max,
                    apply_absolute_value=True),
                    r=("rC/wqf",), w=("s/wqam",))
            if "noquant" not in ABL:
             p.op("vector", lambda e: e.tensor_scalar(
                wq_am2[:], wq_am[:], 1e-8, None, ALU.max),
                r=("s/wqam",), w=("s/wqam2",))
             p.op("vector", lambda e: e.reciprocal(wq_r[:], wq_am2[:]),
                 r=("s/wqam2",), w=("s/wqr",))
             p.op("vector", lambda e: e.tensor_scalar(
                wq_sc[:], wq_r[:], 127.0, None, ALU.mult),
                r=("s/wqr",), w=("s/wqsc",))
             p.op("vector", lambda e: e.tensor_scalar(
                wq_is[:], wq_am2[:], 1.0 / 127.0, None, ALU.mult),
                r=("s/wqam2",), w=("s/wqis",))
             for rt in range(4):
                p.op("vector", lambda e, rt=rt: e.tensor_scalar(
                    wq_i8[:, rt], wq_f[:, rt], wq_sc[:, rt:rt + 1], None, ALU.mult),
                    r=("rC/wqf", "s/wqsc"), w=(f"rC/wqi8/{rt}",))
                p.op("vector", lambda e, rt=rt: e.tensor_scalar(
                    wq_bf[:, rt], wq_i8[:, rt], wq_is[:, rt:rt + 1], None, ALU.mult),
                    r=(f"rC/wqi8/{rt}", "s/wqis"), w=(f"rD/wqbf/{rt}",))
            if do_transpose and "notransp" not in ABL:
                for rt in range(4):
                    bank = rt % 2
                    for dch in range(8):
                        p.op("tensor", lambda e, rt=rt, dch=dch, bank=bank:
                             e.transpose(pt[bank][:, dch * 128:(dch + 1) * 128],
                                         wq_bf[:, rt, dch * 128:(dch + 1) * 128],
                                         identB[:]),
                             r=(f"rD/wqbf/{rt}", "c/identB"),
                             w=(f"pt{bank}/{dch}",))
                    eng = "vector" if "noscal" in ABL else ("scalar" if rt % 2 else "vector")
                    p.op(eng, lambda e, rt=rt, bank=bank: ecopy(
                        e, wq_st[:, :, rt * 128:(rt + 1) * 128],
                        pt[bank].ap().rearrange("p (a b) -> p a b", a=8, b=128)),
                        r=tuple(f"pt{bank}/{d}" for d in range(8)),
                        w=tuple(f"rD/wqst/{d}/{rt}" for d in range(8)))
                p.dma("gpsimd", lambda e: e.dma_start(
                    agi_dram.ap().rearrange("(a p) b -> p a b", p=128),
                    wq_st[:]),
                    r=tuple(f"rD/wqst/{dch}/{rt}" for dch in range(8) for rt in range(4)),
                    w=(f"s/{tag}",))
            else:
                p.dma("gpsimd", lambda e: e.dma_start(
                    agi_dram.ap().rearrange("(a p) b -> p a b", p=128),
                    wq_bf[:]),
                    r=tuple(f"rD/wqbf/{rt}" for rt in range(4)),
                    w=(f"s/{tag}",))

        if upto > -3:
            if "nowin" not in ABL:
                quant_shard(winT_sh, True, agwi_in, "agwi_in")
            if upto > -2:
                p.cc(lambda e: e.collective_compute(
                    "AllGather", ALU.bypass, replica_groups=[list(range(8))],
                    ins=[agwi_in.ap()], outs=[agwi_out.ap()]),
                    r=("s/agwi_in",), w=("s/agwi_out",))
            if "nowout" not in ABL:
                quant_shard(wout_sh, False, agwo_in, "agwo_in")
            if upto > -2:
                p.cc(lambda e: e.collective_compute(
                    "AllGather", ALU.bypass, replica_groups=[list(range(8))],
                    ins=[agwo_in.ap()], outs=[agwo_out.ap()]),
                    r=("s/agwo_in",), w=("s/agwo_out",))
        p.close_role("rC")
        p.close_role("rD")

        if upto <= 0:
            p.barrier_engine("sync")
            block = st.enter_context(nc.Block())
            p.emit(block, sems)
            return nc

        NT = seg // 128
        for gq in range(NT // 4):
            for k4 in range(4):
                ti = gq * 4 + k4
                xres = f"rB/xin{k4}"
                p.dma("sync", lambda e, ti=ti, k4=k4: e.dma_start(
                    xin[k4][:], x_in.ap()[ti * 128:(ti + 1) * 128, :]), w=(xres,))
                p.op("vector", lambda e, k4=k4: e.bn_stats(
                    bnall[:, k4, 0:6], xin[k4][:, 0:512]),
                    r=(xres,), w=(f"rB/bn/{k4}",))
                p.op("vector", lambda e, k4=k4: e.bn_stats(
                    bnall[:, k4, 6:12], xin[k4][:, 512:1024]),
                    r=(xres,), w=(f"rB/bn/{k4}",))
                p.op("vector", lambda e, k4=k4: e.bn_aggr(
                    mvall[:, k4, :], bnall[:, k4, :]),
                    r=(f"rB/bn/{k4}",), w=(f"rB/mv/{k4}",))
            p.op("scalar", lambda e: e.activation(
                rrq[:, 0:4], mvall[:, :, 1], AF.Ln, bias=EPS),
                r=tuple(f"rB/mv/{k}" for k in range(4)) + ("c/epsc",), w=("rB/rr",))
            p.op("scalar", lambda e: e.activation(
                mrrq[:, 0:4], rrq[:, 0:4], AF.Exp, scale=-0.5),
                r=("rB/rr",), w=("rB/rr2",))
            p.op("vector", lambda e: e.scalar_tensor_tensor(
                rrq[:, 0:4], mvall[:, :, 0], -1.0,
                mrrq[:, 0:4], ALU.mult, ALU.mult),
                r=tuple(f"rB/mv/{k}" for k in range(4)) + ("rB/rr2",), w=("rB/mrr",))
            for k4 in range(4):
                ti = gq * 4 + k4
                par = ti % 2
                p.op("scalar", lambda e, k4=k4, par=par: e.activation(
                    xmb[par][:], xin[k4][:], AF.Identity,
                    scale=mrrq[:, k4:k4 + 1], bias=rrq[:, k4:k4 + 1]),
                    r=(f"rB/xin{k4}", "rB/rr2", "rB/mrr"), w=(f"rB/xmb{par}",))
                for dch in range(8):
                    p.op("tensor", lambda e, dch=dch, par=par: e.transpose(
                        pt[par][:, dch * 128:(dch + 1) * 128],
                        xmb[par][:, dch * 128:(dch + 1) * 128], identB[:]),
                        r=(f"rB/xmb{par}", "c/identB"), w=(f"pt{par}/{dch}",))
                eng = "scalar" if par else "vector"
                p.op(eng, lambda e, ti=ti, par=par: ecopy(
                    e, xnF[:, :, ti * 128:(ti + 1) * 128],
                    pt[par].ap().rearrange("p (a b) -> p a b", a=8, b=128)),
                    r=tuple(f"pt{par}/{dch}" for dch in range(8)),
                    w=tuple(f"rA/xnF/{ti}/{dch}" for dch in range(8)))
        if debug and upto >= 1:
            p.dma("gpsimd", lambda e: e.dma_start(dbg["d_xnF"].ap(), xnF[:]),
                  r=tuple(f"rA/xnF/{ti}/{d}" for ti in range(NT) for d in range(8)))
        if upto <= 1:
            p.barrier_engine("sync")
            block = st.enter_context(nc.Block())
            p.emit(block, sems)
            return nc

        p.close_role("rB")
        vstage = hst_v
        vstage2 = sinft.ap().rearrange("p a b -> p (a b)").bitcast(BF16)[:, 0:1024]

        def proj_featmaj(wd, wi, dstF, dreg, act):
            wb = wbuf0[:, wi % 2]
            wres = f"rE/wb{wi % 2}"
            p.dma("gpsimd", lambda e, wb=wb, wd=wd: e.dma_start(
                wb, wd.ap().rearrange("(a p) b -> p a b", p=128)), w=(wres,))
            for g in range(NG):
                gsl = slice(g * 512, (g + 1) * 512)
                xres = tuple(f"rA/xnF/{g * 4 + t}/{kch}"
                             for t in range(4) for kch in range(8))
                for mch in range(8):
                    bank = mch % 4
                    for kch in range(8):
                        p.op("tensor", lambda e, wb=wb, mch=mch, kch=kch, gsl=gsl, bank=bank:
                             e.matmul(pb[bank][:], wb[:, kch, mch * 128:(mch + 1) * 128],
                                      xnF[:, kch, gsl], start=(kch == 0), stop=(kch == 7)),
                             r=(wres,) + xres, w=(f"pb{bank}",))
                    if act == "sig":
                        p.op("scalar", lambda e, mch=mch, gsl=gsl, bank=bank:
                             e.activation(sigF[:, mch, gsl], pb[bank][:], AF.Sigmoid),
                             r=(f"pb{bank}",), w=(f"{dreg}/sigF/{mch}/{g}",))
                    else:
                        p.op("scalar", lambda e, dstF=dstF, mch=mch, gsl=gsl, bank=bank:
                             e.activation(dstF[:, mch, gsl], pb[bank][:], AF.Silu),
                             r=(f"pb{bank}",),
                             w=(f"{dreg}/{'qF' if dreg == 'rB' else 'kF'}/{mch}/{g}",))

        proj_featmaj(wqT, 0, qF, "rB", "silu")
        proj_featmaj(wkT, 1, kF, "rC", "silu")
        wb = wbuf0[:, 0]
        p.dma("gpsimd", lambda e, wb=wb: e.dma_start(
            wb, wvT.ap().rearrange("(a p) b -> p a b", p=128)), w=("rE/wb0",))
        for g in range(NG):
            for tt in range(4):
                ti = g * 4 + tt
                par = ti % 2
                xres = tuple(f"rA/xnF/{ti}/{kch}" for kch in range(8))
                for nh in range(2):
                    bank = par * 2 + nh
                    for kch in range(8):
                        p.op("tensor", lambda e, wb=wb, ti=ti, nh=nh, kch=kch, bank=bank:
                             e.matmul(pb[bank][:], xnF[:, kch, ti * 128:(ti + 1) * 128],
                                      wb[:, kch, nh * 512:(nh + 1) * 512],
                                      start=(kch == 0), stop=(kch == 7)),
                             r=("rE/wb0",) + xres, w=(f"pb{bank}",))
                    vs = vstage if par == 0 else vstage2
                    vr = f"s/s2f0/{nh}" if par == 0 else "s/sinf0"
                    p.op("scalar", lambda e, nh=nh, bank=bank, vs=vs:
                         e.activation(vs[:, nh * 512:(nh + 1) * 512],
                                      pb[bank][:], AF.Silu),
                         r=(f"pb{bank}",), w=(vr,))
                vs = vstage if par == 0 else vstage2
                vres = ("s/s2f0/0", "s/s2f0/1") if par == 0 else ("s/sinf0",)
                p.dma("sync", lambda e, ti=ti, vs=vs: e.dma_start(
                    v_spill.ap()[ti * 128:(ti + 1) * 128, :], vs[:, 0:1024]),
                    r=vres, w=(f"s/vsp/{ti}",))
        proj_featmaj(wfT, 1, None, "rD", "sig")
        if debug and upto >= 2:
            p.dma("gpsimd", lambda e: e.dma_start(dbg["d_qF"].ap(), qF[:]),
                  r=tuple(f"rB/qF/{m}/{g}" for m in range(8) for g in range(NG)))
            p.dma("gpsimd", lambda e: e.dma_start(dbg["d_kF"].ap(), kF[:]),
                  r=tuple(f"rC/kF/{m}/{g}" for m in range(8) for g in range(NG)))
            p.dma("gpsimd", lambda e: e.dma_start(dbg["d_sigF"].ap(), sigF[:]),
                  r=tuple(f"rD/sigF/{m}/{g}" for m in range(8) for g in range(NG)))
            p.dma("gpsimd", lambda e: e.dma_start(dbg["d_v"].ap(), v_spill.ap()),
                  r=tuple(f"s/vsp/{t}" for t in range(NT)))
        if upto <= 2:
            p.barrier_engine("sync")
            block = st.enter_context(nc.Block())
            p.emit(block, sems)
            return nc

        p.close_role("rA")

        def pipeline(niter, stages, pre=None, start=0):
            maxsk = max(sk for sk, _ in stages)
            for slot in range(start, niter + maxsk + 1):
                if pre is not None and slot in pre:
                    for fn in pre[slot]:
                        fn()
                for sk, fn in stages:
                    i = slot - sk
                    if 0 <= i < niter:
                        fn(i)

        lnfP = [(lnf, "s/sinf0"), (s1f[0], "s/s1f0")]
        csbP = [(csb, "s/sinf1"), (s1f[1], "s/s1f1")]
        aivP = [(aiv, "s/sin2f0"), (s2f[1], "s/s2f1")]
        aFtP = [(aFt, "s/aFt"), (s1b, "s/s1b")]
        uFtP = [(uFt, "s/uFt"), (s2b, "s/s2b")]

        def ph2b_stages():
            def ctx(i):
                g, mt = divmod(i, 8)
                return g, mt, slice(g * 512, (g + 1) * 512)

            def Sln(i):
                g, mt, gsl = ctx(i)
                lnfX, lnfR = lnfP[i % 2]
                p.op("scalar", lambda e, mt=mt, gsl=gsl, lnfX=lnfX: e.activation(
                    lnfX[:], sigF[:, mt, gsl], AF.Ln),
                    r=(f"rD/sigF/{mt}/{g}",), w=(lnfR,))

            def Vcs(i):
                lnfX, lnfR = lnfP[i % 2]
                csbX, csbR = csbP[i % 2]
                for cj in range(4):
                    p.op("vector", lambda e, cj=cj, lnfX=lnfX, csbX=csbX:
                         e.tensor_tensor_scan(
                             csbX[:, cj * 128:(cj + 1) * 128],
                             lnfX[:, cj * 128:(cj + 1) * 128],
                             zeros[:], 0.0, ALU.add, ALU.add),
                         r=(lnfR, "c/zeros"), w=(csbR,))

            def Sai(i):
                csbX, csbR = csbP[i % 2]
                aivX, aivR = aivP[i % 2]
                p.op("scalar", lambda e, csbX=csbX, aivX=aivX: e.activation(
                    aivX[:], csbX[:], AF.Exp, scale=-0.125),
                    r=(csbR,), w=(aivR,))

            def Vu(i):
                g, mt, gsl = ctx(i)
                aivX, aivR = aivP[i % 2]
                aFtX, aFtR = aFtP[i % 2]
                uFtX, uFtR = uFtP[i % 2]
                if "norecip" in ABL:
                    csbX, csbR = csbP[i % 2]
                    p.op("scalar", lambda e, csbX=csbX, aFtX=aFtX: e.activation(
                        aFtX[:], csbX[:], AF.Exp, scale=0.125),
                        r=(csbR,), w=(aFtR,))
                    p.op("scalar", lambda e, mt=mt, g=g, csbX=csbX: e.activation(
                        acF[:, mt, g * 4:(g + 1) * 4],
                        csbX[:, 127:512:128], AF.Exp, scale=0.125),
                        r=(csbR,), w=(f"s/acF/{mt}/{g}",))
                else:
                    def _recip_bf(e, aivX=aivX, aFtX=aFtX):
                        with nc.allow_low_precision(reason="aFt consumed as bf16"):
                            return e.reciprocal(aFtX[:], aivX[:])
                    p.op("vector", _recip_bf, r=(aivR,), w=(aFtR,))
                    p.op("vector", lambda e, mt=mt, g=g, aivX=aivX: e.reciprocal(
                        acF[:, mt, g * 4:(g + 1) * 4], aivX[:, 127:512:128]),
                        r=(aivR,), w=(f"s/acF/{mt}/{g}",))
                p.op("vector", lambda e, aivX=aivX, uFtX=uFtX: e.tensor_tensor(
                    uFtX[:, 1:512], aivX[:, 1:512], aivX[:, 0:511], ALU.subtract),
                    r=(aivR,), w=(uFtR,))
                p.op("vector", lambda e, aivX=aivX, uFtX=uFtX: e.tensor_scalar(
                    uFtX[:, 0:512:128], aivX[:, 0:512:128], 1.0, None, ALU.subtract),
                    r=(aivR,), w=(uFtR,))

            pbt = [pb[4].ap().bitcast(BF16), pb[5].ap().bitcast(BF16)]

            def Ttr(i):
                ptA = pt[i % 2]
                ptU = pbt[i % 2]
                aFtX, aFtR = aFtP[i % 2]
                uFtX, uFtR = uFtP[i % 2]
                for cj in range(4):
                    p.op("tensor", lambda e, cj=cj, ptA=ptA, aFtX=aFtX:
                         e.transpose(ptA[:, cj * 128:(cj + 1) * 128],
                                     aFtX[:, cj * 128:(cj + 1) * 128], identB[:]),
                         r=(aFtR, "c/identB"), w=(f"pt{i % 2}/{cj}",))
                    p.op("tensor", lambda e, cj=cj, ptU=ptU, uFtX=uFtX:
                         e.transpose(ptU[:, cj * 128:(cj + 1) * 128],
                                     uFtX[:, cj * 128:(cj + 1) * 128], identB[:]),
                         r=(uFtR, "c/identB"), w=(f"pb{4 + i % 2}",))

            def Vea(i):
                g, mt, gsl = ctx(i)
                ptA = pt[i % 2]
                p.op("vector", lambda e, g=g, mt=mt, ptA=ptA: e.tensor_copy(
                    A_tok[:, g * 4:(g + 1) * 4, mt * 128:(mt + 1) * 128],
                    ptA[:, 0:512].rearrange("p (a b) -> p a b", a=4, b=128)),
                    r=tuple(f"pt{i % 2}/{c}" for c in range(4)),
                    w=tuple(f"rA/A/{g * 4 + c}/{mt}" for c in range(4)))

            def Seu(i):
                g, mt, gsl = ctx(i)
                ptU = pbt[i % 2]
                p.op("scalar", lambda e, g=g, mt=mt, ptU=ptU: e.copy(
                    U_tok[:, g * 4:(g + 1) * 4, mt * 128:(mt + 1) * 128],
                    ptU[:, 0:512].rearrange("p (a b) -> p a b", a=4, b=128)),
                    r=(f"pb{4 + i % 2}",),
                    w=tuple(f"rE/U/{g * 4 + c}/{mt}" for c in range(4)))

            if "noppl" in ABL:
                return [(0, Sln), (0, Vcs), (0, Sai), (0, Vu), (0, Ttr),
                        (0, Vea), (0, Seu)]
            return [(0, Sln), (1, Vcs), (2, Sai), (3, Vu), (4, Ttr),
                    (5, Vea), (5, Seu)]

        pipeline(NG * 8, ph2b_stages())
        p.op("vector", lambda e: e.memset(acum[:, :, 0:1], 1.0), w=("s/acum0",))
        for mt in range(8):
            p.op("vector", lambda e, mt=mt: e.tensor_tensor_scan(
                acum[:, mt, 1:NCH + 1], acF[:, mt, 0:NCH], zer16[:, 0:NCH],
                1.0, ALU.mult, ALU.add),
                r=tuple(f"s/acF/{mt}/{g}" for g in range(NG)) + ("c/zer16",),
                w=(f"s/acum/{mt}",))
        acF_all = tuple(f"s/acF/{mt}/{g}" for mt in range(8) for g in range(NG))
        acum_all = ("s/acum0",) + tuple(f"s/acum/{mt}" for mt in range(8))
        if "notab" not in ABL:
            for half in range(2):
                for mm_ in range(4):
                    mt = half * 4 + mm_
                    p.op("tensor", lambda e, mt=mt, mm_=mm_: e.transpose(
                        pb[5][0:NCH, mm_ * 128:(mm_ + 1) * 128],
                        acF[:, mt, 0:NCH], identF[:]),
                        r=acF_all + ("c/identF",), w=("pb5",))
                p.op("vector", lambda e, half=half: e.tensor_copy(
                    acFT16[0:NCH, half * 512:(half + 1) * 512],
                    pb[5][0:NCH, 0:512]),
                    r=("pb5",), w=("s/acFT16",))
            for half in range(2):
                for mm_ in range(4):
                    mt = half * 4 + mm_
                    p.op("tensor", lambda e, mt=mt, mm_=mm_: e.transpose(
                        pb[5][0:NCH, mm_ * 128:(mm_ + 1) * 128],
                        acum[:, mt, 0:NCH], identF[:]),
                        r=acum_all + ("c/identF",), w=("pb5",))
                p.op("scalar", lambda e, half=half: e.copy(
                    acumT16[0:NCH, half * 512:(half + 1) * 512],
                    pb[5][0:NCH, 0:512]),
                    r=("pb5",), w=("s/acumT16",))
        if debug and upto >= 3:
            p.dma("gpsimd", lambda e: e.dma_start(dbg["d_A"].ap(), A_tok[:]),
                  r=tuple(f"rA/A/{c}/{m}" for c in range(NCH) for m in range(8)))
            p.dma("gpsimd", lambda e: e.dma_start(dbg["d_U"].ap(), U_tok[:]),
                  r=tuple(f"rE/U/{c}/{m}" for c in range(NCH) for m in range(8)))
            p.dma("gpsimd", lambda e: e.dma_start(dbg["d_acF"].ap(), acF[:]),
                  r=tuple(f"s/acF/{m}/{g}" for m in range(8) for g in range(NG)))
            p.dma("gpsimd", lambda e: e.dma_start(dbg["d_acum"].ap(), acum[:]),
                  r=("s/acum0",) + tuple(f"s/acum/{m}" for m in range(8)))
        if upto <= 3:
            p.barrier_engine("sync")
            block = st.enter_context(nc.Block())
            p.emit(block, sems)
            return nc

        p.close_role("rD")
        s1bP = [(s1b, "s/s1b"), (sinb, "s/sinb")]

        def ph3_stages():
            def ctx(i):
                h, ci = divmod(i, NCH)
                return (h, ci, slice(h * 256, (h + 1) * 256),
                        slice(ci * 128, (ci + 1) * 128))

            def Tkt(i):
                h, ci, hsl, cisl = ctx(i)
                ptX = pt[i % 2]
                for cc in range(2):
                    p.op("tensor", lambda e, cc=cc, cisl=cisl, h=h, ptX=ptX:
                         e.transpose(ptX[:, cc * 128:(cc + 1) * 128],
                                     kF[:, 2 * h + cc, cisl], identB[:]),
                         r=(f"rC/kF/{2 * h + cc}/{ci // 4}", "c/identB"),
                         w=(f"pt{i % 2}/{cc}",))

            pbP = [(pb[4], "pb4"), (pb[0], "pb0")]

            def Tp(i):
                h, ci, hsl, cisl = ctx(i)
                bk, bkr = pbP[i % 2]
                for cc in range(2):
                    p.op("tensor", lambda e, cc=cc, cisl=cisl, h=h, bk=bk:
                         e.matmul(bk[:, 0:128], kF[:, 2 * h + cc, cisl],
                                  qF[:, 2 * h + cc, cisl],
                                  start=(cc == 0), stop=(cc == 1),
                                  skip_group_check=True),
                         r=(f"rC/kF/{2 * h + cc}/{ci // 4}",
                            f"rB/qF/{2 * h + cc}/{ci // 4}"),
                         w=(bkr,))

            def Skt(i):
                ptX = pt[i % 2]
                p.op("scalar", lambda e, ptX=ptX, cur=i % 2: e.copy(
                    ktT[cur][:], ptX[:, 0:256]),
                    r=(f"pt{i % 2}/0", f"pt{i % 2}/1"), w=(f"s/ktT{i % 2}",))

            def Vp(i):
                bk, bkr = pbP[i % 2]
                p.op("vector", lambda e, bk=bk, cur=i % 2: e.tensor_tensor(
                    pmk[cur][:], bk[:, 0:128], maskB[:], ALU.mult),
                    r=(bkr, "c/maskB"), w=(f"s/pmk{i % 2}",))

            def Tku(i):
                h, ci, hsl, cisl = ctx(i)
                pbC, pbCr = (pb[2], "pb2") if i % 2 == 0 else (pb[1], "pb1")
                for cc in range(2):
                    p.op("tensor", lambda e, cc=cc, cur=i % 2, ci=ci, hsl=hsl,
                         pbC=pbC: e.matmul(
                             pbC[:, cc * 256:(cc + 1) * 256],
                             ktT[cur][:, cc * 128:(cc + 1) * 128],
                             U_tok[:, ci, hsl], start=True, stop=True),
                         r=(f"s/ktT{i % 2}", f"rE/U/{ci}/{2 * h + cc}"),
                         w=(pbCr,))

            pbB = [(pb[3], "pb3"), (pb[5], "pb5")]

            def Tbc(i):
                h, ci, hsl, cisl = ctx(i)
                bk, bkr = pbB[i % 2]
                p.op("tensor", lambda e, ci=ci, hsl=hsl, bk=bk: e.matmul(
                    bk[:, 0:256], selT[0:NCH, ci * 128:(ci + 1) * 128],
                    acFT16[0:NCH, hsl], start=True, stop=True),
                    r=("c/selT", "s/acFT16"), w=(bkr,))

            def Tok(i):
                h, ci, hsl, cisl = ctx(i)
                bk, bkr = pbP[i % 2]
                one = (ci == 0)
                p.op("tensor", lambda e, cur=i % 2, ci=ci, hsl=hsl, bk=bk,
                     one=one: e.matmul(
                         bk[:, 128:384], pmk[cur][:], U_tok[:, ci, hsl],
                         start=True, stop=one, skip_group_check=True),
                     r=(f"s/pmk{i % 2}", f"rE/U/{ci}/{2 * h}",
                        f"rE/U/{ci}/{2 * h + 1}"),
                     w=(bkr,))
                if not one:
                    sb_, sbr = s1bP[(i - 1) % 2]
                    for cc in range(2):
                        p.op("tensor", lambda e, cc=cc, cisl=cisl, h=h, bk=bk,
                             sb_=sb_: e.matmul(
                                 bk[:, 128:384], qF[:, 2 * h + cc, cisl],
                                 sb_[:, cc * 256:(cc + 1) * 256],
                                 start=False, stop=(cc == 1),
                                 skip_group_check=True),
                             r=(f"rB/qF/{2 * h + cc}/{ci // 4}", sbr), w=(bkr,))

            def Vst(i):
                h, ci, hsl, cisl = ctx(i)
                pbC, pbCr = (pb[2], "pb2") if i % 2 == 0 else (pb[1], "pb1")
                bcb, bcr = pbB[i % 2]
                if ci == 0:
                    p.op("vector", lambda e, pbC=pbC: e.tensor_copy(
                        s2f[1][:], pbC[:]), r=(pbCr,), w=("s/s2f1",))
                else:
                    p.op("vector", lambda e, pbC=pbC, cur=i % 2:
                         e.scalar_tensor_tensor(
                             s2f[1][:], pbC[:], 1.0, s1f[cur][:],
                             ALU.mult, ALU.add),
                         r=(pbCr, f"s/s1f{i % 2}"), w=("s/s2f1",))
                sb_, sbr = s1bP[i % 2]
                for half in range(2):
                    hs2 = slice(half * 256, (half + 1) * 256)
                    p.op("vector", lambda e, hs2=hs2, sb_=sb_, bcb=bcb:
                         e.tensor_tensor(sb_[:, hs2], s2f[1][:, hs2],
                                         bcb[:, 0:256], ALU.mult),
                         r=("s/s2f1", bcr), w=(sbr,))
                nxt = (i + 1) % 2
                for half in range(2):
                    hs2 = slice(half * 256, (half + 1) * 256)
                    p.op("vector", lambda e, hs2=hs2, nxt=nxt, bcb=bcb:
                         e.tensor_tensor(s1f[nxt][:, hs2], s2f[1][:, hs2],
                                         bcb[:, 0:256], ALU.mult),
                         r=("s/s2f1", bcr), w=(f"s/s1f{nxt}",))
                if ci == NCH - 1:
                    p.dma("gpsimd", lambda e, h=h, nxt=nxt: e.dma_start(
                        ag1_in.ap()[:, h * 512:(h + 1) * 512], s1f[nxt][:]),
                        r=(f"s/s1f{nxt}",), w=(f"s/ag1T/{h}",))

            def Vok(i):
                h, ci, hsl, cisl = ctx(i)
                bk, bkr = pbP[i % 2]
                p.op("vector", lambda e, ci=ci, hsl=hsl, bk=bk: e.tensor_tensor(
                    okloc[:, ci, hsl], bk[:, 128:384], A_tok[:, ci, hsl],
                    ALU.mult),
                    r=(bkr, f"rA/A/{ci}/{2 * h}", f"rA/A/{ci}/{2 * h + 1}"),
                    w=(f"rD/ok/{ci}/{h}",))

            return [(0, Tkt), (1, Skt), (1, Tku), (1, Tbc),
                    (2, Vp), (2, Vst), (2, Tok), (2, Vok), (0, Tp)]

        pipeline(4 * NCH, ph3_stages())
        p.dma("gpsimd", lambda e: e.dma_start(
            ag1_in.ap()[:, 2048:2056], acum[:, :, NCH],
            allow_slow_non_contiguous=True),
            r=tuple(f"s/acum/{m}" for m in range(8)), w=("s/ag1D",))
        p.cc(lambda e: e.collective_compute(
            "AllGather", ALU.bypass, replica_groups=[[0, 1, 2, 3], [4, 5, 6, 7]],
            ins=[ag1_in.ap()], outs=[ag1_out.ap()]),
            r=tuple(f"s/ag1T/{h}" for h in range(4)) + ("s/ag1D",),
            w=("s/ag1_out",))
        if debug and upto >= 4:
            p.dma("gpsimd", lambda e: e.dma_start(dbg["d_ok"].ap(), okloc[:]),
                  r=tuple(f"rD/ok/{c}/{h}" for c in range(NCH) for h in range(4)))
            p.dma("gpsimd", lambda e: e.dma_start(
                dbg["d_s1T"].ap(),
                ag1_in.ap()[:, 0:2048].rearrange("p (a b) -> p a b", a=4, b=512)),
                r=tuple(f"s/ag1T/{h}" for h in range(4)))
        if upto <= 4:
            p.barrier_engine("sync")
            block = st.enter_context(nc.Block())
            p.emit(block, sems)
            return nc

        p.close_role("rC")
        smP = [(sgt[0][:, 0:256], sgt[0][:, 256:512], "s/sgt0", "s/sgt0b"),
               (sgt[1][:, 0:256], sgt[1][:, 256:512], "s/sgt1", "s/sgt1"),
               (s2f[0][:, 0:256], s2f[0][:, 256:512], "s/s2f0/0", "s/s2f0/1")]
        sinbP = [(sinb, "s/sinb"), (acb, "s/acb")]

        def c1_comb_dma(h, sp):
            def fn():
                p.dma("sync", lambda e, sp=sp, h=h: e.dma_start(
                    peerTh[:], ag1_out.ap()[sp, :, h * 512:(h + 1) * 512]),
                    r=("s/ag1_out",), w=("s/s1f1",))
                p.dma("sync", lambda e, sp=sp: e.dma_start(
                    peerD[:], ag1_out.ap()[sp, :, 2048:2056]),
                    r=("s/ag1_out",), w=("s/peerD",))
            return fn

        def c1_comb_step(h, sp):
            F, Fr = sinf[h % 2], f"s/sinf{h % 2}"

            def fn():
                if sp == 0:
                    p.op("vector", lambda e, F=F: e.memset(F[:], 0.0), w=(Fr,))
                p.op("vector", lambda e, h=h: e.tensor_scalar(
                    diagt[:, 0:128], identB[:], peerD[:, 2 * h:2 * h + 1],
                    None, ALU.mult),
                    r=("c/identB", "s/peerD"), w=("s/diagt",))
                p.op("vector", lambda e, h=h: e.tensor_scalar(
                    diagt[:, 128:256], identB[:], peerD[:, 2 * h + 1:2 * h + 2],
                    None, ALU.mult),
                    r=("c/identB", "s/peerD"), w=("s/diagt",))
                p.op("tensor", lambda e: e.matmul(
                    pb[2][:, 0:256], onesB[:], diagt[:, 0:256],
                    start=True, stop=True),
                    r=("c/onesB", "s/diagt"), w=("pb2",))
                for half in range(2):
                    p.op("scalar", lambda e, sp=sp, half=half: e.activation(
                        dsel2[:, half * 256:(half + 1) * 256], pb[2][:, 0:256],
                        AF.Identity, scale=selm[:, 3 + sp:4 + sp],
                        bias=selm[:, 6 + sp:7 + sp]),
                        r=("pb2", "c/selm"), w=("s/s1f0",))
                p.op("vector", lambda e, F=F: e.tensor_tensor(
                    s2f[1][:], F[:], dsel2[:], ALU.mult),
                    r=(Fr, "s/s1f0"), w=("s/s2f1",))
                p.op("vector", lambda e, sp=sp, F=F: e.scalar_tensor_tensor(
                    F[:], peerTh[:], selm[:, sp:sp + 1], s2f[1][:],
                    ALU.mult, ALU.add),
                    r=("s/s1f1", "c/selm", "s/s2f1"), w=(Fr,))
            return fn

        spg = 3 if NCH >= 10 else 1
        c1_pre = {}
        for h in range(4):
            base = h * NCH
            for sp in range(3):
                stepslot = base - 1 - (2 - sp) * spg
                c1_pre.setdefault(stepslot - 1, []).append(c1_comb_dma(h, sp))
                c1_pre.setdefault(stepslot, []).append(c1_comb_step(h, sp))

        def c1_stages():
            def ctx(i):
                h, ci = divmod(i, NCH)
                return (h, ci, slice(h * 256, (h + 1) * 256),
                        slice(ci * 128, (ci + 1) * 128))

            bcP = [(pb[3], "pb3"), (pb[0], "pb0")]
            corP1 = [(pb[5], "pb5"), (pb[4], "pb4")]

            def Tbc1(i):
                h, ci, hsl, cisl = ctx(i)
                bk, bkr = bcP[i % 2]
                p.op("tensor", lambda e, ci=ci, hsl=hsl, bk=bk: e.matmul(
                    bk[:, 0:256], selT[0:NCH, ci * 128:(ci + 1) * 128],
                    acumT16[0:NCH, hsl], start=True, stop=True),
                    r=("c/selT", "s/acumT16"), w=(bkr,))

            def Vsb(i):
                h, ci, hsl, cisl = ctx(i)
                bk, bkr = bcP[i % 2]
                sbX, sbr = sinbP[i % 2]
                Fr = f"s/sinf{h % 2}"
                for half in range(2):
                    hs2 = slice(half * 256, (half + 1) * 256)
                    p.op("vector", lambda e, hs2=hs2, h=h, bk=bk, sbX=sbX:
                         e.tensor_tensor(sbX[:, hs2], sinf[h % 2][:, hs2],
                                         bk[:, 0:256], ALU.mult),
                         r=(Fr, bkr), w=(sbr,))
                if debug and ci == 0:
                    p.dma("gpsimd", lambda e, h=h, sbX=sbX: e.dma_start(
                        dbg["d_sin"].ap()[:, h, :], sbX[:]), r=(sbr,))

            def Tcor(i):
                h, ci, hsl, cisl = ctx(i)
                bk, bkr = corP1[i % 2]
                sbX, sbr = sinbP[i % 2]
                for cc in range(2):
                    p.op("tensor", lambda e, cc=cc, cisl=cisl, h=h, bk=bk,
                         sbX=sbX: e.matmul(
                             bk[:, 0:256], qF[:, 2 * h + cc, cisl],
                             sbX[:, cc * 256:(cc + 1) * 256],
                             start=(cc == 0), stop=(cc == 1)),
                         r=(f"rB/qF/{2 * h + cc}/{ci // 4}", sbr),
                         w=(bkr,))

            def Vsm(i):
                h, ci, hsl, cisl = ctx(i)
                smxX, smeX, smxR, smeR = smP[i % 3]
                ares = (f"rA/A/{ci}/{2 * h}", f"rA/A/{ci}/{2 * h + 1}")
                p.op("vector", lambda e, ci=ci, hsl=hsl, bk=corP1[i % 2][0],
                     smxX=smxX: e.tensor_tensor(smxX[:], bk[:, 0:256],
                                                A_tok[:, ci, hsl], ALU.mult),
                     r=(corP1[i % 2][1],) + ares, w=(smxR,))
                p.op("vector", lambda e, ci=ci, hsl=hsl, smxX=smxX, smeX=smeX:
                     e.tensor_tensor(smeX[:], smxX[:], okloc[:, ci, hsl],
                                     ALU.add),
                     r=(smxR, f"rD/ok/{ci}/{h}"), w=(smeR,))
                p.op("vector", lambda e, smeX=smeX, smsX=sms[i % 2]:
                     e.reduce_max(smsX[:, 0:1], smeX[:], AX.X, negate=True),
                     r=(smeR,), w=(f"s/sms{i % 2}0",))

            def Sex(i):
                smxX, smeX, smxR, smeR = smP[i % 3]
                p.op("scalar", lambda e, smxX=smxX, smeX=smeX, smsX=sms[i % 2]:
                     e.activation(smxX[:], smeX[:], AF.Exp, bias=smsX[:, 0:1],
                                  accum_out=smsX[:, 1:2]),
                     r=(smeR, f"s/sms{i % 2}0"), w=(smxR, f"s/sms{i % 2}1"))

            def Vqt(i):
                h, ci, hsl, cisl = ctx(i)
                smxX, smeX, smxR, smeR = smP[i % 3]
                ares = (f"rA/A/{ci}/{2 * h}", f"rA/A/{ci}/{2 * h + 1}")
                p.op("vector", lambda e, smsX=sms[i % 2]: e.reciprocal(
                    smsX[:, 2:3], smsX[:, 1:2]),
                    r=(f"s/sms{i % 2}1",), w=(f"s/sms{i % 2}2",))
                p.op("vector", lambda e, ci=ci, hsl=hsl, smxX=smxX, smeX=smeX:
                     e.tensor_tensor(smeX[:], smxX[:], A_tok[:, ci, hsl],
                                     ALU.mult),
                     r=(smxR,) + ares, w=(smeR,))
                p.op("vector", lambda e, smeX=smeX, smsX=sms[i % 2],
                     qttX=qtt[i % 2]: e.tensor_scalar(
                         qttX[:], smeX[:], smsX[:, 2:3], None, ALU.mult),
                     r=(smeR, f"s/sms{i % 2}2"), w=(f"s/qtt{i % 2}",))

            def Ttr(i):
                ptX = pt[i % 2]
                for cc in range(2):
                    p.op("tensor", lambda e, cc=cc, ptX=ptX, qttX=qtt[i % 2]:
                         e.transpose(ptX[:, cc * 128:(cc + 1) * 128],
                                     qttX[:, cc * 128:(cc + 1) * 128], identB[:]),
                         r=(f"s/qtt{i % 2}", "c/identB"), w=(f"pt{i % 2}/{cc}",))

            def Sev(i):
                h, ci, hsl, cisl = ctx(i)
                ptX = pt[i % 2]
                p.op("scalar", lambda e, h=h, cisl=cisl, ptX=ptX: e.copy(
                    qtF[:, 2 * h:2 * h + 2, cisl],
                    ptX[:, 0:256].rearrange("p (a b) -> p a b", a=2, b=128)),
                    r=(f"pt{i % 2}/0", f"pt{i % 2}/1"),
                    w=(f"rC/qt/{ci}/{2 * h}", f"rC/qt/{ci}/{2 * h + 1}"))

            return [(0, Tbc1), (1, Vsb), (1, Tcor), (2, Vsm), (3, Sex),
                    (4, Vqt), (5, Ttr), (6, Sev)]

        pipeline(4 * NCH, c1_stages(), pre=c1_pre, start=-9)
        if debug and upto >= 5:
            p.dma("gpsimd", lambda e: e.dma_start(dbg["d_qt"].ap(), qtF[:]),
                  r=tuple(f"rC/qt/{c}/{m}" for c in range(NCH) for m in range(8)))
        if upto <= 5:
            p.barrier_engine("sync")
            block = st.enter_context(nc.Block())
            p.emit(block, sems)
            return nc

        p.close_role("rA")
        vstP = [(vst[0], "s/vst0"), (vst[1], "s/vst1"),
                (ktT[0], "s/ktT0"), (ktT[1], "s/ktT1"),
                (qtt[0], "s/qtt0"), (qtt[1], "s/qtt1")]
        s2bP = [(s2b, "s/s2b"), (sin2b, "s/sin2b")]

        def ph4_stages():
            def ctx(i):
                h, ci = divmod(i, NCH)
                return (h, ci, slice(h * 256, (h + 1) * 256),
                        slice(ci * 128, (ci + 1) * 128))

            def Dv(i):
                h, ci, hsl, cisl = ctx(i)
                vb, vbr = vstP[i % 6]
                p.dma("sync", lambda e, ci=ci, h=h, vb=vb: e.dma_start(
                    vb[:], v_spill.ap()[ci * 128:(ci + 1) * 128,
                                        h * 256:(h + 1) * 256]),
                    r=(f"s/vsp/{ci}",), w=(vbr,))

            pbP2 = [(pb[4], "pb4"), (pb[5], "pb5")]

            def Tuf(i):
                h, ci, hsl, cisl = ctx(i)
                ptX = pt[i % 2]
                for mc in range(2):
                    p.op("tensor", lambda e, mc=mc, ci=ci, h=h, ptX=ptX:
                         e.transpose(
                             ptX[:, mc * 128:(mc + 1) * 128],
                             U_tok[:, ci, h * 256 + mc * 128:h * 256 + (mc + 1) * 128],
                             identB[:]),
                         r=(f"rE/U/{ci}/{2 * h + mc}", "c/identB"),
                         w=(f"pt{i % 2}/{mc}",))

            def Vuf(i):
                ptX = pt[i % 2]
                p.op("vector", lambda e, ptX=ptX, cur=i % 2: e.tensor_copy(
                    ufT[cur][:], ptX[:, 0:256]),
                    r=(f"pt{i % 2}/0", f"pt{i % 2}/1"), w=(f"s/ufT{i % 2}",))

            def Tp2(i):
                h, ci, hsl, cisl = ctx(i)
                bk, bkr = pbP2[i % 2]
                for mc in range(2):
                    p.op("tensor", lambda e, mc=mc, cur=i % 2, ci=ci, h=h,
                         cisl=cisl, bk=bk: e.matmul(
                             bk[:, 0:128],
                             ufT[cur][:, mc * 128:(mc + 1) * 128],
                             qtF[:, 2 * h + mc, cisl],
                             start=(mc == 0), stop=(mc == 1),
                             skip_group_check=True),
                         r=(f"s/ufT{i % 2}", f"rC/qt/{ci}/{2 * h + mc}"),
                         w=(bkr,))

            def Vp2(i):
                bk, bkr = pbP2[i % 2]
                p.op("vector", lambda e, bk=bk, cur=i % 2: e.tensor_tensor(
                    pmk[cur][:], bk[:, 0:128], maskB[:], ALU.mult),
                    r=(bkr, "c/maskB"), w=(f"s/pmk{i % 2}",))

            def Tku2(i):
                h, ci, hsl, cisl = ctx(i)
                vb, vbr = vstP[i % 6]
                pbC, pbCr = (pb[2], "pb2") if i % 2 == 0 else (pb[1], "pb1")
                for mc in range(2):
                    p.op("tensor", lambda e, mc=mc, ci=ci, h=h, vb=vb, pbC=pbC:
                         e.matmul(
                             pbC[:, mc * 256:(mc + 1) * 256],
                             U_tok[:, ci, h * 256 + mc * 128:h * 256 + (mc + 1) * 128],
                             vb[:], start=True, stop=True),
                         r=(f"rE/U/{ci}/{2 * h + mc}", vbr), w=(pbCr,))

            def Tov(i):
                h, ci, hsl, cisl = ctx(i)
                bk, bkr = pbP2[i % 2]
                vb, vbr = vstP[i % 6]
                one = (ci == 0)
                for vv in range(2):
                    vsl = slice(128 + vv * 128, 128 + (vv + 1) * 128)
                    p.op("tensor", lambda e, vv=vv, cur=i % 2, vsl=vsl, vb=vb,
                         bk=bk, one=one: e.matmul(
                             bk[:, vsl], vb[:, vv * 128:(vv + 1) * 128],
                             pmk[cur][:], start=True, stop=one,
                             skip_group_check=True),
                         r=(vbr, f"s/pmk{i % 2}"), w=(bkr,))
                    if not one:
                        sb_, sbr = s2bP[(i - 1) % 2]
                        for mc in range(2):
                            p.op("tensor", lambda e, vv=vv, mc=mc, h=h,
                                 cisl=cisl, vsl=vsl, bk=bk, sb_=sb_: e.matmul(
                                     bk[:, vsl],
                                     sb_[:, mc * 256 + vv * 128:mc * 256 + (vv + 1) * 128],
                                     qtF[:, 2 * h + mc, cisl],
                                     start=False, stop=(mc == 1),
                                     skip_group_check=True),
                                 r=(sbr, f"rC/qt/{ci}/{2 * h + mc}"), w=(bkr,))

            def Sov(i):
                h, ci, hsl, cisl = ctx(i)
                bk, bkr = pbP2[i % 2]
                p.op("scalar", lambda e, h=h, cisl=cisl, bk=bk: e.copy(
                    ovF[:, 2 * h:2 * h + 2, cisl],
                    bk[:, 128:384].rearrange("p (a b) -> p a b", a=2, b=128)),
                    r=(bkr,),
                    w=(f"rA/ov/{ci}/{2 * h}", f"rA/ov/{ci}/{2 * h + 1}"))

            def Vst2(i):
                h, ci, hsl, cisl = ctx(i)
                pbC, pbCr = (pb[2], "pb2") if i % 2 == 0 else (pb[1], "pb1")
                cur, nxt = i % 2, (i + 1) % 2
                curres = ("s/s2f0/0", "s/s2f0/1") if cur == 0 else ("s/s2f1",)
                nxtres = ("s/s2f0/0", "s/s2f0/1") if nxt == 0 else ("s/s2f1",)
                if ci == 0:
                    p.op("vector", lambda e, pbC=pbC: e.tensor_copy(
                        s1f[0][:], pbC[:]), r=(pbCr,), w=("s/s1f0",))
                else:
                    p.op("vector", lambda e, pbC=pbC, cur=cur:
                         e.scalar_tensor_tensor(
                             s1f[0][:], pbC[:], 1.0, s2f[cur][:],
                             ALU.mult, ALU.add),
                         r=(pbCr,) + curres, w=("s/s1f0",))
                sb_, sbr = s2bP[i % 2]
                for mc in range(2):
                    msl = slice(mc * 256, (mc + 1) * 256)
                    p.op("vector", lambda e, mc=mc, msl=msl, ci=ci, h=h,
                         sb_=sb_: e.tensor_scalar(
                             sb_[:, msl], s1f[0][:, msl],
                             acF[:, 2 * h + mc, ci:ci + 1], None, ALU.mult),
                         r=("s/s1f0", f"s/acF/{2 * h + mc}/{ci // 4}"),
                         w=(sbr,))
                for mc in range(2):
                    msl = slice(mc * 256, (mc + 1) * 256)
                    p.op("vector", lambda e, mc=mc, msl=msl, ci=ci, h=h,
                         nxt=nxt: e.tensor_scalar(
                             s2f[nxt][:, msl], s1f[0][:, msl],
                             acF[:, 2 * h + mc, ci:ci + 1], None, ALU.mult),
                         r=("s/s1f0", f"s/acF/{2 * h + mc}/{ci // 4}"),
                         w=nxtres)
                if ci == NCH - 1:
                    finres = ("s/s2f0/0", "s/s2f0/1") if nxt == 0 else ("s/s2f1",)
                    p.dma("gpsimd", lambda e, h=h, nxt=nxt: e.dma_start(
                        ag2_in.ap()[:, h * 512:(h + 1) * 512], s2f[nxt][:]),
                        r=finres, w=(f"s/ag2T/{h}",))

            return [(0, Dv), (0, Tuf), (0, Vuf), (2, Tku2),
                    (2, Vp2), (2, Vst2), (2, Tov), (2, Sov), (0, Tp2)]

        pipeline(4 * NCH, ph4_stages())
        p.cc(lambda e: e.collective_compute(
            "AllGather", ALU.bypass, replica_groups=[[0, 1, 2, 3], [4, 5, 6, 7]],
            ins=[ag2_in.ap()], outs=[ag2_out.ap()]),
            r=tuple(f"s/ag2T/{h}" for h in range(4)), w=("s/ag2_out",))
        if debug and upto >= 6:
            p.dma("gpsimd", lambda e: e.dma_start(dbg["d_ovF"].ap(), ovF[:]),
                  r=tuple(f"rA/ov/{c}/{m}" for c in range(NCH) for m in range(8)))
            p.dma("gpsimd", lambda e: e.dma_start(
                dbg["d_s2T"].ap(),
                ag2_in.ap()[:, 0:2048].rearrange("p (a b) -> p a b", a=4, b=512)),
                r=tuple(f"s/ag2T/{h}" for h in range(4)))
        if upto <= 6:
            p.barrier_engine("sync")
            block = st.enter_context(nc.Block())
            p.emit(block, sems)
            return nc

        p.close_role("rB")
        ssqbank = [0, 1, 2, 3]
        ssqcnt = {}

        def c2_comb_dma(h, sp):
            def fn():
                p.dma("sync", lambda e, sp=sp, h=h: e.dma_start(
                    peerTh[:], ag2_out.ap()[sp, :, h * 512:(h + 1) * 512]),
                    r=("s/ag2_out",), w=("s/s1f1",))
                p.dma("sync", lambda e, sp=sp: e.dma_start(
                    peerD[:], ag1_out.ap()[sp, :, 2048:2056]),
                    r=("s/ag1_out",), w=("s/peerD",))
            return fn

        def c2_comb_step(h, sp):
            F, Fr = sin2f[h % 2], f"s/sin2f{h % 2}"

            def fn():
                if sp == 0:
                    p.op("vector", lambda e, F=F: e.memset(F[:], 0.0), w=(Fr,))
                for mc in range(2):
                    msl = slice(mc * 256, (mc + 1) * 256)
                    p.op("vector", lambda e, sp=sp, h=h, mc=mc: e.tensor_scalar(
                        lt[:, 4 + mc:5 + mc], peerD[:, 2 * h + mc:2 * h + mc + 1],
                        selm[:, 3 + sp:4 + sp], selm[:, 6 + sp:7 + sp],
                        ALU.mult, ALU.add),
                        r=("s/peerD", "c/selm"), w=(f"s/lt{4 + mc}",))
                    p.op("vector", lambda e, mc=mc, msl=msl, F=F: e.tensor_scalar(
                        s1f[0][:, msl], F[:, msl], lt[:, 4 + mc:5 + mc],
                        None, ALU.mult),
                        r=(Fr, f"s/lt{4 + mc}"), w=("s/s1f0",))
                p.op("vector", lambda e, sp=sp, F=F: e.scalar_tensor_tensor(
                    F[:], peerTh[:], selm[:, sp:sp + 1], s1f[0][:],
                    ALU.mult, ALU.add),
                    r=("s/s1f1", "c/selm", "s/s1f0"), w=(Fr,))
                if debug and sp == 2:
                    p.dma("gpsimd", lambda e, h=h, F=F: e.dma_start(
                        dbg["d_sin2"].ap()[:, h, :], F[:]), r=(Fr,))
            return fn

        c2_pre = {}
        for h in range(4):
            base = h * NCH
            for sp in range(3):
                stepslot = base - 1 - (2 - sp) * spg
                c2_pre.setdefault(stepslot - 1, []).append(c2_comb_dma(h, sp))
                c2_pre.setdefault(stepslot, []).append(c2_comb_step(h, sp))

        sgP = [(sgt[0][:, 0:256], "s/sgt0"), (sgt[1][:, 0:256], "s/sgt1")]
        o2P = [(o2t[:, 0:256], "s/o2t"), (o2t[:, 256:512], "s/o2tb")]
        corP = [(pb[4], ("pb4",)), (pb[5], ("pb5",))]

        def c2p1_stages():
            def ctx(i):
                h, ci = divmod(i, NCH)
                return (h, ci, slice(h * 256, (h + 1) * 256),
                        slice(ci * 128, (ci + 1) * 128))

            def Vs2(i):
                h, ci, hsl, cisl = ctx(i)
                s2bX, s2bR = s2bP[i % 2]
                for mc in range(2):
                    msl = slice(mc * 256, (mc + 1) * 256)
                    acres = "s/acum0" if ci == 0 else f"s/acum/{2 * h + mc}"
                    p.op("vector", lambda e, mc=mc, msl=msl, ci=ci, h=h,
                         s2bX=s2bX, F=sin2f[h % 2]: e.tensor_scalar(
                             s2bX[:, msl], F[:, msl],
                             acum[:, 2 * h + mc, ci:ci + 1], None, ALU.mult),
                         r=(f"s/sin2f{h % 2}", acres), w=(s2bR,))

            def Tcor2(i):
                h, ci, hsl, cisl = ctx(i)
                s2bX, s2bR = s2bP[i % 2]
                pbA, pbAw = corP[i % 2]
                for vv in range(2):
                    vsl = slice(vv * 128, (vv + 1) * 128)
                    for mc in range(2):
                        p.op("tensor", lambda e, vv=vv, mc=mc, h=h, cisl=cisl,
                             vsl=vsl, pbA=pbA, s2bX=s2bX: e.matmul(
                                 pbA[:, vsl],
                                 s2bX[:, mc * 256 + vv * 128:mc * 256 + (vv + 1) * 128],
                                 qtF[:, 2 * h + mc, cisl],
                                 start=(mc == 0), stop=(mc == 1),
                                 skip_group_check=True),
                             r=(s2bR, f"rC/qt/{ci}/{2 * h + mc}"), w=pbAw)

            def Vov(i):
                h, ci, hsl, cisl = ctx(i)
                sgX, sgR = sgP[i % 2]
                pbA, pbAw = corP[i % 2]
                p.op("vector", lambda e, h=h, cisl=cisl, pbA=pbA, sgX=sgX:
                     e.tensor_tensor(
                         sgX.rearrange("p (a b) -> p a b", a=2, b=128),
                         pbA[:, 0:256].rearrange("p (a b) -> p a b", a=2, b=128),
                         ovF[:, 2 * h:2 * h + 2, cisl], ALU.add),
                     r=pbAw + (f"rA/ov/{ci}/{2 * h}", f"rA/ov/{ci}/{2 * h + 1}"),
                     w=(sgR,))

            def Ssi(i):
                h, ci, hsl, cisl = ctx(i)
                sgX, sgR = sgP[i % 2]
                p.op("scalar", lambda e, h=h, cisl=cisl, sgX=sgX: e.activation(
                    ormsF[:, 2 * h:2 * h + 2, cisl],
                    sgX.rearrange("p (a b) -> p a b", a=2, b=128), AF.Silu),
                    r=(sgR,),
                    w=(f"rB/orms/{ci}/{2 * h}", f"rB/orms/{ci}/{2 * h + 1}"))

            def Vsq(i):
                h, ci, hsl, cisl = ctx(i)
                o2X, o2R = o2P[i % 2]
                p.op("vector", lambda e, h=h, cisl=cisl, o2X=o2X: e.tensor_tensor(
                    o2X.rearrange("p (a b) -> p a b", a=2, b=128),
                    ormsF[:, 2 * h:2 * h + 2, cisl],
                    ormsF[:, 2 * h:2 * h + 2, cisl], ALU.mult),
                    r=(f"rB/orms/{ci}/{2 * h}", f"rB/orms/{ci}/{2 * h + 1}"),
                    w=(o2R,))

            def Tssq(i):
                h, ci, hsl, cisl = ctx(i)
                g = ci // 4
                o2X, o2R = o2P[i % 2]
                for vv in range(2):
                    nfirst = ssqcnt.get(g, 0)
                    ssqcnt[g] = nfirst + 1
                    p.op("tensor", lambda e, g=g, ci=ci, vv=vv, nfirst=nfirst,
                         o2X=o2X: e.matmul(
                             pb[ssqbank[g]][0:1, (ci % 4) * 128:(ci % 4 + 1) * 128],
                             onesc[:], o2X[:, vv * 128:(vv + 1) * 128],
                             start=(nfirst == 0), stop=(nfirst == 31),
                             skip_group_check=True),
                         r=(o2R, "c/onesc"), w=(f"pb{ssqbank[g]}",))

            return [(0, Vs2), (1, Tcor2), (2, Vov), (3, Ssi), (4, Vsq),
                    (5, Tssq)]

        pipeline(4 * NCH, c2p1_stages(), pre=c2_pre, start=-9)
        if debug and upto >= 7:
            p.dma("gpsimd", lambda e: e.dma_start(dbg["d_orms"].ap(), ormsF[:]),
                  r=tuple(f"rB/orms/{c}/{m}" for c in range(NCH) for m in range(8)))
        if upto <= 7:
            p.barrier_engine("sync")
            block = st.enter_context(nc.Block())
            p.emit(block, sems)
            return nc

        p.close_role("rC")
        p.close_role("rD")
        wost = rview(rC, F32, [4, 1024], 0)
        for half in range(2):
            p.dma("sync", lambda e, half=half: e.dma_start(
                wost[:], woT.ap().rearrange("(a p) b -> p a b", p=128)[
                    :, half * 4:(half + 1) * 4, :]),
                r=(), w=("rC/wost",))
            for dq in range(4):
                dch = half * 4 + dq
                p.op("scalar", lambda e, dch=dch, dq=dq: e.activation(
                    woTg[:, dch, :], wost[:, dq, :], AF.Identity,
                    scale=gwp[:, dch:dch + 1]),
                    r=("rC/wost", "c/gwp"), w=(f"rC/woTg/{dch}",))
        ptf = [pt[0].ap().bitcast(F32), pt[1].ap().bitcast(F32)]
        ptfR = [tuple(f"pt0/{c}" for c in range(8)),
                tuple(f"pt1/{c}" for c in range(8))]
        rc16 = sb("rc16", [128, 16], F32)
        for g in range(NG):
            gb = ssqbank[g]
            p.op("vector", lambda e, gb=gb: e.tensor_scalar(
                srow, pb[gb][0:1, 0:512], 1.0 / 1024.0, EPS, ALU.mult, ALU.add),
                r=(f"pb{gb}",), w=("s/sgt1",))
            p.op("scalar", lambda e: e.activation(srowB, srow, AF.Ln),
                 r=("s/sgt1",), w=("s/sgt0",))
            p.op("scalar", lambda e: e.activation(rrow, srowB, AF.Exp, scale=-0.5),
                 r=("s/sgt0",), w=("s/aFt",))
            for tt in range(4):
                p.op("tensor", lambda e, tt=tt, gb=gb: e.matmul(
                    pb[gb][:, tt:tt + 1], rrow[0:1, tt * 128:(tt + 1) * 128],
                    onesr[0:1, 0:1], start=True, stop=True,
                    skip_group_check=True),
                    r=("s/aFt", "c/onesr"), w=(f"pb{gb}",))
            p.op("vector", lambda e, g=g, gb=gb: e.tensor_copy(
                rc16[:, g * 4:(g + 1) * 4], pb[gb][:, 0:4]),
                r=(f"pb{gb}",), w=(f"s/rc16/{g}",))

        ltG = sb("ltG", [128, 16, 8], F32)
        mvG = sb("mvG", [128, 16, 2], F32)
        bnG = sb("bnG", [128, 16, 12], F32)
        xrlP2 = [((xrl[:, 0:512], xrl[:, 512:1024]), ("s/sinf0", "s/sinf1")),
                 ((s1f[0][:], s1f[1][:]), ("s/s1f0", "s/s1f1"))]
        x1sP2 = [((x1s[:, 0:512], x1s[:, 512:1024]), ("s/sin2f0", "s/sin2f1")),
                 ((s2f[0][:], s2f[1][:]), ("s/s2f0/0", "s/s2f0/1", "s/s2f1"))]
        pbW3 = [((pb[4], pb[5]), ("pb4", "pb5")),
                ((pb[0], pb[1]), ("pb0", "pb1")),
                ((pb[2], pb[3]), ("pb2", "pb3"))]

        def c2p2_stages():
            def WO(k):
                tisl = slice(k * 128, (k + 1) * 128)
                (bk0, bk1), bkr = pbW3[k % 3]
                ormres = tuple(f"rB/orms/{k}/{m}" for m in range(8))
                for nh in range(2):
                    bank = (bk0, bk1)[nh]
                    for dvch in range(8):
                        p.op("tensor", lambda e, dvch=dvch, tisl=tisl,
                             bank=bank, nh=nh: e.matmul(
                                 bank[:], ormsF[:, dvch, tisl],
                                 woTg[:, dvch, nh * 512:(nh + 1) * 512],
                                 start=(dvch == 0), stop=(dvch == 7)),
                             r=ormres + (f"rC/woTg/{dvch}",),
                             w=(bkr[nh],))

            def TLa(k):
                (bk0, bk1), bkr = pbW3[k % 3]
                ltX = ltG[:, k, :]
                mvX = mvG[:, k, :]
                bnX = bnG[:, k, :]
                ltR, mvR, bnR = f"s/ltG/{k}", f"s/mvG/{k}", f"s/bnG/{k}"
                rcr = f"s/rc16/{k // 4}"
                rck = rc16[:, k:k + 1]
                p.op("vector", lambda e, bnX=bnX, bk0=bk0: e.bn_stats(
                    bnX[:, 0:6], bk0[:]), r=(bkr[0],), w=(bnR,))
                p.op("vector", lambda e, bnX=bnX, bk1=bk1: e.bn_stats(
                    bnX[:, 6:12], bk1[:]), r=(bkr[1],), w=(bnR,))
                p.op("vector", lambda e, bnX=bnX, mvX=mvX: e.bn_aggr(
                    mvX[:], bnX[:]), r=(bnR,), w=(mvR,))
                p.op("vector", lambda e, ltX=ltX, rck=rck: e.tensor_tensor(
                    ltX[:, 4:5], rck, rck, ALU.mult), r=(rcr,), w=(ltR,))
                p.op("vector", lambda e, ltX=ltX, mvX=mvX, rck=rck:
                     e.scalar_tensor_tensor(
                         ltX[:, 5:6], mvX[:, 0:1], -1.0, rck,
                         ALU.mult, ALU.mult),
                     r=(mvR, rcr), w=(ltR,))
                p.op("vector", lambda e, ltX=ltX, mvX=mvX: e.tensor_tensor(
                    ltX[:, 6:7], mvX[:, 1:2], ltX[:, 4:5], ALU.mult),
                    r=(mvR, ltR), w=(ltR,))
                p.op("vector", lambda e, ltX=ltX: e.tensor_scalar(
                    ltX[:, 7:8], ltX[:, 6:7], EPS, None, ALU.add),
                    r=(ltR,), w=(ltR,))

            def SLn(k):
                ltX = ltG[:, k, :]
                ltR = f"s/ltG/{k}"
                rcr = f"s/rc16/{k // 4}"
                p.op("scalar", lambda e, ltX=ltX: e.activation(
                    ltX[:, 0:1], ltX[:, 7:8], AF.Ln), r=(ltR,), w=(ltR,))
                p.op("scalar", lambda e, ltX=ltX: e.activation(
                    ltX[:, 1:2], ltX[:, 0:1], AF.Exp, scale=-0.5),
                    r=(ltR,), w=(ltR,))
                p.op("scalar", lambda e, ltX=ltX, k=k: e.activation(
                    ltX[:, 2:3], ltX[:, 1:2], AF.Identity,
                    scale=rc16[:, k:k + 1]),
                    r=(ltR, rcr), w=(ltR,))
                p.op("scalar", lambda e, ltX=ltX: e.activation(
                    ltX[:, 3:4], ltX[:, 5:6], AF.Identity,
                    scale=ltX[:, 1:2]),
                    r=(ltR,), w=(ltR,))

            def XL(k):
                (x0, x1_), xr = xrlP2[k % 2]
                p.dma("sync", lambda e, k=k, x0=x0: e.dma_start(
                    x0, x_in.ap()[k * 128:(k + 1) * 128, 0:512]),
                    r=(), w=xr)
                p.dma("sync", lambda e, k=k, x1_=x1_: e.dma_start(
                    x1_, x_in.ap()[k * 128:(k + 1) * 128, 512:1024]),
                    r=(), w=xr)

            def SAp(k):
                (bk0, bk1), bkr = pbW3[k % 3]
                ltX = ltG[:, k, :]
                ltR = f"s/ltG/{k}"
                p.op("scalar", lambda e, ltX=ltX, bk0=bk0: e.activation(
                    ptf[0][:], bk0[:], AF.Identity,
                    scale=ltX[:, 2:3], bias=ltX[:, 3:4]),
                    r=(bkr[0], ltR), w=ptfR[0])
                p.op("scalar", lambda e, ltX=ltX, bk1=bk1: e.activation(
                    ptf[1][:], bk1[:], AF.Identity,
                    scale=ltX[:, 2:3], bias=ltX[:, 3:4]),
                    r=(bkr[1], ltR), w=ptfR[1])

            def TLc(k):
                ti = k
                (x0, x1_), xr = xrlP2[k % 2]
                (y0, y1_), yr = x1sP2[k % 2]
                ltX = ltG[:, k, :]
                mvX = mvG[:, k, :]
                bnX = bnG[:, k, :]
                ltR, mvR, bnR = f"s/ltG/{k}", f"s/mvG/{k}", f"s/bnG/{k}"
                p.op("vector", lambda e, y0=y0, x0=x0: e.tensor_tensor(
                    y0, ptf[0][:], x0, ALU.add),
                    r=ptfR[0] + xr, w=yr)
                p.op("vector", lambda e, y1_=y1_, x1_=x1_: e.tensor_tensor(
                    y1_, ptf[1][:], x1_, ALU.add),
                    r=ptfR[1] + xr, w=yr)
                p.dma("sync", lambda e, ti=ti, y0=y0: e.dma_start(
                    x1_spill.ap()[ti * 128:(ti + 1) * 128, 0:512], y0),
                    r=yr, w=(f"s/x1sp/{ti}",))
                p.dma("sync", lambda e, ti=ti, y1_=y1_: e.dma_start(
                    x1_spill.ap()[ti * 128:(ti + 1) * 128, 512:1024], y1_),
                    r=yr, w=(f"s/x1sp/{ti}",))
                p.op("vector", lambda e, bnX=bnX, y0=y0: e.bn_stats(
                    bnX[:, 0:6], y0), r=yr, w=(bnR,))
                p.op("vector", lambda e, bnX=bnX, y1_=y1_: e.bn_stats(
                    bnX[:, 6:12], y1_), r=yr, w=(bnR,))
                p.op("vector", lambda e, bnX=bnX, mvX=mvX: e.bn_aggr(
                    mvX[:], bnX[:]), r=(bnR,), w=(mvR,))
                p.op("vector", lambda e, ltX=ltX, mvX=mvX: e.tensor_scalar(
                    ltX[:, 7:8], mvX[:, 1:2], EPS, None, ALU.add),
                    r=(mvR,), w=(ltR,))
                p.op("vector", lambda e, ltX=ltX, mvX=mvX: e.tensor_scalar(
                    ltX[:, 5:6], mvX[:, 0:1], -1.0, None, ALU.mult),
                    r=(mvR,), w=(ltR,))

            def SLn2(k):
                ltX = ltG[:, k, :]
                ltR = f"s/ltG/{k}"
                p.op("scalar", lambda e, ltX=ltX: e.activation(
                    ltX[:, 0:1], ltX[:, 7:8], AF.Ln), r=(ltR,), w=(ltR,))
                p.op("scalar", lambda e, ltX=ltX: e.activation(
                    ltX[:, 1:2], ltX[:, 0:1], AF.Exp, scale=-0.5),
                    r=(ltR,), w=(ltR,))
                p.op("scalar", lambda e, ltX=ltX: e.activation(
                    ltX[:, 3:4], ltX[:, 5:6], AF.Identity,
                    scale=ltX[:, 1:2]),
                    r=(ltR,), w=(ltR,))

            def SAp2(k):
                ti = k
                (y0, y1_), yr = x1sP2[k % 2]
                ltX = ltG[:, k, :]
                ltR = f"s/ltG/{k}"
                xt = xn2a[:, ti, :] if ti < 8 else xn2b[:, ti - 8, :]
                xreg = f"rD/xn2/{ti}" if ti < 8 else f"rE/xn2/{ti}"
                p.op("scalar", lambda e, xt=xt, ltX=ltX, y0=y0: e.activation(
                    xt[:, 0:512], y0, AF.Identity,
                    scale=ltX[:, 1:2], bias=ltX[:, 3:4]),
                    r=yr + (ltR,), w=(xreg,))
                p.op("scalar", lambda e, xt=xt, ltX=ltX, y1_=y1_: e.activation(
                    xt[:, 512:1024], y1_, AF.Identity,
                    scale=ltX[:, 1:2], bias=ltX[:, 3:4]),
                    r=yr + (ltR,), w=(xreg,))

            def TLe(k):
                ti = k
                xt = xn2a[:, ti, :] if ti < 8 else xn2b[:, ti - 8, :]
                xreg = f"rD/xn2/{ti}" if ti < 8 else f"rE/xn2/{ti}"
                p.op("vector", lambda e, xt=xt, ti=ti: e.tensor_reduce(
                    amxall[:, ti:ti + 1], xt, AX.X, ALU.max,
                    apply_absolute_value=True),
                    r=(xreg,), w=(f"s/amx/{ti}",))

            return [(7, TLe), (6, SAp2), (5, SLn2), (4, TLc), (3, XL),
                    (3, SAp), (2, SLn), (1, TLa), (0, WO)]

        pipeline(NT, c2p2_stages())
        p.op("vector", lambda e: e.tensor_reduce(
            armx[:], amxall[:, 0:NT], AX.X, ALU.max),
            r=tuple(f"s/amx/{t}" for t in range(NT)), w=("s/armx",))
        p.dma("gpsimd", lambda e: e.dma_start(arm1_in.ap(), armx[:]),
              r=("s/armx",), w=("s/arm1_in",))
        p.cc(lambda e: e.collective_compute(
            "AllReduce", ALU.max, replica_groups=[list(range(8))],
            ins=[arm1_in.ap()], outs=[arm1_out.ap()]),
            r=("s/arm1_in",), w=("s/arm1_out",))
        p.dma("gpsimd", lambda e: e.dma_start(armx[:], arm1_out.ap()),
              r=("s/arm1_out",), w=("s/armx",))
        if debug and upto >= 8:
            p.dma("gpsimd", lambda e: e.dma_start(dbg["d_x1s"].ap(), x1_spill.ap()),
                  r=tuple(f"s/x1sp/{t}" for t in range(NT)))
            for ti in range(NT):
                xt = xn2a[:, ti, :] if ti < 8 else xn2b[:, ti - 8, :]
                xreg = f"rD/xn2/{ti}" if ti < 8 else f"rE/xn2/{ti}"
                p.dma("gpsimd", lambda e, xt=xt, ti=ti: e.dma_start(
                    dbg["d_xn2"].ap()[ti], xt), r=(xreg,))
            p.dma("gpsimd", lambda e: e.dma_start(
                dbg["d_amax"].ap()[:, 0:1], armx[:],
                allow_slow_non_contiguous=True), r=("s/armx",))
        if upto <= 8:
            p.barrier_engine("sync")
            block = st.enter_context(nc.Block())
            p.emit(block, sems)
            return nc

        p.close_role("rA")
        p.close_role("rB")
        p.close_role("rC")

        def scale_pair(col_lo, r_res, w_res):
            p.op("tensor", lambda e: e.matmul(
                pb[2][0:1, 0:128], armx[:], identF[:], start=True, stop=True),
                r=(r_res, "c/identF"), w=("pb2",))
            p.op("vector", lambda e: e.reduce_max(
                sc1[0:1, 0:1], pb[2][0:1, 0:128], AX.X),
                r=("pb2",), w=("s/sc1",))
            p.op("vector", lambda e: e.tensor_scalar(
                sc1[0:1, 1:2], sc1[0:1, 0:1], 1e-8, None, ALU.max),
                r=("s/sc1",), w=("s/sc1",))
            p.op("vector", lambda e: e.reciprocal(sc1[0:1, 2:3], sc1[0:1, 1:2]),
                 r=("s/sc1",), w=("s/sc1",))
            p.op("vector", lambda e: e.tensor_scalar(
                sc1[0:1, 3:4], sc1[0:1, 2:3], 32767.0, None, ALU.mult),
                r=("s/sc1",), w=("s/sc1",))
            p.op("vector", lambda e: e.tensor_scalar(
                sc1[0:1, 0:1], sc1[0:1, 1:2], 1.0 / 32767.0, None, ALU.mult),
                r=("s/sc1",), w=("s/sc1",))
            p.op("tensor", lambda e: e.matmul(
                pb[2][:, 0:1], onesrf[:], sc1[0:1, 3:4], start=True, stop=True),
                r=("c/onesrf", "s/sc1"), w=("pb2",))
            p.op("tensor", lambda e: e.matmul(
                pb[2][:, 1:2], onesrf[:], sc1[0:1, 0:1], start=True, stop=True),
                r=("c/onesrf", "s/sc1"), w=("pb2",))
            p.op("vector", lambda e, col_lo=col_lo: e.tensor_copy(
                scp[:, col_lo:col_lo + 2], pb[2][:, 0:2]),
                r=("pb2",), w=(w_res,))

        scale_pair(0, "s/armx", "s/scp01")
        for r_ in range(8):
            dst = winsA if r_ < 4 else winsB
            reg = "rB" if r_ < 4 else "rC"
            p.dma("sync", lambda e, r_=r_, dst=dst: e.dma_start(
                dst[:, :, (r_ % 4) * 512:(r_ % 4 + 1) * 512],
                agwi_out.ap()[r_].rearrange("(a p) b -> p a b", p=128)),
                r=("s/agwi_out",), w=(f"{reg}/wins/{r_ % 4}",))

        hst8 = [(hst[0], ("s/s2f0/0",)), (hst[1], ("s/s2f0/1",)),
                (s1b, ("s/s1b",)), (s2b, ("s/s2b",)),
                (sinb, ("s/sinb",)), (acb, ("s/acb",)),
                (sin2b, ("s/sin2b",)), (o2t, ("s/o2t", "s/o2tb"))]
        amx128 = sb("amx128", [128, 128], F32)

        def f1x_stages():
            def XQ(ti):
                xt = xn2a[:, ti, :] if ti < 8 else xn2b[:, ti - 8, :]
                xreg = f"rD/xn2/{ti}" if ti < 8 else f"rE/xn2/{ti}"
                p.op("vector", lambda e, xt=xt: e.tensor_scalar(
                    q16[:, 0:1024], xt, scp[:, 0:1], None, ALU.mult),
                    r=(xreg, "s/scp01"), w=("s/s1f0",))
                p.op("vector", lambda e: e.tensor_scalar(
                    xsimt[:, 0:1024], q16[:, 0:1024], scp[:, 1:2], None,
                    ALU.mult),
                    r=("s/s1f0", "s/scp01"), w=("s/s1f1",))
                bank = ti % 2
                for dch in range(8):
                    p.op("tensor", lambda e, dch=dch, bank=bank: e.transpose(
                        pt[bank][:, dch * 128:(dch + 1) * 128],
                        xsimt[:, dch * 128:(dch + 1) * 128], identB[:]),
                        r=("s/s1f1", "c/identB"), w=(f"pt{bank}/{dch}",))

            def XE(ti):
                bank = ti % 2
                eng = "scalar" if ti % 2 else "vector"
                p.op(eng, lambda e, ti=ti, bank=bank: ecopy(
                    e, xsimF[:, :, ti * 128:(ti + 1) * 128],
                    pt[bank].ap().rearrange("p (a b) -> p a b", a=8, b=128)),
                    r=tuple(f"pt{bank}/{dch}" for dch in range(8)),
                    w=tuple(f"rA/xs/{ti}/{dch}" for dch in range(8)))

            return [(0, XQ), (1, XE)]

        pipeline(NT, f1x_stages())

        def f1mm_stages():
            def MM(m):
                g, dffc = divmod(m, 32)
                wbufv = winsA if dffc < 16 else winsB
                wres = f"{'rB' if dffc < 16 else 'rC'}/wins/{(dffc % 16) // 4}"
                bank = m % 2
                xres = tuple(f"rA/xs/{g * 4 + t}/{k}"
                             for t in range(4) for k in range(8))
                for kch in range(8):
                    p.op("tensor", lambda e, dffc=dffc, g=g, kch=kch,
                         bank=bank, wbufv=wbufv: e.matmul(
                             pb[bank][:],
                             wbufv[:, kch, (dffc % 16) * 128:(dffc % 16 + 1) * 128],
                             xsimF[:, kch, g * 512:(g + 1) * 512],
                             start=(kch == 0), stop=(kch == 7)),
                         r=(wres,) + xres, w=(f"pb{bank}",))

            def GE(m):
                g, dffc = divmod(m, 32)
                hb, hbr = hst8[m % 8]
                p.op("scalar", lambda e, dffc=dffc, bank=m % 2, hb=hb:
                     e.activation(hb[:, 0:512], pb[bank][:], AF.Gelu,
                                  bias=binp[:, dffc:dffc + 1]),
                     r=(f"pb{m % 2}", "c/binp"), w=hbr)

            def AM(m):
                hb, hbr = hst8[m % 8]
                p.op("vector", lambda e, m=m, hb=hb: e.tensor_reduce(
                    amx128[:, m:m + 1], hb[:, 0:512], AX.X, ALU.max,
                    apply_absolute_value=True),
                    r=hbr, w=(f"s/amx/{m}",))

            def SP(m):
                g, dffc = divmod(m, 32)
                hb, hbr = hst8[m % 8]
                p.dma("sync", lambda e, dffc=dffc, g=g, hb=hb: e.dma_start(
                    hid_spill.ap()[dffc, :, g * 512:(g + 1) * 512],
                    hb[:, 0:512]),
                    r=hbr, w=(f"s/hsp/{dffc}/{g}",))

            return [(0, MM), (1, GE), (2, AM), (2, SP)]

        pipeline(NG * 32, f1mm_stages())
        p.op("vector", lambda e: e.tensor_reduce(
            armx[:], amx128[:, 0:NG * 32], AX.X, ALU.max),
            r=tuple(f"s/amx/{d}" for d in range(NG * 32)), w=("s/armx",))
        p.dma("gpsimd", lambda e: e.dma_start(arm2_in.ap(), armx[:]),
              r=("s/armx",), w=("s/arm2_in",))
        p.cc(lambda e: e.collective_compute(
            "AllReduce", ALU.max, replica_groups=[list(range(8))],
            ins=[arm2_in.ap()], outs=[arm2_out.ap()]),
            r=("s/arm2_in",), w=("s/arm2_out",))
        p.dma("gpsimd", lambda e: e.dma_start(armx[:], arm2_out.ap()),
              r=("s/arm2_out",), w=("s/armx",))
        if debug and upto >= 9:
            p.dma("gpsimd", lambda e: e.dma_start(dbg["d_hids"].ap(), hid_spill.ap()),
                  r=tuple(f"s/hsp/{d}/{g}" for d in range(32) for g in range(NG)))
            p.dma("gpsimd", lambda e: e.dma_start(
                dbg["d_wins"].ap()[:, :, 0:2048], winsA[:]),
                r=tuple(f"rB/wins/{r_}" for r_ in range(4)))
            p.dma("gpsimd", lambda e: e.dma_start(
                dbg["d_wins"].ap()[:, :, 2048:4096], winsB[:]),
                r=tuple(f"rC/wins/{r_}" for r_ in range(4)))
            p.dma("gpsimd", lambda e: e.dma_start(
                dbg["d_amax"].ap()[:, 1:2], armx[:],
                allow_slow_non_contiguous=True), r=("s/armx",))
        if upto <= 9:
            p.barrier_engine("sync")
            block = st.enter_context(nc.Block())
            p.emit(block, sems)
            return nc

        p.close_role("rA")
        p.close_role("rB")
        p.close_role("rC")
        p.close_role("rD")
        p.close_role("rE")
        scale_pair(2, "s/armx", "s/scp23")
        for r_ in range(8):
            dst = wousA if r_ < 4 else wousB
            reg = "rB" if r_ < 4 else "rC"
            p.dma("sync", lambda e, r_=r_, dst=dst: e.dma_start(
                dst[:, (r_ % 4) * 4:(r_ % 4) * 4 + 4, :],
                agwo_out.ap()[r_].rearrange("(a p) b -> p a b", p=128)),
                r=("s/agwo_out",), w=(f"{reg}/wous/{r_ % 4}",))
        p.dma("sync", lambda e: e.dma_start(boutb_v[:, 0:1024], b_outb.ap()),
              w=("rD/boutb",))
        outs_v = rview(rD, F32, [1024], 1024)
        hidqG2 = rview(rE, BF16, [32, 512])
        hidqP = [(hidqG, "rA"), (hidqG2, "rE")]

        def f2_stages():
            def QD(j):
                g, cb = j // 4, (j % 4) * 8
                for c in range(8):
                    dffc = cb + c
                    hb, hbr = hst8[c]
                    p.dma("sync", lambda e, dffc=dffc, g=g, hb=hb: e.dma_start(
                        hb[:, 0:512],
                        hid_spill.ap()[dffc, :, g * 512:(g + 1) * 512]),
                        r=(f"s/hsp/{dffc}/{g}",), w=hbr)

            def QT(j):
                g, cb = j // 4, (j % 4) * 8
                hq, hqreg = hidqP[g % 2]
                for c in range(8):
                    dffc = cb + c
                    hb, hbr = hst8[c]
                    p.op("vector", lambda e, hb=hb: e.tensor_scalar(
                        q16[:, 0:512], hb[:, 0:512], scp[:, 2:3], None,
                        ALU.mult),
                        r=hbr + ("s/scp23",), w=("s/s1f0",))
                    p.op("vector", lambda e, dffc=dffc, hq=hq: e.tensor_scalar(
                        hq[:, dffc, :], q16[:, 0:512], scp[:, 3:4], None,
                        ALU.mult),
                        r=("s/s1f0", "s/scp23"), w=(f"{hqreg}/hq/{dffc}",))

            def MM(k):
                g, tt_ = divmod(k, 4)
                hq, hqreg = hidqP[g % 2]
                bks = (4, 5) if k % 2 == 0 else (0, 1)
                for nh in range(2):
                    bank = bks[nh]
                    for dffc in range(32):
                        wv = wousA if dffc < 16 else wousB
                        wres = f"{'rB' if dffc < 16 else 'rC'}/wous/{(dffc % 16) // 4}"
                        p.op("tensor", lambda e, dffc=dffc, tt_=tt_, nh=nh,
                             bank=bank, wv=wv, hq=hq: e.matmul(
                                 pb[bank][:],
                                 hq[:, dffc, tt_ * 128:(tt_ + 1) * 128],
                                 wv[:, dffc % 16, nh * 512:(nh + 1) * 512],
                                 start=(dffc == 0), stop=(dffc == 31)),
                             r=(f"{hqreg}/hq/{dffc}", wres), w=(f"pb{bank}",))

            def TAIL(k):
                ti = k
                tisl = slice(ti * 128, (ti + 1) * 128)
                bks = (4, 5) if k % 2 == 0 else (0, 1)
                p.dma("sync", lambda e, tisl=tisl, ti=ti: e.dma_start(
                    xrl[:, 0:1024], x1_spill.ap()[tisl, :]),
                    r=(f"s/x1sp/{ti}",), w=("s/sinf0", "s/sinf1"))
                p.op("vector", lambda e, b0=bks[0]: e.tensor_tensor(
                    pb[2][:, 0:512], pb[b0][:], boutb_v[:, 0:512], ALU.add),
                    r=(f"pb{bks[0]}", "rD/boutb"), w=("pb2",))
                p.op("vector", lambda e, b1=bks[1]: e.tensor_tensor(
                    pb[3][:, 0:512], pb[b1][:], boutb_v[:, 512:1024], ALU.add),
                    r=(f"pb{bks[1]}", "rD/boutb"), w=("pb3",))
                p.op("vector", lambda e: e.bn_stats(bn12[:, 0:6], pb[2][:]),
                     r=("pb2",), w=("s/bn12a",))
                p.op("vector", lambda e: e.bn_stats(bn12[:, 6:12], pb[3][:]),
                     r=("pb3",), w=("s/bn12b",))
                p.op("vector", lambda e: e.bn_aggr(mv[:], bn12[:]),
                     r=("s/bn12a", "s/bn12b"), w=("s/mv",))
                p.op("vector", lambda e: e.tensor_scalar(
                    lt[:, 7:8], mv[:, 1:2], EPS, None, ALU.add),
                    r=("s/mv",), w=("s/lt7",))
                p.op("scalar", lambda e: e.activation(
                    lt[:, 0:1], lt[:, 7:8], AF.Ln),
                    r=("s/lt7",), w=("s/lt0",))
                p.op("scalar", lambda e: e.activation(
                    lt[:, 1:2], lt[:, 0:1], AF.Exp, scale=-0.5),
                    r=("s/lt0",), w=("s/lt1",))
                p.op("vector", lambda e: e.scalar_tensor_tensor(
                    lt[:, 3:4], mv[:, 0:1], -1.0, lt[:, 1:2],
                    ALU.mult, ALU.mult),
                    r=("s/mv", "s/lt1"), w=("s/lt3",))
                p.op("scalar", lambda e: e.activation(
                    ptf[0][:], pb[2][:], AF.Identity,
                    scale=lt[:, 1:2], bias=lt[:, 3:4]),
                    r=("pb2", "s/lt1", "s/lt3"), w=ptfR[0])
                p.op("scalar", lambda e: e.activation(
                    ptf[1][:], pb[3][:], AF.Identity,
                    scale=lt[:, 1:2], bias=lt[:, 3:4]),
                    r=("pb3", "s/lt1", "s/lt3"), w=ptfR[1])
                p.op("vector", lambda e: e.tensor_tensor(
                    outs_v[:, 0:512], ptf[0][:], xrl[:, 0:512], ALU.add),
                    r=ptfR[0] + ("s/sinf0", "s/sinf1"), w=("rD/outs",))
                p.op("vector", lambda e: e.tensor_tensor(
                    outs_v[:, 512:1024], ptf[1][:], xrl[:, 512:1024], ALU.add),
                    r=ptfR[1] + ("s/sinf0", "s/sinf1"), w=("rD/outs",))
                p.dma("sync", lambda e, tisl=tisl, ti=ti: e.dma_start(
                    y_out.ap()[tisl, :], outs_v[:, 0:1024]),
                    r=("rD/outs",), w=(f"s/y/{ti}",))

            def DBG(j):
                if not debug:
                    return
                g, cb = j // 4, (j % 4) * 8
                hq, hqreg = hidqP[g % 2]
                for c in range(8):
                    dffc = cb + c
                    p.dma("gpsimd", lambda e, dffc=dffc, g=g, hq=hq: e.dma_start(
                        dbg["d_hid"].ap()[dffc, :, g * 512:(g + 1) * 512],
                        hq[:, dffc, :]),
                        r=(f"{hqreg}/hq/{dffc}",))

            return [(1, QT), (0, QD), (2, DBG), (5, MM), (6, TAIL)]

        pipeline(NT, f2_stages())
        p.barrier_engine("sync")
        block = st.enter_context(nc.Block())
        p.emit(block, sems)
    return nc


def _host_consts(seg):
    identB = np.eye(128, dtype=ml_dtypes.bfloat16)
    identF = np.eye(128, dtype=np.float32)
    j = np.arange(128)
    maskB = (j[:, None] <= j[None, :]).astype(ml_dtypes.bfloat16)
    onesr = np.ones((1, 128), ml_dtypes.bfloat16)
    onesB = np.ones((128, 128), ml_dtypes.bfloat16)
    onesrf = np.ones((1, 128), np.float32)
    onesc = np.ones((128, 1), ml_dtypes.bfloat16)
    selT = np.kron(np.eye(16, dtype=np.float32),
                   np.ones((1, 128), np.float32)).astype(ml_dtypes.bfloat16)
    return identB, identF, maskB, onesr, onesrf, onesc, onesB, selT


def make_in_maps(inputs, seg=2048):
    x = np.ascontiguousarray(np.asarray(inputs["L0_attn_hidden_states"], np.float32))
    B, T, D = x.shape
    xf = x.reshape(B * T, D)
    wqT = np.ascontiguousarray(np.asarray(inputs["L0_attn_Wq"], np.float32).T)
    wkT = np.ascontiguousarray(np.asarray(inputs["L0_attn_Wk"], np.float32).T)
    wvT = np.ascontiguousarray(np.asarray(inputs["L0_attn_Wv"], np.float32).T)
    wfT = np.ascontiguousarray(np.asarray(inputs["L0_attn_Wf"], np.float32).T)
    woT = np.ascontiguousarray(np.asarray(inputs["L0_attn_Wo"], np.float32).T)
    g_w = np.asarray(inputs["L0_attn_g_w"], np.float32)
    g_wp = np.ascontiguousarray(g_w.reshape(8, 128).T)
    w_in = np.asarray(inputs["L0_ffn_W_in"], np.float32)
    w_out = np.asarray(inputs["L0_ffn_W_out"], np.float32)
    winT = np.ascontiguousarray(w_in.T)
    b_in = np.asarray(inputs["L0_ffn_b_in"], np.float32)
    b_inp = np.ascontiguousarray(b_in.reshape(32, 128).T)
    b_out = np.asarray(inputs["L0_ffn_b_out"], np.float32)
    b_outb = np.ascontiguousarray(np.broadcast_to(b_out, (128, 1024)))
    identB, identF, maskB, onesr, onesrf, onesc, onesB, selT = _host_consts(seg)

    in_maps = []
    for c in range(8):
        b, s = divmod(c, 4)
        tmask = np.array([1.0 if sp < s else 0.0 for sp in range(3)], np.float32)
        dflag = tmask.copy()
        selm = np.zeros((128, 12), np.float32)
        selm[:, 0:3] = tmask
        selm[:, 3:6] = dflag
        selm[:, 6:9] = 1.0 - dflag
        in_maps.append({
            "x_in": np.ascontiguousarray(xf[c * seg:(c + 1) * seg]),
            "wqT": wqT, "wkT": wkT, "wvT": wvT, "wfT": wfT, "woT": woT,
            "g_wp": g_wp,
            "winT_sh": np.ascontiguousarray(winT[c * 512:(c + 1) * 512]),
            "wout_sh": np.ascontiguousarray(w_out[c * 512:(c + 1) * 512]),
            "b_inp": b_inp, "b_outb": b_outb,
            "identB": identB, "identF": identF, "maskB": maskB,
            "onesr": onesr, "onesrf": onesrf, "onesc": onesc, "onesB": onesB,
            "selm": selm, "selT": selT,
        })
    return in_maps


def kernel(**inputs):
    seg = 2048
    nc = build_nc(seg=seg)
    in_maps = make_in_maps(inputs, seg)
    res = run_bass_kernel_spmd(nc, in_maps, core_ids=list(range(8)))
    y = np.concatenate([np.asarray(r["y_out"]) for r in res.results], axis=0)
    return y.reshape(2, 8192, 1024).astype(np.float32)
